# version 01
# baseline (speedup 1.0000x reference)
import contextlib
import numpy as np
import ml_dtypes
import concourse.bass as bass
import concourse.mybir as mybir
from concourse.bass_utils import run_bass_kernel_spmd

F32 = mybir.dt.float32
BF16 = mybir.dt.bfloat16
AF = mybir.ActivationFunctionType
ALU = mybir.AluOpType
AX = mybir.AxisListType.X

B, S, D, NH, E = 2, 4096, 2048, 16, 128
SEQ8 = S // 8
TOK = 2 * SEQ8
NT = TOK // 128
DCK = D // 128
NG = S // 512
I0, I1, I1P = 8192, 5632, 6144
EPS = 1e-5
LNEPS = 1e-5
MAGIC = float(1.5 * 2 ** 23)
SCALE_ATT = 1.0 / float(np.sqrt(128.0))
TWO_PI = float(2.0 * np.pi)
HALF_PI = float(0.5 * np.pi)
INV_2PI = float(1.0 / (2.0 * np.pi))

WSLICES = [
    ("wg", D // 8, I0, D * I0),
    ("wu", D // 8, I0, D * I0),
    ("wd", I0 // 8, D, I0 * D),
    ("w1", D // 8, I1, D * I1),
    ("w3", D // 8, I1, D * I1),
    ("w2", I1P // 8, D, I1 * D),
]
WQ_RANK = sum(r * c for _, r, c, _ in WSLICES)
WOFF = {}
_o = 0
for _nm, _r, _c, _ in WSLICES:
    WOFF[_nm] = _o
    _o += _r * _c
WCOL = {nm: i for i, (nm, *_) in enumerate(WSLICES)}


EDGE_REG = {}


def _inst_key(inst):
    try:
        return inst.ins.name
    except AttributeError:
        return id(inst)


class Ctr:
    def __init__(self, sem):
        self.sem = sem
        self.v = 0

    def inc(self, inst, n=1):
        self.v += n
        inst.then_inc(self.sem, n)
        EDGE_REG[_inst_key(inst)] = (self.sem, self.v)
        return inst

    def dma(self, inst):
        return self.inc(inst, 16)

    def wait(self, eng, v=None):
        eng.wait_ge(self.sem, self.v if v is None else v)


def cdiv(a, b):
    return (a + b - 1) // b


def _ap_tname(a):
    try:
        return a.tensor.name
    except AttributeError:
        return None


class SafeEng:

    def __init__(self, eng, hctr, state):
        self.e = eng
        self.h = hctr
        self.st = state
        self.w = state["w"]
        self.r = state["r"]
        self.lv = state["lv"]

    def _resolve_pending(self):
        inst = self.st.get("pend")
        if inst is not None:
            k = _inst_key(inst)
            if k not in EDGE_REG:
                self.h.inc(inst)
                EDGE_REG[k] = (self.h.sem, self.h.v)
            self.st["pend"] = None

    def wait_ge(self, sem, v):
        self.e.wait_ge(sem, v)

    def nop(self, **kw):
        return self.e.nop(**kw)

    def dma_start(self, **kw):
        return self.e.dma_start(**kw)

    def _edge_of(self, rec):
        if rec[1] is None:
            rec[1] = EDGE_REG[_inst_key(rec[0])]
        return rec[1]

    def _need(self, needs, rec):
        sem, val = self._edge_of(rec)
        k = id(sem)
        if self.lv.get(k, 0) < val:
            needs[k] = (sem, max(val, needs[k][1] if k in needs else 0))

    def _apply(self, needs):
        for k, (sem, val) in needs.items():
            self.e.wait_ge(sem, val)
            self.lv[k] = val

    def flush(self):
        self._resolve_pending()
        needs = {}
        for recs in self.w.values():
            for rec in recs:
                self._need(needs, rec)
        self._apply(needs)

    def _pre(self, reads, writes):
        self._resolve_pending()
        rn = {n for n in (_ap_tname(r) for r in reads) if n}
        wn = {n for n in (_ap_tname(w) for w in writes) if n}
        needs = {}
        for n in rn | wn:
            for rec in self.w.get(n, ()):
                self._need(needs, rec)
        for n in wn:
            for rec in self.r.get(n, ()):
                self._need(needs, rec)
        self._apply(needs)
        return rn, wn

    def _post(self, inst, rn, wn):
        self.st["pend"] = inst
        for n in wn:
            self.w[n] = [[inst, None]]
            self.r[n] = []
        for n in rn:
            if n not in wn:
                self.r.setdefault(n, []).append([inst, None])
        return inst

    def memset(self, ap, const):
        rn, wn = self._pre([], [ap])
        return self._post(self.e.memset(ap, const), rn, wn)

    def memzero(self, ap):
        rn, wn = self._pre([], [ap])
        return self._post(self.e.memzero(ap), rn, wn)

    def tensor_copy(self, out, in_):
        rn, wn = self._pre([in_], [out])
        return self._post(self.e.tensor_copy(out=out, in_=in_), rn, wn)

    def tensor_tensor(self, out, in0, in1, op):
        rn, wn = self._pre([in0, in1], [out])
        return self._post(self.e.tensor_tensor(out=out, in0=in0, in1=in1, op=op), rn, wn)

    def tensor_scalar(self, out, in0, scalar1, scalar2, op0, op1=None, accum_out=None):
        rn, wn = self._pre([in0, scalar1, scalar2], [out, accum_out])
        kw = {}
        if op1 is not None:
            kw["op1"] = op1
        if accum_out is not None:
            kw["accum_out"] = accum_out
        return self._post(self.e.tensor_scalar(out=out, in0=in0, scalar1=scalar1,
                                               scalar2=scalar2, op0=op0, **kw), rn, wn)

    def scalar_tensor_tensor(self, out, in0, scalar, in1, op0, op1, accum_out=None):
        rn, wn = self._pre([in0, scalar, in1], [out, accum_out])
        kw = {"accum_out": accum_out} if accum_out is not None else {}
        return self._post(self.e.scalar_tensor_tensor(out=out, in0=in0, scalar=scalar,
                                                      in1=in1, op0=op0, op1=op1, **kw),
                          rn, wn)

    def reduce_sum(self, out, in_, axis, apply_absolute_value=None):
        rn, wn = self._pre([in_], [out])
        kw = {}
        if apply_absolute_value is not None:
            kw["apply_absolute_value"] = apply_absolute_value
        return self._post(self.e.reduce_sum(out=out, in_=in_, axis=axis, **kw), rn, wn)

    def reduce_max(self, out, in_, axis, apply_absolute_value=None):
        rn, wn = self._pre([in_], [out])
        kw = {}
        if apply_absolute_value is not None:
            kw["apply_absolute_value"] = apply_absolute_value
        return self._post(self.e.reduce_max(out=out, in_=in_, axis=axis, **kw), rn, wn)

    def tensor_reduce(self, out, in_, axis, op):
        rn, wn = self._pre([in_], [out])
        return self._post(self.e.tensor_reduce(out=out, in_=in_, axis=axis, op=op), rn, wn)

    def reciprocal(self, out, in_):
        rn, wn = self._pre([in_], [out])
        return self._post(self.e.reciprocal(out=out, in_=in_), rn, wn)

    def activation(self, out, in_, func, bias=None, scale=None, accum_out=None):
        rn, wn = self._pre([in_, bias, scale], [out, accum_out])
        kw = {}
        if bias is not None:
            kw["bias"] = bias
        if scale is not None:
            kw["scale"] = scale
        if accum_out is not None:
            kw["accum_out"] = accum_out
        return self._post(self.e.activation(out=out, in_=in_, func=func, **kw), rn, wn)

    def copy(self, out, in_):
        rn, wn = self._pre([in_], [out])
        return self._post(self.e.copy(out=out, in_=in_), rn, wn)

    def add(self, out, in_, add):
        rn, wn = self._pre([in_, add], [out])
        return self._post(self.e.add(out=out, in_=in_, add=add), rn, wn)

    def mul(self, out, in_, mul):
        rn, wn = self._pre([in_, mul], [out])
        return self._post(self.e.mul(out=out, in_=in_, mul=mul), rn, wn)


def build(debug=(), upto='all', skip_cc=()):
    EDGE_REG.clear()
    nc = bass.Bass("TRN2", target_bir_lowering=False)

    xa = nc.dram_tensor("xa", [S, D], F32, kind="ExternalInput").ap()
    xo = nc.dram_tensor("xo", [TOK, D], F32, kind="ExternalInput").ap()
    wq_t = nc.dram_tensor("wq_t", [4, D, E], F32, kind="ExternalInput").ap()
    wk_t = nc.dram_tensor("wk_t", [4, D, E], F32, kind="ExternalInput").ap()
    wv_t = nc.dram_tensor("wv_t", [4, D, E], F32, kind="ExternalInput").ap()
    wo_t = nc.dram_tensor("wo_t", [NH, E, D], F32, kind="ExternalInput").ap()
    wsl = {nm: nc.dram_tensor(nm + "_s", [r, c], F32, kind="ExternalInput").ap()
           for nm, r, c, _ in WSLICES}
    tri = nc.dram_tensor("tri", [128, 128], F32, kind="ExternalInput").ap()
    ident = nc.dram_tensor("ident", [128, 128], F32, kind="ExternalInput").ap()
    onesc = nc.dram_tensor("onesc", [128, 1], F32, kind="ExternalInput").ap()
    onesr = nc.dram_tensor("onesr", [1, 128], F32, kind="ExternalInput").ap()
    ldt_in = nc.dram_tensor("ldt_in", [128, DCK], F32, kind="ExternalInput").ap()
    dpar_in = nc.dram_tensor("dpar_in", [128, DCK], F32, kind="ExternalInput").ap()
    arl_in = nc.dram_tensor("arl_in", [128, DCK * 8], F32, kind="ExternalInput").ap()
    aim_in = nc.dram_tensor("aim_in", [128, DCK * 8], F32, kind="ExternalInput").ap()
    cre_in = nc.dram_tensor("cre_in", [128, DCK * 8], F32, kind="ExternalInput").ap()
    cim_in = nc.dram_tensor("cim_in", [128, DCK * 8], F32, kind="ExternalInput").ap()
    zsel_in = nc.dram_tensor("zsel_in", [1024, 1], mybir.dt.int32, kind="ExternalInput").ap()
    jnk_in = nc.dram_tensor("jnk_in", [128, 16], F32, kind="ExternalInput").ap()

    out = nc.dram_tensor("out", [TOK, D], F32, kind="ExternalOutput").ap()

    ar_in = nc.dram_tensor("ar_in", [1, 8], F32, kind="Internal").ap()
    ar_out = nc.dram_tensor("ar_out", [1, 8], F32, kind="Internal").ap()
    wqag_in = nc.dram_tensor("wqag_in", [WQ_RANK], BF16, kind="Internal").ap()
    wqag_out = nc.dram_tensor("wqag_out", [8, WQ_RANK], BF16, kind="Internal").ap()
    qd = nc.dram_tensor("qd", [4, E, S], BF16, kind="Internal").ap()
    kd = nc.dram_tensor("kd", [4, E, S], BF16, kind="Internal").ap()
    vd = nc.dram_tensor("vd", [4, E, S], BF16, kind="Internal").ap()
    zag_in = nc.dram_tensor("zag_in", [NG, E, 4, 512], BF16, kind="Internal").ap()
    zag_out = nc.dram_tensor("zag_out", [8, NG, E, 4, 512], BF16, kind="Internal").ap()
    sd = nc.dram_tensor("sd", [TOK, D], F32, kind="Internal").ap()
    prodd = nc.dram_tensor("prodd", [TOK, I0], BF16, kind="Internal").ap()
    pqd = nc.dram_tensor("pqd", [TOK, I0], BF16, kind="Internal").ap()
    hdd = nc.dram_tensor("hdd", [TOK, I1], BF16, kind="Internal").ap()
    hqd = nc.dram_tensor("hqd", [TOK, I1P], BF16, kind="Internal").ap()
    x1d = nc.dram_tensor("x1d", [TOK, D], F32, kind="Internal").ap()
    xn1d = nc.dram_tensor("xn1d", [TOK, D], F32, kind="Internal").ap()
    ktmp_d = nc.dram_tensor("ktmp_d", [32, 128], F32, kind="Internal").ap()

    dbg = {}

    def dbg_ap(name, shape, dtype=F32):
        if name in debug:
            dbg[name] = nc.dram_tensor("dbg_" + name, shape, dtype, kind="ExternalOutput").ap()
            return dbg[name]
        return None

    d_wsc = dbg_ap("wsc", [1, 8])
    d_kvec = dbg_ap("kvec", [1, 2 * D])
    d_qkv = dbg_ap("qkv", [3, 4, E, S], BF16)
    d_xna = dbg_ap("xna", [S, D], BF16)
    d_aw = dbg_ap("aw", [128, 4 * DCK, E], BF16)
    d_xnf = dbg_ap("xnf", [128, DCK, 512], BF16)
    d_z = dbg_ap("z", [8, NG, E, 4, 512], BF16)
    d_zb = dbg_ap("zb", [128, 32, 128], BF16)
    d_f0 = dbg_ap("f0", [128, 320], F32)
    d_f0p = dbg_ap("f0", [128, 128], BF16) if False else None
    d_f0b = dbg_ap("f0b", [128, 128], BF16)
    d_qt = dbg_ap("qt", [2, 128, S], BF16)
    d_qt0 = dbg_ap("qt0", [2, 128, S], BF16)
    d_pb = dbg_ap("pb", [128, S], BF16)
    d_ssb = dbg_ap("ssb", [128, S], F32)
    d_s = dbg_ap("s", [TOK, D])
    d_xn = dbg_ap("xn", [TOK, D])
    d_g0 = dbg_ap("g0", [TOK, D])
    d_x1 = dbg_ap("x1", [TOK, D])
    d_xn1 = dbg_ap("xn1", [TOK, D])
    d_g1 = dbg_ap("g1", [TOK, D])

    CSUB = 2048
    wchunks = []
    for nm, r, c_, _ in WSLICES:
        for r0 in range(0, r, 128):
            nr = min(128, r - r0)
            for c0 in range(0, c_, CSUB):
                wchunks.append((nm, r0, nr, c0, min(CSUB, c_ - c0)))
    NWCH = len(wchunks)

    with contextlib.ExitStack() as es:
        c_tri = es.enter_context(nc.sbuf_tensor("c_tri", [128, 128], F32))
        c_idb = es.enter_context(nc.sbuf_tensor("c_idb", [128, 128], BF16))
        c_idf = es.enter_context(nc.sbuf_tensor("c_idf", [128, 128], F32))
        c_onesr = es.enter_context(nc.sbuf_tensor("c_onesr", [1, 128], F32))
        c_onesc = es.enter_context(nc.sbuf_tensor("c_onesc", [128, 1], F32))
        acc6 = es.enter_context(nc.sbuf_tensor("acc6", [128, 8], F32))
        wsc = es.enter_context(nc.sbuf_tensor("wsc", [1, 8], F32))
        sc8a = es.enter_context(nc.sbuf_tensor("sc8a", [1, 8], F32))
        sc8b = es.enter_context(nc.sbuf_tensor("sc8b", [1, 8], F32))
        q_inv = es.enter_context(nc.sbuf_tensor("q_inv", [128, 8], F32))
        q_deq = es.enter_context(nc.sbuf_tensor("q_deq", [128, 8], F32))
        kb0 = es.enter_context(nc.sbuf_tensor("kb0", [128, D], F32))
        kb1 = es.enter_context(nc.sbuf_tensor("kb1", [128, D], F32))
        st_a = es.enter_context(nc.sbuf_tensor("st_a", [128, 64], F32))
        st_b = es.enter_context(nc.sbuf_tensor("st_b", [128, 64], F32))
        st_c = es.enter_context(nc.sbuf_tensor("st_c", [128, 64], F32))
        st_d = es.enter_context(nc.sbuf_tensor("st_d", [128, 64], F32))
        st_e = es.enter_context(nc.sbuf_tensor("st_e", [128, 64], F32))
        st_f = es.enter_context(nc.sbuf_tensor("st_f", [128, 64], F32))
        vjk = es.enter_context(nc.sbuf_tensor("vjk", [128, 8], F32))
        sjk = es.enter_context(nc.sbuf_tensor("sjk", [128, 8], F32))
        zsel_t = [es.enter_context(nc.sbuf_tensor(f"zsel{r}", [128, 1], mybir.dt.int32))
                  for r in range(8)]
        _m_sy = es.enter_context(nc.semaphore("m_sy"))
        _m_sc = es.enter_context(nc.semaphore("m_sc"))
        _m_gp = es.enter_context(nc.semaphore("m_gp"))
        _m_xa = es.enter_context(nc.semaphore("m_xa"))
        _m_xb = es.enter_context(nc.semaphore("m_xb"))
        _m_wa = es.enter_context(nc.semaphore("m_wa"))
        _m_wb = es.enter_context(nc.semaphore("m_wb"))
        _m_kv = es.enter_context(nc.semaphore("m_kv"))
        _m_w2a = es.enter_context(nc.semaphore("m_w2a"))
        _m_w2b = es.enter_context(nc.semaphore("m_w2b"))
        _m_spa = es.enter_context(nc.semaphore("m_spa"))
        _m_spb = es.enter_context(nc.semaphore("m_spb"))
        _m_out = es.enter_context(nc.semaphore("m_out"))
        _m_cc = es.enter_context(nc.semaphore("m_cc"))
        _m_dbg = es.enter_context(nc.semaphore("m_dbg"))
        _m_q0 = es.enter_context(nc.semaphore("m_q0"))
        _m_q1 = es.enter_context(nc.semaphore("m_q1"))
        _m_q2 = es.enter_context(nc.semaphore("m_q2"))
        _m_q3 = es.enter_context(nc.semaphore("m_q3"))
        _m_oa = es.enter_context(nc.semaphore("m_oa"))
        _m_ob = es.enter_context(nc.semaphore("m_ob"))
        _c_v = es.enter_context(nc.semaphore("c_v"))
        _c_s = es.enter_context(nc.semaphore("c_s"))
        _c_t = es.enter_context(nc.semaphore("c_t"))
        _c_g = es.enter_context(nc.semaphore("c_g"))
        _h_v = es.enter_context(nc.semaphore("h_v"))
        _m_zc = es.enter_context(nc.semaphore("m_zc"))
        _m_pwa = es.enter_context(nc.semaphore("m_pwa"))
        _m_pwb = es.enter_context(nc.semaphore("m_pwb"))
        _h_s = es.enter_context(nc.semaphore("h_s"))
        blk = es.enter_context(nc.Block())
        m_sy, m_sc, m_gp = Ctr(_m_sy), Ctr(_m_sc), Ctr(_m_gp)
        m_xa, m_xb = Ctr(_m_xa), Ctr(_m_xb)
        m_wa, m_wb = Ctr(_m_wa), Ctr(_m_wb)
        m_kv = Ctr(_m_kv)
        m_w2a, m_w2b = Ctr(_m_w2a), Ctr(_m_w2b)
        m_spa, m_spb = Ctr(_m_spa), Ctr(_m_spb)
        m_out, m_cc = Ctr(_m_out), Ctr(_m_cc)
        m_dbg = Ctr(_m_dbg)
        m_qs = [Ctr(_m_q0), Ctr(_m_q1), Ctr(_m_q2), Ctr(_m_q3)]
        h_v, h_s = Ctr(_h_v), Ctr(_h_s)
        m_zc = Ctr(_m_zc)
        m_pwa, m_pwb = Ctr(_m_pwa), Ctr(_m_pwb)
        vstate = {"w": {}, "r": {}, "lv": {}, "pend": None}
        sstate = {"w": {}, "r": {}, "lv": {}, "pend": None}

        def phase_join(eng):
            for ctr in (c_t, c_v, c_s, h_v, h_s, m_sy, m_sc, m_gp, m_xa, m_xb,
                        m_wa, m_wb, m_kv, m_zc, m_w2a, m_w2b, m_spa, m_spb,
                        m_oa, m_ob, m_pwa, m_pwb, m_dbg, *m_qs):
                if ctr.v:
                    eng.wait_ge(ctr.sem, ctr.v)
        m_oa, m_ob = Ctr(_m_oa), Ctr(_m_ob)
        c_v, c_s, c_t, c_g = Ctr(_c_v), Ctr(_c_s), Ctr(_c_t), Ctr(_c_g)

        with contextlib.ExitStack() as es01:
            awq = es01.enter_context(nc.sbuf_tensor("awq", [128, 4 * DCK, E], BF16))
            awk = es01.enter_context(nc.sbuf_tensor("awk", [128, 4 * DCK, E], BF16))
            awv = es01.enter_context(nc.sbuf_tensor("awv", [128, 4 * DCK, E], BF16))
            with contextlib.ExitStack() as es0:
                wsb0 = es0.enter_context(nc.sbuf_tensor("wsb0", [128, CSUB], F32))
                wsb1 = es0.enter_context(nc.sbuf_tensor("wsb1", [128, CSUB], F32))
                wqb0 = es0.enter_context(nc.sbuf_tensor("wqb0", [128, CSUB], BF16))
                wqb1 = es0.enter_context(nc.sbuf_tensor("wqb1", [128, CSUB], BF16))
                s4 = es0.enter_context(nc.sbuf_tensor("s4", [128, 16 * DCK * 8], F32))
                ktr = es0.enter_context(nc.sbuf_tensor("ktr", [32, 128], F32))
                kvec = es0.enter_context(nc.sbuf_tensor("kvec", [1, 2 * D], F32))
                p0a = es0.enter_context(nc.psum_tensor("p0a", [128, 512], F32))
                p0b = es0.enter_context(nc.psum_tensor("p0b", [128, 512], F32))
                NP = DCK * 8

                def pl(i):
                    return s4[:, i * NP:(i + 1) * NP]

                wbufs = [(wsb0, m_wa), (wsb1, m_wb)]
                wqbufs = [wqb0, wqb1]

                vA0 = c_v.v
                passA_vals = []

                @blk.sync
                def _(sync):
                    m_sy.dma(sync.dma_start(out=vjk[:], in_=jnk_in[:, 0:8]))
                    m_sy.dma(sync.dma_start(out=sjk[:], in_=jnk_in[:, 8:16]))
                    m_sy.dma(sync.dma_start(out=c_tri[:], in_=tri))
                    m_sy.dma(sync.dma_start(out=c_idf[:], in_=ident))
                    m_sy.dma(sync.dma_start(out=c_onesr[:], in_=onesr))
                    m_sy.dma(sync.dma_start(out=c_onesc[:], in_=onesc))
                    for i, (nm, r0, nr, c0, ncl) in enumerate(wchunks):
                        buf, sem = wbufs[i % 2]
                        if i >= 2:
                            sync.wait_ge(_c_v, vA0 + 1 + (i - 2) + 1)
                        sem.dma(sync.dma_start(out=buf[0:nr, 0:ncl],
                                               in_=wsl[nm][r0:r0 + nr, c0:c0 + ncl]))
                        passA_vals.append(sem.v)

                @blk.gpsimd
                def _(gpsimd):
                    for r in range(8):
                        m_gp.dma(gpsimd.dma_start(out=zsel_t[r][:],
                                                  in_=zsel_in[128 * r:128 * (r + 1), :]))
                    m_gp.dma(gpsimd.dma_start(out=c_idb[:], in_=ident))
                    m_gp.dma(gpsimd.dma_start(
                        out=awq[:], in_=wq_t.rearrange("u (ck p) e -> p (u ck) e", p=128)))
                    m_gp.dma(gpsimd.dma_start(
                        out=awk[:], in_=wk_t.rearrange("u (ck p) e -> p (u ck) e", p=128)))
                    m_gp.dma(gpsimd.dma_start(
                        out=awv[:], in_=wv_t.rearrange("u (ck p) e -> p (u ck) e", p=128)))
                    m_gp.dma(gpsimd.dma_start(out=pl(0)[:, 0:DCK], in_=ldt_in))
                    m_gp.dma(gpsimd.dma_start(out=pl(0)[:, DCK:2 * DCK], in_=dpar_in))
                    m_gp.dma(gpsimd.dma_start(out=pl(1), in_=arl_in))
                    m_gp.dma(gpsimd.dma_start(out=pl(2), in_=aim_in))
                    m_gp.dma(gpsimd.dma_start(out=pl(3), in_=cre_in))
                    m_gp.dma(gpsimd.dma_start(out=pl(4), in_=cim_in))
                gp_loaded = m_gp.v

                @blk.vector
                def _(vector):
                    vector = SafeEng(vector, h_v, vstate)
                    vector.wait_ge(_m_sy, 96)
                    for i, (nm, r0, nr, c0, ncl) in enumerate(wchunks):
                        buf, sem = wbufs[i % 2]
                        sem.wait(vector, passA_vals[i])
                        if i == 0:
                            c_v.inc(vector.tensor_scalar(out=acc6[:], in0=vjk[:, 0:8],
                                                         scalar1=0.0, scalar2=None,
                                                         op0=ALU.mult))
                        col = WCOL[nm]
                        vector.reduce_sum(out=st_a[0:nr, 0:1], in_=buf[0:nr, 0:ncl],
                                          axis=AX, apply_absolute_value=True)
                        c_v.inc(vector.tensor_tensor(out=acc6[0:nr, col:col + 1],
                                                     in0=acc6[0:nr, col:col + 1],
                                                     in1=st_a[0:nr, 0:1], op=ALU.add))
                vA_done = c_v.v

                @blk.tensor
                def _(tensor):
                    c_v.wait(tensor, vA_done)
                    m_sy.wait(tensor, 96)
                    c_t.inc(tensor.matmul(p0a[0:1, 0:8], lhsT=c_onesc[:], rhs=acc6[:, 0:8],
                                          start=True, stop=True))
                t_abs = c_t.v

                @blk.scalar
                def _(scalar):
                    scalar = SafeEng(scalar, h_s, sstate)
                    c_t.wait(scalar, t_abs)
                    c_s.inc(scalar.copy(out=sc8a[0:1, 0:8], in_=p0a[0:1, 0:8]))
                    scalar.wait_ge(_c_s, c_s.v)
                    m_sc.dma(scalar.dma_start(out=ar_in, in_=sc8a[0:1, 0:8]))
                sc_ar = m_sc.v

                @blk.gpsimd
                def _(gpsimd):
                    m_sc.wait(gpsimd, sc_ar)
                    m_cc.inc(gpsimd.collective_compute(
                        "AllReduce", ALU.add, ins=[ar_in], outs=[ar_out],
                        replica_groups=[[0, 1, 2, 3, 4, 5, 6, 7]]))
                    gpsimd.wait_ge(_m_cc, m_cc.v)
                    m_gp.dma(gpsimd.dma_start(out=wsc[:], in_=ar_out))
                    if d_wsc is not None:
                        m_gp.dma(gpsimd.dma_start(out=d_wsc, in_=ar_out))
                gp_wsc = m_gp.v

                @blk.vector
                def _(vector):
                    vector = SafeEng(vector, h_v, vstate)
                    m_gp.wait(vector, gp_wsc)
                    for j, (_, _, _, cnt) in enumerate(WSLICES):
                        vector.tensor_scalar(out=sc8a[0:1, j:j + 1], in0=wsc[0:1, j:j + 1],
                                             scalar1=1.0 / float(cnt), scalar2=None, op0=ALU.mult)
                    for j in (0, 1, 2):
                        vector.tensor_scalar(out=sc8a[0:1, j:j + 1], in0=sc8a[0:1, j:j + 1],
                                             scalar1=EPS, scalar2=None, op0=ALU.add)
                    for j in (3, 4, 5):
                        vector.tensor_scalar(out=sc8a[0:1, j:j + 1], in0=sc8a[0:1, j:j + 1],
                                             scalar1=EPS, scalar2=None, op0=ALU.max)
                    vector.tensor_scalar(out=sc8a[0:1, 6:8], in0=wsc[0:1, 6:8],
                                         scalar1=0.0, scalar2=1.0, op0=ALU.mult, op1=ALU.add)
                    c_v.inc(vector.reciprocal(out=sc8b[0:1, 0:8], in_=sc8a[0:1, 0:8]))
                v_scales = c_v.v

                @blk.tensor
                def _(tensor):
                    c_v.wait(tensor, v_scales)
                    tensor.matmul(p0b[:, 0:8], lhsT=c_onesr[:], rhs=sc8a[0:1, 0:8],
                                  start=True, stop=True)
                    c_t.inc(tensor.matmul(p0b[:, 8:16], lhsT=c_onesr[:], rhs=sc8b[0:1, 0:8],
                                          start=True, stop=True))
                t_bcast = c_t.v

                @blk.vector
                def _(vector):
                    vector = SafeEng(vector, h_v, vstate)
                    c_t.wait(vector, t_bcast)
                    vector.tensor_copy(out=q_deq[:], in_=p0b[:, 0:8])
                    c_v.inc(vector.tensor_copy(out=q_inv[:], in_=p0b[:, 8:16]))
                v_bcast = c_v.v

                @blk.scalar
                def _(scalar):
                    scalar = SafeEng(scalar, h_s, sstate)
                    m_gp.wait(scalar, gp_wsc)
                    scalar.activation(out=st_e[:, 0:DCK], in_=pl(0)[:, 0:DCK], func=AF.Exp)
                    c_s.inc(scalar.activation(out=pl(5), in_=pl(1), func=AF.Exp))
                s_s4a = c_s.v

                @blk.vector
                def _(vector):
                    vector = SafeEng(vector, h_v, vstate)
                    c_s.wait(vector, s_s4a)
                    for n in range(8):
                        vector.tensor_copy(out=pl(6)[:, n::8], in_=st_e[:, 0:DCK])
                    vector.tensor_scalar(out=pl(5), in0=pl(5), scalar1=-1.0, scalar2=None,
                                         op0=ALU.mult)
                    vector.tensor_tensor(out=pl(7), in0=pl(2), in1=pl(6), op=ALU.mult)
                    vector.tensor_tensor(out=pl(6), in0=pl(5), in1=pl(6), op=ALU.mult)
                    vector.tensor_scalar(out=pl(8), in0=pl(7), scalar1=INV_2PI, scalar2=MAGIC,
                                         op0=ALU.mult, op1=ALU.add)
                    vector.tensor_scalar(out=pl(8), in0=pl(8), scalar1=-MAGIC, scalar2=None,
                                         op0=ALU.add)
                    c_v.inc(vector.scalar_tensor_tensor(out=pl(8), in0=pl(8), scalar=-TWO_PI,
                                                        in1=pl(7), op0=ALU.mult, op1=ALU.add))
                v_s4a = c_v.v

                @blk.scalar
                def _(scalar):
                    scalar = SafeEng(scalar, h_s, sstate)
                    c_v.wait(scalar, v_s4a)
                    scalar.activation(out=pl(9), in_=pl(6), func=AF.Exp)
                    c_s.inc(scalar.activation(out=pl(10), in_=pl(8), func=AF.Sin))
                s_s4b = c_s.v

                @blk.vector
                def _(vector):
                    vector = SafeEng(vector, h_v, vstate)
                    c_s.wait(vector, s_s4b)
                    vector.tensor_scalar(out=pl(11), in0=pl(8), scalar1=HALF_PI, scalar2=None,
                                         op0=ALU.add)
                    vector.tensor_scalar(out=pl(12), in0=pl(11), scalar1=INV_2PI, scalar2=MAGIC,
                                         op0=ALU.mult, op1=ALU.add)
                    vector.tensor_scalar(out=pl(12), in0=pl(12), scalar1=-MAGIC, scalar2=None,
                                         op0=ALU.add)
                    c_v.inc(vector.scalar_tensor_tensor(out=pl(11), in0=pl(12), scalar=-TWO_PI,
                                                        in1=pl(11), op0=ALU.mult, op1=ALU.add))
                v_s4b = c_v.v

                @blk.scalar
                def _(scalar):
                    scalar = SafeEng(scalar, h_s, sstate)
                    c_v.wait(scalar, v_s4b)
                    c_s.inc(scalar.activation(out=pl(11), in_=pl(11), func=AF.Sin))
                s_s4c = c_s.v

                @blk.vector
                def _(vector):
                    vector = SafeEng(vector, h_v, vstate)
                    c_s.wait(vector, s_s4c)
                    vector.tensor_tensor(out=pl(12), in0=pl(9), in1=pl(11), op=ALU.mult)
                    vector.tensor_tensor(out=pl(9), in0=pl(9), in1=pl(10), op=ALU.mult)
                    vector.tensor_scalar(out=pl(10), in0=pl(12), scalar1=-1.0, scalar2=None,
                                         op0=ALU.add)
                    vector.tensor_tensor(out=pl(13), in0=pl(5), in1=pl(5), op=ALU.mult)
                    vector.tensor_tensor(out=pl(14), in0=pl(2), in1=pl(2), op=ALU.mult)
                    vector.tensor_tensor(out=pl(13), in0=pl(13), in1=pl(14), op=ALU.add)
                    vector.reciprocal(out=pl(13), in_=pl(13))
                    vector.tensor_tensor(out=pl(14), in0=pl(3), in1=pl(10), op=ALU.mult)
                    vector.tensor_tensor(out=pl(15), in0=pl(4), in1=pl(9), op=ALU.mult)
                    vector.tensor_tensor(out=pl(14), in0=pl(14), in1=pl(15), op=ALU.subtract)
                    vector.tensor_tensor(out=pl(15), in0=pl(3), in1=pl(9), op=ALU.mult)
                    vector.tensor_tensor(out=pl(10), in0=pl(4), in1=pl(10), op=ALU.mult)
                    vector.tensor_tensor(out=pl(15), in0=pl(15), in1=pl(10), op=ALU.add)
                    vector.tensor_tensor(out=pl(10), in0=pl(14), in1=pl(5), op=ALU.mult)
                    vector.tensor_tensor(out=pl(11), in0=pl(15), in1=pl(2), op=ALU.mult)
                    vector.tensor_tensor(out=pl(10), in0=pl(10), in1=pl(11), op=ALU.add)
                    vector.tensor_tensor(out=pl(10), in0=pl(10), in1=pl(13), op=ALU.mult)
                    vector.tensor_tensor(out=pl(11), in0=pl(15), in1=pl(5), op=ALU.mult)
                    vector.tensor_tensor(out=pl(15), in0=pl(14), in1=pl(2), op=ALU.mult)
                    vector.tensor_tensor(out=pl(11), in0=pl(11), in1=pl(15), op=ALU.subtract)
                    vector.tensor_tensor(out=pl(11), in0=pl(11), in1=pl(13), op=ALU.mult)
                    vector.tensor_reduce(out=st_e[:, 16:16 + DCK],
                                         in_=pl(10).rearrange("p (c n) -> p c n", n=8),
                                         axis=AX, op=ALU.add)
                    vector.tensor_tensor(out=pl(14), in0=pl(10), in1=pl(12), op=ALU.mult)
                    vector.tensor_tensor(out=pl(15), in0=pl(11), in1=pl(9), op=ALU.mult)
                    vector.tensor_tensor(out=pl(14), in0=pl(14), in1=pl(15), op=ALU.subtract)
                    vector.tensor_reduce(out=st_e[:, 32:32 + DCK],
                                         in_=pl(14).rearrange("p (c n) -> p c n", n=8),
                                         axis=AX, op=ALU.add)
                    vector.tensor_scalar(out=st_f[:, 0:DCK], in0=st_e[:, 16:16 + DCK],
                                         scalar1=2.0, scalar2=None, op0=ALU.mult)
                    vector.tensor_tensor(out=st_f[:, 0:DCK], in0=st_f[:, 0:DCK],
                                         in1=pl(0)[:, DCK:2 * DCK], op=ALU.add)
                    c_v.inc(vector.tensor_scalar(out=st_f[:, DCK:2 * DCK],
                                                 in0=st_e[:, 32:32 + DCK],
                                                 scalar1=2.0, scalar2=None, op0=ALU.mult))
                v_s4c = c_v.v

                @blk.tensor
                def _(tensor):
                    c_v.wait(tensor, v_s4c)
                    c_t.inc(tensor.matmul(p0a[0:32, 0:128], lhsT=st_f[:, 0:32],
                                          rhs=c_idf[:], start=True, stop=True))
                t_ktr = c_t.v

                @blk.vector
                def _(vector):
                    vector = SafeEng(vector, h_v, vstate)
                    c_t.wait(vector, t_ktr)
                    c_v.inc(vector.tensor_copy(out=ktr[:], in_=p0a[0:32, 0:128]))
                v_ktr = c_v.v

                @blk.gpsimd
                def _(gpsimd):
                    c_v.wait(gpsimd, v_ktr)
                    m_gp.dma(gpsimd.dma_start(out=ktmp_d, in_=ktr[:]))
                    gpsimd.wait_ge(_m_gp, m_gp.v)
                    m_gp.dma(gpsimd.dma_start(out=kvec[0:1, :],
                                              in_=ktmp_d.rearrange("a b -> (a b)")[None, :]))
                    if d_kvec is not None:
                        gpsimd.wait_ge(_m_gp, m_gp.v)
                        m_gp.dma(gpsimd.dma_start(out=d_kvec, in_=kvec[0:1, :]))
                gp_kvec = m_gp.v

                t_kb0 = c_t.v
                v_kb0 = c_v.v

                @blk.tensor
                def _(tensor):
                    m_gp.wait(tensor, gp_kvec)
                    for q in range(4):
                        if q >= 1:
                            tensor.wait_ge(_c_v, v_kb0 + q)
                        tensor.matmul(p0a[:, 0:512], lhsT=c_onesr[:],
                                      rhs=kvec[0:1, 512 * q:512 * (q + 1)],
                                      start=True, stop=True)
                        c_t.inc(tensor.matmul(p0b[:, 0:512], lhsT=c_onesr[:],
                                              rhs=kvec[0:1, D + 512 * q:D + 512 * (q + 1)],
                                              start=True, stop=True))

                @blk.vector
                def _(vector):
                    vector = SafeEng(vector, h_v, vstate)
                    for q in range(4):
                        vector.wait_ge(_c_t, t_kb0 + q + 1)
                        vector.tensor_copy(out=kb0[:, 512 * q:512 * (q + 1)], in_=p0a[:, 0:512])
                        c_v.inc(vector.tensor_copy(out=kb1[:, 512 * q:512 * (q + 1)],
                                                   in_=p0b[:, 0:512]))
                v_kb = c_v.v

                vB0 = c_v.v
                syB0 = m_sy.v
                wq_views = {nm: wqag_in[WOFF[nm]:WOFF[nm] + r * c_].rearrange("(r c) -> r c", r=r)
                            for nm, r, c_, _ in WSLICES}
                passB_in_vals = []

                @blk.scalar
                def _(scalar):
                    scalar = SafeEng(scalar, h_s, sstate)
                    c_v.wait(scalar, v_bcast)
                    for i, (nm, r0, nr, c0, ncl) in enumerate(wchunks):
                        buf, sem = wbufs[i % 2]
                        if i >= 2:
                            scalar.wait_ge(_c_v, vB0 + (i - 1))
                        sem.dma(scalar.dma_start(out=buf[0:nr, 0:ncl],
                                                 in_=wsl[nm][r0:r0 + nr, c0:c0 + ncl]))
                        passB_in_vals.append(sem.v)

                @blk.vector
                def _(vector):
                    vector = SafeEng(vector, h_v, vstate)
                    for i, (nm, r0, nr, c0, ncl) in enumerate(wchunks):
                        buf, sem = wbufs[i % 2]
                        qb = wqbufs[i % 2]
                        col = WCOL[nm]
                        sem.wait(vector, passB_in_vals[i])
                        if i >= 2:
                            osem = m_oa if i % 2 == 0 else m_ob
                            vector.wait_ge(osem.sem, 16 * ((i - 2) // 2 + 1))
                        vector.tensor_scalar(out=buf[0:nr, 0:ncl], in0=buf[0:nr, 0:ncl],
                                             scalar1=q_inv[0:nr, col:col + 1], scalar2=MAGIC,
                                             op0=ALU.mult, op1=ALU.add)
                        vector.tensor_scalar(out=buf[0:nr, 0:ncl], in0=buf[0:nr, 0:ncl],
                                             scalar1=MAGIC - 1.0, scalar2=MAGIC + 1.0,
                                             op0=ALU.max, op1=ALU.min)
                        c_v.inc(vector.tensor_scalar(out=qb[0:nr, 0:ncl], in0=buf[0:nr, 0:ncl],
                                                     scalar1=-MAGIC, scalar2=None, op0=ALU.add))

                @blk.sync
                def _(sync):
                    for i, (nm, r0, nr, c0, ncl) in enumerate(wchunks):
                        qb = wqbufs[i % 2]
                        osem = m_oa if i % 2 == 0 else m_ob
                        sync.wait_ge(_c_v, vB0 + (i + 1))
                        osem.dma(sync.dma_start(out=wq_views[nm][r0:r0 + nr, c0:c0 + ncl],
                                                in_=qb[0:nr, 0:ncl]))
                syB_done = None

                @blk.gpsimd
                def _(gpsimd):
                    gpsimd.wait_ge(_m_oa, m_oa.v)
                    gpsimd.wait_ge(_m_ob, m_ob.v)
                    if "wq" not in skip_cc:
                        m_cc.inc(gpsimd.collective_compute(
                            "AllGather", ALU.bypass, ins=[wqag_in], outs=[wqag_out],
                            replica_groups=[[0, 1, 2, 3, 4, 5, 6, 7]]))
                cc_wq = m_cc.v

            with contextlib.ExitStack() as es1a:
                xab0 = es1a.enter_context(nc.sbuf_tensor("xab0", [128, 4, D], F32))
                xab1 = es1a.enter_context(nc.sbuf_tensor("xab1", [128, 4, D], F32))
                xnb = es1a.enter_context(nc.sbuf_tensor("xnb", [128, 4, D], BF16))
                sqj = es1a.enter_context(nc.sbuf_tensor("sqj", [128, D], F32))
                xnf = es1a.enter_context(nc.sbuf_tensor("xnf", [128, DCK, 512], BF16))
                qst = es1a.enter_context(nc.sbuf_tensor("qst", [128, 4, 512], BF16))
                psA = es1a.enter_context(nc.psum_tensor("psA", [128, 512], F32))
                psB = es1a.enter_context(nc.psum_tensor("psB", [128, 512], F32))
                psT0 = es1a.enter_context(nc.psum_tensor("psT0", [128, 512], BF16))
                psT1 = es1a.enter_context(nc.psum_tensor("psT1", [128, 512], BF16))
                xbufs = [(xab0, m_xa), (xab1, m_xb)]
                v0, s0, t0 = c_v.v, c_s.v, c_t.v
                sc0, sy0 = m_sc.v, m_sy.v
                aws = {0: awq, 1: awk, 2: awv}
                wdst = {0: qd, 1: kd, 2: vd}

                def v_vsum(g):
                    return v0 + 19 * g + 1

                def v_vst1(g):
                    return v0 + 19 * g + 2

                def v_vst2(g):
                    return v0 + 19 * g + 3

                def v_evac(j):
                    return v0 + 19 * (j // 16) + 3 + (j % 16) + 1

                def s_ssq(g):
                    return s0 + 18 * g + 1

                def s_ssq2(g):
                    return s0 + 18 * g + 2

                def s_sxn(g, tt):
                    return s0 + 18 * g + 2 + tt + 1

                def s_qevac(j):
                    return s0 + 18 * (j // 12) + 6 + (j % 12) + 1

                def t_tr(j):
                    return t0 + 28 * (j // 16) + (j % 16) + 1

                def t_qkv(j):
                    return t0 + 28 * (j // 12) + 16 + (j % 12) + 1

                @blk.sync
                def _(sync):
                    sync.wait_ge(_m_oa, m_oa.v)
                    sync.wait_ge(_m_ob, m_ob.v)
                    sync.wait_ge(_m_wa, m_wa.v)
                    sync.wait_ge(_m_wb, m_wb.v)
                    sync.wait_ge(_m_gp, m_gp.v)
                    for g in range(NG):
                        buf, sem = xbufs[g % 2]
                        if g >= 2:
                            sync.wait_ge(_c_s, s_sxn(g - 2, 3))
                        sem.dma(sync.dma_start(
                            out=buf[:],
                            in_=xa[512 * g:512 * (g + 1), :]
                            .rearrange("(tt p) d -> p tt d", p=128)))

                @blk.vector
                def _(vector):
                    vector = SafeEng(vector, h_v, vstate)
                    vector.wait_ge(_m_oa, m_oa.v)
                    vector.wait_ge(_m_ob, m_ob.v)
                    for g in range(NG):
                        buf, sem = xbufs[g % 2]
                        cc0 = 8 * (g % 2)
                        sem.wait(vector, 16 * (g // 2 + 1))
                        for tt in range(4):
                            vector.reduce_sum(out=st_a[:, cc0 + tt:cc0 + tt + 1],
                                              in_=buf[:, tt, :], axis=AX)
                        c_v.inc(vector.tensor_copy(out=st_a[:, cc0 + 4:cc0 + 5],
                                                   in_=st_a[:, cc0:cc0 + 1]))
                        vector.wait_ge(_c_s, s_ssq(g))
                        a4 = st_a[:, cc0:cc0 + 4]
                        b4 = st_b[:, cc0:cc0 + 4]
                        c4 = st_c[:, cc0:cc0 + 4]
                        d4 = st_d[:, cc0:cc0 + 4]
                        e4 = st_e[:, cc0:cc0 + 4]
                        f4 = st_f[:, cc0:cc0 + 4]
                        vector.tensor_scalar(out=a4, in0=a4, scalar1=1.0 / D, scalar2=None,
                                             op0=ALU.mult)
                        vector.tensor_scalar(out=b4, in0=b4, scalar1=1.0 / D, scalar2=None,
                                             op0=ALU.mult)
                        vector.tensor_tensor(out=c4, in0=a4, in1=a4, op=ALU.mult)
                        vector.tensor_tensor(out=c4, in0=b4, in1=c4, op=ALU.subtract)
                        vector.tensor_scalar(out=c4, in0=c4, scalar1=LNEPS, scalar2=None,
                                             op0=ALU.add)
                        c_v.inc(vector.reciprocal(out=d4, in_=c4))
                        vector.wait_ge(_c_s, s_ssq2(g))
                        vector.tensor_tensor(out=f4, in0=e4, in1=e4, op=ALU.mult)
                        vector.tensor_tensor(out=f4, in0=f4, in1=c4, op=ALU.mult)
                        vector.tensor_scalar(out=f4, in0=f4, scalar1=-0.5, scalar2=1.5,
                                             op0=ALU.mult, op1=ALU.add)
                        vector.tensor_tensor(out=c4, in0=e4, in1=f4, op=ALU.mult)
                        c_v.inc(vector.scalar_tensor_tensor(out=d4, in0=a4, scalar=-1.0,
                                                            in1=c4, op0=ALU.mult,
                                                            op1=ALU.mult))
                        for ck in range(DCK):
                            j = 16 * g + ck
                            vector.wait_ge(_c_t, t_tr(j))
                            ps = psT0 if j % 2 == 0 else psT1
                            c_v.inc(vector.tensor_copy(out=xnf[:, ck, :], in_=ps[:]))

                @blk.scalar
                def _(scalar):
                    scalar = SafeEng(scalar, h_s, sstate)
                    scalar.wait_ge(_m_oa, m_oa.v)
                    scalar.wait_ge(_m_ob, m_ob.v)
                    for g in range(NG):
                        buf, sem = xbufs[g % 2]
                        cc0 = 8 * (g % 2)
                        sem.wait(scalar, 16 * (g // 2 + 1))
                        for tt in range(4):
                            acc = st_b[:, cc0 + tt:cc0 + tt + 1]
                            inst = scalar.activation(out=sqj[:], in_=buf[:, tt, :],
                                                     func=AF.Square, accum_out=acc)
                            if tt == 3:
                                c_s.inc(inst)
                        scalar.wait_ge(_c_v, v_vst1(g))
                        c_s.inc(scalar.activation(out=st_e[:, cc0:cc0 + 4],
                                                  in_=st_d[:, cc0:cc0 + 4], func=AF.Sqrt))
                        scalar.wait_ge(_c_v, v_vst2(g))
                        if g >= 1:
                            scalar.wait_ge(_c_t, t_tr(16 * (g - 1) + 15))
                        for tt in range(4):
                            c_s.inc(scalar.activation(
                                out=xnb[:, tt, :], in_=buf[:, tt, :], func=AF.Identity,
                                scale=st_c[:, cc0 + tt:cc0 + tt + 1],
                                bias=st_d[:, cc0 + tt:cc0 + tt + 1]))
                        if d_xna is not None:
                            scalar.wait_ge(_c_s, c_s.v)
                            m_dbg.dma(scalar.dma_start(
                                out=d_xna[512 * g:512 * (g + 1), :]
                                .rearrange("(tt p) d -> p tt d", p=128),
                                in_=xnb[:]))
                            scalar.wait_ge(_m_dbg, m_dbg.v)
                        for m in range(12):
                            j = 12 * g + m
                            u, which = m % 4, m // 4
                            slot = m_qs[j % 4]
                            scalar.wait_ge(_c_t, t_qkv(j))
                            if j >= 4:
                                scalar.wait_ge(slot.sem, slot.v)
                            ps = psA if j % 2 == 0 else psB
                            c_s.inc(scalar.activation(out=qst[:, j % 4, :], in_=ps[:],
                                                      func=AF.Identity))
                            scalar.wait_ge(_c_s, c_s.v)
                            slot.dma(scalar.dma_start(
                                out=wdst[which][u, :, 512 * g:512 * (g + 1)],
                                in_=qst[:, j % 4, :]))

                @blk.tensor
                def _(tensor):
                    m_gp.wait(tensor)
                    for g in range(NG):
                        for ck in range(DCK):
                            j = 16 * g + ck
                            if ck == 0:
                                tensor.wait_ge(_c_s, s_sxn(g, 3))
                            if j >= 2:
                                tensor.wait_ge(_c_v, v_evac(j - 2))
                            ps = psT0 if j % 2 == 0 else psT1
                            for tt in range(4):
                                inst = tensor.transpose(ps[:, 128 * tt:128 * (tt + 1)],
                                                        in_=xnb[:, tt, 128 * ck:128 * (ck + 1)],
                                                        identity=c_idb[:])
                                if tt == 3:
                                    c_t.inc(inst)
                        tensor.wait_ge(_c_v, v_evac(16 * g + 15))
                        for m in range(12):
                            j = 12 * g + m
                            u, which = m % 4, m // 4
                            if j >= 2:
                                tensor.wait_ge(_c_s, s_qevac(j - 2))
                            ps = psA if j % 2 == 0 else psB
                            aw = aws[which]
                            for ck in range(DCK):
                                inst = tensor.matmul(ps[:], lhsT=aw[:, u * DCK + ck, :],
                                                     rhs=xnf[:, ck, :],
                                                     start=(ck == 0), stop=(ck == DCK - 1))
                                if ck == DCK - 1:
                                    c_t.inc(inst)
                ph1a_done_vals = [s.v for s in m_qs]

                if d_aw is not None:
                    @blk.gpsimd
                    def _(gpsimd):
                        for s, v in zip(m_qs, ph1a_done_vals):
                            gpsimd.wait_ge(s.sem, v)
                        m_dbg.dma(gpsimd.dma_start(out=d_aw, in_=awq[:]))
                        m_dbg.dma(gpsimd.dma_start(out=d_xnf, in_=xnf[:]))
                if d_qkv is not None:
                    @blk.gpsimd
                    def _(gpsimd):
                        for s, v in zip(m_qs, ph1a_done_vals):
                            gpsimd.wait_ge(s.sem, v)
                        m_gp.dma(gpsimd.dma_start(out=dbg["qkv"][0], in_=qd))
                        m_gp.dma(gpsimd.dma_start(out=dbg["qkv"][1], in_=kd))
                        m_gp.dma(gpsimd.dma_start(out=dbg["qkv"][2], in_=vd))


        if upto != 'ph1a':
            with contextlib.ExitStack() as es1b:
                wo_b = es1b.enter_context(nc.sbuf_tensor("wo_b", [128, NH, D], BF16))
                Kt = es1b.enter_context(nc.sbuf_tensor("Kt", [128, S], BF16))
                Vf = es1b.enter_context(nc.sbuf_tensor("Vf", [128, S], BF16))
                Qt = es1b.enter_context(nc.sbuf_tensor("Qt", [128, S], BF16))
                Vt = es1b.enter_context(nc.sbuf_tensor("Vt", [128, 32, 128], BF16))
                Ssb = es1b.enter_context(nc.sbuf_tensor("Ssb", [128, S], F32))
                Pb = es1b.enter_context(nc.sbuf_tensor("Pb", [128, S], BF16))
                Ptb = es1b.enter_context(nc.sbuf_tensor("Ptb", [128, 2, 4, 128], BF16))
                zb = es1b.enter_context(nc.sbuf_tensor("zb", [128, 32, 128], BF16))
                zfst = es1b.enter_context(nc.sbuf_tensor("zfst", [128, NG, 512], BF16))
                zc_b = es1b.enter_context(nc.sbuf_tensor("zc_b", [128, 32, 512], BF16))
                sst0 = es1b.enter_context(nc.sbuf_tensor("sst0", [128, D], F32))
                sst1 = es1b.enter_context(nc.sbuf_tensor("sst1", [128, D], F32))
                psS0 = es1b.enter_context(nc.psum_tensor("psS0", [128, 512], F32))
                psS1 = es1b.enter_context(nc.psum_tensor("psS1", [128, 512], F32))
                psPt0 = es1b.enter_context(nc.psum_tensor("psPt0", [128, 512], BF16))
                psPt1 = es1b.enter_context(nc.psum_tensor("psPt1", [128, 512], BF16))
                psZ = es1b.enter_context(nc.psum_tensor("psZ", [128, 128], F32))
                psO0 = es1b.enter_context(nc.psum_tensor("psO0", [128, 512], F32))
                psO1 = es1b.enter_context(nc.psum_tensor("psO1", [128, 512], F32))

                @blk.gpsimd
                def _(gpsimd):
                    for s_, v_ in zip(m_qs, ph1a_done_vals):
                        gpsimd.wait_ge(s_.sem, v_)
                    m_gp.dma(gpsimd.dma_start(
                        out=wo_b[:], in_=wo_t.rearrange("h p d -> p h d")))
                gp_wo = m_gp.v

                def nchunks(t):
                    return cdiv(t + 1, 4)

                UT_T = 8 + sum(nchunks(t) + 2 * nchunks(t) for t in range(32)) + 8
                UT_V = 8 + sum(nchunks(t) for t in range(32)) + 32 * 3 + \
                    sum(nchunks(t) for t in range(32)) + 8
                UT_S = sum(nchunks(t) for t in range(32)) + 32
                t1b0, v1b0, s1b0 = c_t.v, c_v.v, c_s.v

                def tb(u):
                    return t1b0 + u * UT_T

                def vb(u):
                    return v1b0 + u * UT_V

                def sb(u):
                    return s1b0 + u * UT_S

                NCH = [nchunks(t) for t in range(32)]
                SNCH = [0]
                for t in range(32):
                    SNCH.append(SNCH[-1] + NCH[t])
                TOT = SNCH[32]

                def t_vtr(u, vbi):
                    return tb(u) + vbi + 1

                def t_score(u, t, ch):
                    return tb(u) + 8 + SNCH[t] * 3 + ch + 1

                def t_ptr(u, t, jb):
                    return tb(u) + 8 + SNCH[t] * 3 + NCH[t] + 2 * jb + 1

                def t_pv(u, t, jb):
                    return tb(u) + 8 + SNCH[t] * 3 + NCH[t] + 2 * jb + 2

                def t_ztr(u, qg):
                    return tb(u) + 8 + SNCH[32] * 3 + qg + 1

                def t_unit_end(u):
                    return tb(u) + UT_T

                def v_vtr(u, vbi):
                    return vb(u) + vbi + 1

                def v_base_t(u, t):
                    return vb(u) + 8 + SNCH[t] * 2 + t * 3

                def v_max(u, t, ch):
                    return v_base_t(u, t) + ch + 1

                def v_rowmax(u, t):
                    return v_base_t(u, t) + NCH[t] + 1

                def v_rl(u, t):
                    return v_base_t(u, t) + NCH[t] + 2

                def v_ptev(u, t, jb):
                    return v_base_t(u, t) + NCH[t] + 2 + jb + 1

                def v_zev(u, t):
                    return v_base_t(u, t) + NCH[t] + 2 + NCH[t] + 1

                def v_zfm(u, qg):
                    return vb(u) + 8 + SNCH[32] * 2 + 32 * 3 + qg + 1

                def s_base_t(u, t):
                    return sb(u) + SNCH[t] + t

                def s_ssb(u, t, ch):
                    return s_base_t(u, t) + ch + 1

                def s_exp(u, t):
                    return s_base_t(u, t) + NCH[t] + 1

                kv_vals = []

                @blk.sync
                def _(sync):
                    for s_, v_ in zip(m_qs, ph1a_done_vals):
                        sync.wait_ge(s_.sem, v_)
                    for u in range(4):
                        if u >= 1:
                            sync.wait_ge(_c_t, t_unit_end(u - 1))
                        m_kv.dma(sync.dma_start(out=Kt[:], in_=kd[u]))
                        m_kv.dma(sync.dma_start(out=Vf[:], in_=vd[u]))
                        m_kv.dma(sync.dma_start(out=Qt[:], in_=qd[u]))
                        kv_vals.append(m_kv.v)

                @blk.tensor
                def _(tensor):
                    m_gp.wait(tensor)
                    for u in range(4):
                        m_kv.wait(tensor, kv_vals[u])
                        if u >= 1:
                            tensor.wait_ge(_c_s, sb(u))
                            tensor.wait_ge(_c_v, vb(u))
                        for vbi in range(8):
                            if vbi >= 2:
                                tensor.wait_ge(_c_v, v_vtr(u, vbi - 2))
                            ps = psPt0 if vbi % 2 == 0 else psPt1
                            for i2 in range(4):
                                j = 4 * vbi + i2
                                inst = tensor.transpose(ps[:, 128 * i2:128 * (i2 + 1)],
                                                        in_=Vf[:, 128 * j:128 * (j + 1)],
                                                        identity=c_idb[:])
                                if i2 == 3:
                                    c_t.inc(inst)
                        for t in range(32):
                            for ch in range(NCH[t]):
                                gch = SNCH[t] + ch
                                if gch >= 2:
                                    pt, pc = divmod(gch - 2, 1)
                                    tp = t
                                    chp = ch - 2
                                    while chp < 0:
                                        tp -= 1
                                        chp += NCH[tp]
                                    tensor.wait_ge(_c_v, v_max(u, tp, chp))
                                    tensor.wait_ge(_c_s, s_ssb(u, tp, chp))
                                w = min(512, 128 * (t + 1) - 512 * ch)
                                ps = psS0 if gch % 2 == 0 else psS1
                                c_t.inc(tensor.matmul(ps[:, 0:w], lhsT=Qt[:, 128 * t:128 * (t + 1)],
                                                      rhs=Kt[:, 512 * ch:512 * ch + w],
                                                      start=True, stop=True))
                            tensor.wait_ge(_c_s, s_exp(u, t))
                            for jb in range(NCH[t]):
                                nj = min(4, (t + 1) - 4 * jb)
                                if jb >= 2:
                                    tensor.wait_ge(_c_v, v_ptev(u, t, jb - 2))
                                ps = psPt0 if jb % 2 == 0 else psPt1
                                for i2 in range(nj):
                                    j = 4 * jb + i2
                                    inst = tensor.transpose(ps[:, 128 * i2:128 * (i2 + 1)],
                                                            in_=Pb[:, 128 * j:128 * (j + 1)],
                                                            identity=c_idb[:])
                                    if i2 == nj - 1:
                                        c_t.inc(inst)
                                tensor.wait_ge(_c_v, v_ptev(u, t, jb))
                                for i2 in range(nj):
                                    j = 4 * jb + i2
                                    inst = tensor.matmul(psZ[:], lhsT=Ptb[:, jb % 2, i2, :],
                                                         rhs=Vt[:, j, :],
                                                         start=(j == 0), stop=(j == t))
                                    if i2 == nj - 1:
                                        c_t.inc(inst)
                            if t < 31:
                                tensor.wait_ge(_c_v, v_zev(u, t))
                        for qg in range(NG):
                            if qg >= 2:
                                tensor.wait_ge(_c_v, v_zfm(u, qg - 2))
                            ps = psPt0 if qg % 2 == 0 else psPt1
                            for i2 in range(4):
                                qt = 4 * qg + i2
                                inst = tensor.transpose(ps[:, 128 * i2:128 * (i2 + 1)],
                                                        in_=zb[:, qt, :],
                                                        identity=c_idb[:])
                                if i2 == 3:
                                    c_t.inc(inst)
                        assert c_t.v == t_unit_end(u), (u, c_t.v, t_unit_end(u))

                @blk.vector
                def _(vector):
                    vector = SafeEng(vector, h_v, vstate)
                    for u in range(4):
                        for vbi in range(8):
                            vector.wait_ge(_c_t, t_vtr(u, vbi))
                            ps = psPt0 if vbi % 2 == 0 else psPt1
                            c_v.inc(vector.tensor_copy(
                                out=Vt[:, 4 * vbi:4 * (vbi + 1), :],
                                in_=ps[:].rearrange("p (a b) -> p a b", a=4)))
                        for t in range(32):
                            diag_ch = NCH[t] - 1
                            diag_off = 128 * t - 512 * diag_ch
                            for ch in range(NCH[t]):
                                gch = SNCH[t] + ch
                                vector.wait_ge(_c_t, t_score(u, t, ch))
                                ps = psS0 if gch % 2 == 0 else psS1
                                w = min(512, 128 * (t + 1) - 512 * ch)
                                if ch == diag_ch:
                                    vector.tensor_tensor(
                                        out=ps[:, diag_off:diag_off + 128],
                                        in0=ps[:, diag_off:diag_off + 128],
                                        in1=c_tri[:], op=ALU.add)
                                c_v.inc(vector.reduce_max(out=st_a[:, ch:ch + 1],
                                                          in_=ps[:, 0:w], axis=AX))
                            vector.reduce_max(out=st_b[:, 0:1], in_=st_a[:, 0:NCH[t]],
                                              axis=AX)
                            c_v.inc(vector.tensor_scalar(out=st_b[:, 1:2], in0=st_b[:, 0:1],
                                                         scalar1=-SCALE_ATT, scalar2=None,
                                                         op0=ALU.mult))
                            vector.wait_ge(_c_s, s_exp(u, t))
                            c_v.inc(vector.reciprocal(out=st_b[:, 3:4], in_=st_b[:, 2:3]))
                            for jb in range(NCH[t]):
                                nj = min(4, (t + 1) - 4 * jb)
                                vector.wait_ge(_c_t, t_ptr(u, t, jb))
                                ps = psPt0 if jb % 2 == 0 else psPt1
                                c_v.inc(vector.tensor_copy(
                                    out=Ptb[:, jb % 2, 0:nj, :],
                                    in_=ps[:, 0:128 * nj].rearrange("p (a b) -> p a b", a=nj)))
                            vector.wait_ge(_c_t, t_pv(u, t, NCH[t] - 1))
                            c_v.inc(vector.tensor_scalar(out=zb[:, t, :], in0=psZ[:],
                                                         scalar1=st_b[:, 3:4], scalar2=None,
                                                         op0=ALU.mult))
                        for qg in range(NG):
                            vector.wait_ge(_c_t, t_ztr(u, qg))
                            if u >= 1 and qg == 0:
                                vector.wait_ge(m_qs[u - 1].sem,
                                               ph1a_done_vals[u - 1] + 16)
                            ps = psPt0 if qg % 2 == 0 else psPt1
                            c_v.inc(vector.tensor_copy(out=zfst[:, qg, :], in_=ps[:]))
                        assert c_v.v == vb(u) + UT_V, (u, c_v.v)

                @blk.scalar
                def _(scalar):
                    scalar = SafeEng(scalar, h_s, sstate)
                    for u in range(4):
                        if d_qt0 is not None and u == 3:
                            scalar.wait_ge(_m_kv, kv_vals[3])
                            m_dbg.dma(scalar.dma_start(out=d_qt0[0], in_=Qt[:]))
                            m_dbg.dma(scalar.dma_start(out=d_qt0[1], in_=Kt[:]))
                        for t in range(32):
                            for ch in range(NCH[t]):
                                gch = SNCH[t] + ch
                                vector_val = v_max(u, t, ch)
                                scalar.wait_ge(_c_v, vector_val)
                                ps = psS0 if gch % 2 == 0 else psS1
                                w = min(512, 128 * (t + 1) - 512 * ch)
                                c_s.inc(scalar.activation(out=Ssb[:, 512 * ch:512 * ch + w],
                                                          in_=ps[:, 0:w], func=AF.Identity))
                            scalar.wait_ge(_c_v, v_rowmax(u, t))
                            c_s.inc(scalar.activation(out=Pb[:, 0:128 * (t + 1)],
                                                      in_=Ssb[:, 0:128 * (t + 1)],
                                                      func=AF.Exp, scale=SCALE_ATT,
                                                      bias=st_b[:, 1:2],
                                                      accum_out=st_b[:, 2:3]))
                            if d_f0 is not None and u == 3 and t == 0:
                                scalar.wait_ge(_c_s, c_s.v)
                                m_dbg.dma(scalar.dma_start(out=d_f0[:, 0:128], in_=Ssb[:, 0:128]))
                                m_dbg.dma(scalar.dma_start(out=d_f0[:, 256:264], in_=st_b[:, 0:8]))
                                m_dbg.dma(scalar.dma_start(out=d_f0b, in_=Pb[:, 0:128]))
                        scalar.wait_ge(_c_v, v_zfm(u, NG - 1))
                        m_qs[u].dma(scalar.dma_start(
                            out=zag_in[:, :, u, :].rearrange("g e q -> e g q"),
                            in_=zfst[:]))
                z_write_vals = [s_.v for s_ in m_qs]

                @blk.gpsimd
                def _(gpsimd):
                    for s_, v_ in zip(m_qs, z_write_vals):
                        gpsimd.wait_ge(s_.sem, v_)
                    if "z" not in skip_cc:
                        m_cc.inc(gpsimd.collective_compute(
                            "AllGather", ALU.bypass, ins=[zag_in], outs=[zag_out],
                            replica_groups=[[0, 1, 2, 3, 4, 5, 6, 7]]))
                cc_z = m_cc.v

                if d_z is not None:
                    @blk.gpsimd
                    def _(gpsimd):
                        gpsimd.wait_ge(_m_cc, cc_z)
                        m_dbg.dma(gpsimd.dma_start(out=d_z, in_=zag_out))

                if d_qt is not None:
                    @blk.gpsimd
                    def _(gpsimd):
                        gpsimd.wait_ge(_m_cc, cc_z)
                        m_dbg.dma(gpsimd.dma_start(out=d_qt[0], in_=qd[3]))
                        m_dbg.dma(gpsimd.dma_start(out=d_qt[1], in_=kd[3]))

                if upto != 'ph1b':
                    zc_val = []

                    @blk.gpsimd
                    def _(gpsimd):
                        gpsimd.wait_ge(_m_cc, cc_z)
                        from concourse.bass_core_rust import IndirectOffsetOnAxis
                        zflat = zag_out.rearrange("r g e u q -> (r g e u q)") \
                            .rearrange("(x y) -> x y", y=4 * 512)
                        for r in range(8):
                            inst = gpsimd.indirect_dma_start(
                                out=zc_b[:, 4 * r:4 * (r + 1), :].rearrange("p a b -> p (a b)"),
                                out_offset=None, in_=zflat,
                                in_offset=IndirectOffsetOnAxis(zsel_t[r][:], axis=0))
                            m_zc.dma(inst)
                            gpsimd.wait_ge(_m_zc, m_zc.v)
                        zc_val.append(m_zc.v)

                    if upto != 'ph1cload':
                        t1c0, v1c0 = c_t.v, c_v.v

                        def t_oproj(i, dc):
                            return t1c0 + 4 * i + dc + 1

                        def v_oev(i, dc):
                            return v1c0 + 4 * i + dc + 1

                        @blk.tensor
                        def _(tensor):
                            m_zc.wait(tensor, zc_val[0])
                            m_gp.wait(tensor, gp_wo)
                            for i in range(NT):
                                bb, k = i // 4, i % 4
                                for dc in range(4):
                                    gdc = 4 * i + dc
                                    if gdc >= 2:
                                        tensor.wait_ge(_c_v, v_oev((gdc - 2) // 4, (gdc - 2) % 4))
                                    ps = psO0 if gdc % 2 == 0 else psO1
                                    for h in range(NH):
                                        blk_i = 4 * (4 * bb + h // 4) + (h % 4)
                                        inst = tensor.matmul(ps[:], lhsT=zc_b[:, blk_i, 128 * k:128 * (k + 1)],
                                                             rhs=wo_b[:, h, 512 * dc:512 * (dc + 1)],
                                                             start=(h == 0), stop=(h == NH - 1))
                                        if h == NH - 1:
                                            c_t.inc(inst)

                        oa_b, ob_b = m_oa.v, m_ob.v

                        @blk.vector
                        def _(vector):
                            vector = SafeEng(vector, h_v, vstate)
                            for i in range(NT):
                                sst = sst0 if i % 2 == 0 else sst1
                                osem = m_oa if i % 2 == 0 else m_ob
                                if i >= 2:
                                    vector.wait_ge(osem.sem,
                                                   (oa_b if i % 2 == 0 else ob_b)
                                                   + 16 * ((i - 2) // 2 + 1))
                                for dc in range(4):
                                    gdc = 4 * i + dc
                                    vector.wait_ge(_c_t, t_oproj(i, dc))
                                    ps = psO0 if gdc % 2 == 0 else psO1
                                    c_v.inc(vector.tensor_copy(out=sst[:, 512 * dc:512 * (dc + 1)],
                                                               in_=ps[:]))

                        @blk.scalar
                        def _(scalar):
                            scalar = SafeEng(scalar, h_s, sstate)
                            for i in range(NT):
                                sst = sst0 if i % 2 == 0 else sst1
                                osem = m_oa if i % 2 == 0 else m_ob
                                scalar.wait_ge(_c_v, v_oev(i, 3))
                                osem.dma(scalar.dma_start(out=sd[128 * i:128 * (i + 1), :],
                                                          in_=sst[:]))
                        ph1c_s_vals = (m_oa.v, m_ob.v)

                        if d_s is not None:
                            @blk.gpsimd
                            def _(gpsimd):
                                gpsimd.wait_ge(_m_oa, ph1c_s_vals[0])
                                gpsimd.wait_ge(_m_ob, ph1c_s_vals[1])
                                m_dbg.dma(gpsimd.dma_start(out=d_s, in_=sd))


        if upto not in ('ph1a', 'ph1b', 'ph1cload', 'ph1c'):
            with contextlib.ExitStack() as es2:
                xq_fm = es2.enter_context(nc.sbuf_tensor("xq_fm", [128, DCK, 1024], BF16))
                sxg = es2.enter_context(nc.sbuf_tensor("sxg", [128, 8], F32))
                sxu = es2.enter_context(nc.sbuf_tensor("sxu", [128, 8], F32))
                rmx = es2.enter_context(nc.sbuf_tensor("rmx", [128, 8], F32))
                spd = es2.enter_context(nc.sbuf_tensor("spd", [128, 8], F32))
                invsp = es2.enter_context(nc.sbuf_tensor("invsp", [128, 8], F32))

                with contextlib.ExitStack() as es2a:
                    xob = es2a.enter_context(nc.sbuf_tensor("xob", [128, NT, D], F32))
                    xnt = es2a.enter_context(nc.sbuf_tensor("xnt", [128, D], F32))
                    tq = es2a.enter_context(nc.sbuf_tensor("tq", [128, D], F32))
                    sqj2 = es2a.enter_context(nc.sbuf_tensor("sqj2", [128, D], F32))
                    xq_tm = es2a.enter_context(nc.sbuf_tensor("xq_tm", [128, NT, D], BF16))
                    psT2a = es2a.enter_context(nc.psum_tensor("psT2a", [128, 512], BF16))
                    psT2b = es2a.enter_context(nc.psum_tensor("psT2b", [128, 512], BF16))

                    v2a0, s2a0, t2a0 = c_v.v, c_s.v, c_t.v
                    xo_val = []

                    @blk.sync
                    def _(sync):
                        sync.wait_ge(_c_t, c_t.v)
                        sync.wait_ge(_c_v, c_v.v)
                        sync.wait_ge(_c_s, c_s.v)
                        sync.wait_ge(_h_v, h_v.v)
                        sync.wait_ge(_h_s, h_s.v)
                        for s_ in m_qs:
                            sync.wait_ge(s_.sem, s_.v)
                        sync.wait_ge(_m_oa, m_oa.v)
                        sync.wait_ge(_m_ob, m_ob.v)
                        sync.wait_ge(_m_kv, m_kv.v)
                        sync.wait_ge(_m_dbg, m_dbg.v)
                        m_sy.dma(sync.dma_start(
                            out=xob[:], in_=xo.rearrange("(i p) d -> p i d", p=128)))
                        xo_val.append(m_sy.v)

                    def v2a(i, k):
                        return v2a0 + 7 * i + k

                    def s2a(i, k):
                        return s2a0 + 3 * i + k

                    @blk.vector
                    def _(vector):
                        vector = SafeEng(vector, h_v, vstate)
                        m_sy.wait(vector, xo_val[0])
                        for i in range(NT):
                            cc0 = 0
                            vector.reduce_sum(out=st_a[:, 0:1], in_=xob[:, i, :], axis=AX)
                            c_v.inc(vector.tensor_copy(out=st_a[:, 1:2], in_=st_a[:, 0:1]))
                            vector.wait_ge(_c_s, s2a(i, 1))
                            vector.tensor_scalar(out=st_a[:, 0:1], in0=st_a[:, 0:1],
                                                 scalar1=1.0 / D, scalar2=None, op0=ALU.mult)
                            vector.tensor_scalar(out=st_b[:, 0:1], in0=st_b[:, 0:1],
                                                 scalar1=1.0 / D, scalar2=None, op0=ALU.mult)
                            vector.tensor_tensor(out=st_c[:, 0:1], in0=st_a[:, 0:1],
                                                 in1=st_a[:, 0:1], op=ALU.mult)
                            vector.tensor_tensor(out=st_c[:, 0:1], in0=st_b[:, 0:1],
                                                 in1=st_c[:, 0:1], op=ALU.subtract)
                            vector.tensor_scalar(out=st_c[:, 0:1], in0=st_c[:, 0:1],
                                                 scalar1=LNEPS, scalar2=None, op0=ALU.add)
                            c_v.inc(vector.reciprocal(out=st_d[:, 0:1], in_=st_c[:, 0:1]))
                            vector.wait_ge(_c_s, s2a(i, 2))
                            vector.tensor_tensor(out=st_f[:, 0:1], in0=st_e[:, 0:1],
                                                 in1=st_e[:, 0:1], op=ALU.mult)
                            vector.tensor_tensor(out=st_f[:, 0:1], in0=st_f[:, 0:1],
                                                 in1=st_c[:, 0:1], op=ALU.mult)
                            vector.tensor_scalar(out=st_f[:, 0:1], in0=st_f[:, 0:1],
                                                 scalar1=-0.5, scalar2=1.5, op0=ALU.mult,
                                                 op1=ALU.add)
                            vector.tensor_tensor(out=st_c[:, 0:1], in0=st_e[:, 0:1],
                                                 in1=st_f[:, 0:1], op=ALU.mult)
                            c_v.inc(vector.scalar_tensor_tensor(out=st_d[:, 0:1], in0=st_a[:, 0:1],
                                                                scalar=-1.0, in1=st_c[:, 0:1],
                                                                op0=ALU.mult, op1=ALU.mult))
                            vector.wait_ge(_c_s, s2a(i, 3))
                            vector.reduce_max(out=st_a[:, 2:3], in_=xnt[:], axis=AX,
                                              apply_absolute_value=True)
                            vector.tensor_scalar(out=st_a[:, 2:3], in0=st_a[:, 2:3],
                                                 scalar1=1.0 / 127.0, scalar2=EPS,
                                                 op0=ALU.mult, op1=ALU.add)
                            c_v.inc(vector.reciprocal(out=st_a[:, 3:4], in_=st_a[:, 2:3]))
                            vector.tensor_tensor(out=sxg[:, i:i + 1], in0=st_a[:, 2:3],
                                                 in1=q_deq[:, 0:1], op=ALU.mult)
                            vector.tensor_tensor(out=sxu[:, i:i + 1], in0=st_a[:, 2:3],
                                                 in1=q_deq[:, 1:2], op=ALU.mult)
                            vector.tensor_scalar(out=tq[:], in0=xnt[:], scalar1=st_a[:, 3:4],
                                                 scalar2=MAGIC, op0=ALU.mult, op1=ALU.add)
                            c_v.inc(vector.tensor_scalar(out=tq[:], in0=tq[:],
                                                         scalar1=MAGIC - 128.0,
                                                         scalar2=MAGIC + 127.0,
                                                         op0=ALU.max, op1=ALU.min))
                            c_v.inc(vector.tensor_scalar(out=xq_tm[:, i, :], in0=tq[:],
                                                         scalar1=-MAGIC, scalar2=None,
                                                         op0=ALU.add))
                            c_v.inc(vector.tensor_copy(out=st_a[:, 4:5], in_=st_a[:, 2:3]))

                    @blk.scalar
                    def _(scalar):
                        scalar = SafeEng(scalar, h_s, sstate)
                        m_sy.wait(scalar, xo_val[0])
                        for i in range(NT):
                            c_s.inc(scalar.activation(out=sqj2[:], in_=xob[:, i, :],
                                                      func=AF.Square, accum_out=st_b[:, 0:1]))
                            scalar.wait_ge(_c_v, v2a(i, 2))
                            c_s.inc(scalar.activation(out=st_e[:, 0:1], in_=st_d[:, 0:1],
                                                      func=AF.Sqrt))
                            scalar.wait_ge(_c_v, v2a(i, 3))
                            c_s.inc(scalar.activation(out=xnt[:], in_=xob[:, i, :],
                                                      func=AF.Identity, scale=st_c[:, 0:1],
                                                      bias=st_d[:, 0:1]))
                            if d_xn is not None:
                                scalar.wait_ge(_c_s, c_s.v)
                                m_dbg.dma(scalar.dma_start(
                                    out=d_xn[128 * i:128 * (i + 1), :], in_=xnt[:]))
                                scalar.wait_ge(_m_dbg, m_dbg.v)
                            scalar.wait_ge(_c_v, v2a(i, 7) - 1)
                    def t2a(j):
                        return t2a0 + j + 1

                    def v2a_ev(j):
                        return v2a0 + 7 * NT + j + 1

                    @blk.tensor
                    def _(tensor):
                        for j in range(2 * DCK):
                            ck, half = j // 2, j % 2
                            if j == 0:
                                tensor.wait_ge(_c_v, v2a0 + 7 * NT)
                            if j >= 2:
                                tensor.wait_ge(_c_v, v2a_ev(j - 2))
                            ps = psT2a if j % 2 == 0 else psT2b
                            for i2 in range(4):
                                i = 4 * half + i2
                                inst = tensor.transpose(ps[:, 128 * i2:128 * (i2 + 1)],
                                                        in_=xq_tm[:, i, 128 * ck:128 * (ck + 1)],
                                                        identity=c_idb[:])
                                if i2 == 3:
                                    c_t.inc(inst)

                    @blk.vector
                    def _(vector):
                        vector = SafeEng(vector, h_v, vstate)
                        for j in range(2 * DCK):
                            ck, half = j // 2, j % 2
                            vector.wait_ge(_c_t, t2a(j))
                            ps = psT2a if j % 2 == 0 else psT2b
                            c_v.inc(vector.tensor_copy(out=xq_fm[:, ck, 512 * half:512 * (half + 1)],
                                                       in_=ps[:]))
                    ph2a_v_end = c_v.v
                    ph2a_t_end = c_t.v

                if upto != 'ph2a':
                    with contextlib.ExitStack() as es2b:
                        wgs0 = es2b.enter_context(nc.sbuf_tensor("wgs0", [128, 8, 2, 512], BF16))
                        wgs1 = es2b.enter_context(nc.sbuf_tensor("wgs1", [128, 8, 2, 512], BF16))
                        wus0 = es2b.enter_context(nc.sbuf_tensor("wus0", [128, 8, 2, 512], BF16))
                        wus1 = es2b.enter_context(nc.sbuf_tensor("wus1", [128, 8, 2, 512], BF16))
                        gst = es2b.enter_context(nc.sbuf_tensor("gst", [128, NT, 512], BF16))
                        pst0 = es2b.enter_context(nc.sbuf_tensor("pst0", [128, NT, 512], BF16))
                        pst1 = es2b.enter_context(nc.sbuf_tensor("pst1", [128, NT, 512], BF16))
                        psG0 = es2b.enter_context(nc.psum_tensor("psG0", [128, 512], F32))
                        psG1 = es2b.enter_context(nc.psum_tensor("psG1", [128, 512], F32))
                        psU0 = es2b.enter_context(nc.psum_tensor("psU0", [128, 512], F32))
                        psU1 = es2b.enter_context(nc.psum_tensor("psU1", [128, 512], F32))

                        NOS = I0 // 512
                        t2b0, s2b0, v2b0 = c_t.v, c_s.v, c_v.v
                        spa_b, spb_b = m_pwa.v, m_pwb.v
                        wg_vals, wu_vals = [], []

                        def t_gate(os, t):
                            return t2b0 + 16 * os + 2 * t + 1

                        def t_up(os, t):
                            return t2b0 + 16 * os + 2 * t + 2

                        def s_silu(os, t):
                            return s2b0 + 8 * os + t + 1

                        def v_prod(os, t):
                            return v2b0 + 1 + 16 * os + 2 * t + 1

                        def v_rmx(os, t):
                            return v2b0 + 1 + 16 * os + 2 * t + 2

                        def wq_view(nm, rpr, cols, os, width):
                            base = WOFF[nm]
                            ckin = rpr // 128
                            return wqag_out[:, base:base + rpr * cols] \
                                .rearrange("r (ki p c) -> r ki p c", ki=ckin, p=128) \
                                [:, :, :, width * os:width * (os + 1)]

                        @blk.sync
                        def _(sync):
                            sync.wait_ge(_m_cc, cc_wq)
                            phase_join(sync)
                            for os in range(NOS):
                                wgs = wgs0 if os % 2 == 0 else wgs1
                                wus = wus0 if os % 2 == 0 else wus1
                                wsem = m_w2a if os % 2 == 0 else m_w2b
                                if os >= 2:
                                    sync.wait_ge(_c_t, t_up(os - 2, NT - 1))
                                for ki in range(2):
                                    wsem.dma(sync.dma_start(
                                        out=wgs[:, :, ki, :],
                                        in_=wq_view("wg", 256, I0, os, 512)[:, ki]
                                        .rearrange("r p c -> p r c")))
                                    wsem.dma(sync.dma_start(
                                        out=wus[:, :, ki, :],
                                        in_=wq_view("wu", 256, I0, os, 512)[:, ki]
                                        .rearrange("r p c -> p r c")))
                                wg_vals.append(wsem.v)

                        @blk.tensor
                        def _(tensor):
                            for os in range(NOS):
                                wgs = wgs0 if os % 2 == 0 else wgs1
                                wus = wus0 if os % 2 == 0 else wus1
                                wsem = m_w2a if os % 2 == 0 else m_w2b
                                wsem.wait(tensor, wg_vals[os])
                                for t in range(NT):
                                    gt = 8 * os + t
                                    psG = psG0 if gt % 2 == 0 else psG1
                                    psU = psU0 if gt % 2 == 0 else psU1
                                    if gt >= 2:
                                        tensor.wait_ge(_c_s, s_silu((gt - 2) // 8, (gt - 2) % 8))
                                        tensor.wait_ge(_c_v, v_prod((gt - 2) // 8, (gt - 2) % 8))
                                    for ck in range(DCK):
                                        inst = tensor.matmul(psG[:], lhsT=xq_fm[:, ck, 128 * t:128 * (t + 1)],
                                                             rhs=wgs[:, ck // 2, ck % 2, :],
                                                             start=(ck == 0), stop=(ck == DCK - 1))
                                        if ck == DCK - 1:
                                            c_t.inc(inst)
                                    for ck in range(DCK):
                                        inst = tensor.matmul(psU[:], lhsT=xq_fm[:, ck, 128 * t:128 * (t + 1)],
                                                             rhs=wus[:, ck // 2, ck % 2, :],
                                                             start=(ck == 0), stop=(ck == DCK - 1))
                                        if ck == DCK - 1:
                                            c_t.inc(inst)

                        @blk.scalar
                        def _(scalar):
                            scalar = SafeEng(scalar, h_s, sstate)
                            for os in range(NOS):
                                for t in range(NT):
                                    gt = 8 * os + t
                                    psG = psG0 if gt % 2 == 0 else psG1
                                    scalar.wait_ge(_c_t, t_gate(os, t))
                                    if os >= 1:
                                        scalar.wait_ge(_c_v, v_prod(os - 1, t))
                                    c_s.inc(scalar.activation(out=gst[:, t, :], in_=psG[:],
                                                              func=AF.Silu, scale=sxg[:, t:t + 1]))

                        @blk.vector
                        def _(vector):
                            vector = SafeEng(vector, h_v, vstate)
                            c_v.inc(vector.tensor_scalar(out=rmx[:], in0=vjk[:, 0:8],
                                                 scalar1=0.0, scalar2=None, op0=ALU.mult))
                            for os in range(NOS):
                                pst = pst0 if os % 2 == 0 else pst1
                                psem = m_spa if os % 2 == 0 else m_spb
                                for t in range(NT):
                                    gt = 8 * os + t
                                    psU = psU0 if gt % 2 == 0 else psU1
                                    vector.wait_ge(_c_t, t_up(os, t))
                                    vector.wait_ge(_c_s, s_silu(os, t))
                                    if os >= 2 and t == 0:
                                        vector.wait_ge(_m_pwa if os % 2 == 0 else _m_pwb,
                                                       (spa_b if os % 2 == 0 else spb_b) + 16 * ((os - 2) // 2 + 1))
                                    c_v.inc(vector.scalar_tensor_tensor(out=pst[:, t, :], in0=psU[:],
                                                                        scalar=sxu[:, t:t + 1],
                                                                        in1=gst[:, t, :],
                                                                        op0=ALU.mult, op1=ALU.mult))
                                    vector.reduce_max(out=st_a[:, 8:9], in_=pst[:, t, :], axis=AX,
                                                      apply_absolute_value=True)
                                    c_v.inc(vector.tensor_tensor(out=rmx[:, t:t + 1],
                                                                 in0=rmx[:, t:t + 1],
                                                                 in1=st_a[:, 8:9], op=ALU.max))

                        @blk.gpsimd
                        def _(gpsimd):
                            for os in range(NOS):
                                pst = pst0 if os % 2 == 0 else pst1
                                psem = m_pwa if os % 2 == 0 else m_pwb
                                gpsimd.wait_ge(_c_v, v_rmx(os, NT - 1))
                                psem.dma(gpsimd.dma_start(
                                    out=prodd.rearrange("(i p) c -> p i c", p=128)
                                    [:, :, 512 * os:512 * (os + 1)],
                                    in_=pst[:]))
                        prod_w_vals = (m_spa.v, m_spb.v)

                        @blk.vector
                        def _(vector):
                            vector = SafeEng(vector, h_v, vstate)
                            vector.wait_ge(_c_v, v_rmx(NOS - 1, NT - 1))
                            vector.tensor_scalar(out=rmx[:], in0=rmx[:], scalar1=1.0 / 127.0,
                                                 scalar2=EPS, op0=ALU.mult, op1=ALU.add)
                            c_v.inc(vector.reciprocal(out=invsp[:], in_=rmx[:]))
                            c_v.inc(vector.tensor_scalar(out=spd[:], in0=rmx[:],
                                                         scalar1=q_deq[:, 2:3], scalar2=None,
                                                         op0=ALU.mult))
                        ph2b_v_end = c_v.v


                    if upto != 'ph2b':
                        with contextlib.ExitStack() as es2c:
                            wdh = es2c.enter_context(nc.sbuf_tensor("wdh", [128, 8, 8, 512], BF16))
                            prt0 = es2c.enter_context(nc.sbuf_tensor("prt0", [128, I0], BF16))
                            prt1 = es2c.enter_context(nc.sbuf_tensor("prt1", [128, I0], BF16))
                            tq2 = es2c.enter_context(nc.sbuf_tensor("tq2", [128, 2048], F32))
                            pq_tm = es2c.enter_context(nc.sbuf_tensor("pq_tm", [128, I0], BF16))
                            pq_fm = es2c.enter_context(nc.sbuf_tensor("pq_fm", [128, 64, 128], BF16))
                            g0h = es2c.enter_context(nc.sbuf_tensor("g0h", [128, 512], F32))
                            sxt0 = es2c.enter_context(nc.sbuf_tensor("sxt0", [128, 1024], F32))
                            sxt1 = es2c.enter_context(nc.sbuf_tensor("sxt1", [128, 1024], F32))
                            x1st0 = es2c.enter_context(nc.sbuf_tensor("x1st0", [128, 512], F32))
                            x1st1 = es2c.enter_context(nc.sbuf_tensor("x1st1", [128, 512], F32))
                            psT2c0 = es2c.enter_context(nc.psum_tensor("psT2c0", [128, 512], BF16))
                            psT2c1 = es2c.enter_context(nc.psum_tensor("psT2c1", [128, 512], BF16))
                            psD0 = es2c.enter_context(nc.psum_tensor("psD0", [128, 512], F32))
                            psD1 = es2c.enter_context(nc.psum_tensor("psD1", [128, 512], F32))

                            t2c0, v2c0, s2c0 = c_t.v, c_v.v, c_s.v
                            w2a0, w2b0 = m_w2a.v, m_w2b.v
                            spa0, spb0 = m_spa.v, m_spb.v
                            oa0, ob0 = m_oa.v, m_ob.v
                            xa0_, xb0_ = m_xa.v, m_xb.v
                            q0_ = [s_.v for s_ in m_qs]

                            wdh_vals = []

                            def wdh_val(q4):
                                return wdh_vals[q4]

                            def prt_val(q4, t):
                                return (spa0 if t % 2 == 0 else spb0) + 16 * (4 * q4 + t // 2 + 1)

                            def pqw_val(t):
                                return (oa0 if t % 2 == 0 else ob0) + 16 * (t // 2 + 1)

                            def sxt_val(q4, t):
                                return (xa0_ if t % 2 == 0 else xb0_) + 32 * (4 * q4 + t // 2 + 1)

                            def x1w_val(q4, t):
                                return q0_[t % 2] + 16 * (4 * q4 + t // 2 + 1)

                            def t_tr2c(q4, t, j):
                                return t2c0 + 17 * (8 * q4 + t) + j + 1

                            def t_dn(q4, t):
                                return t2c0 + 17 * (8 * q4 + t) + 17

                            def vn(q4):
                                return 18 if q4 == 0 else 17

                            def v_base2c(q4, t):
                                return v2c0 + (18 * 8 if q4 >= 1 else 0) + 17 * 8 * max(0, q4 - 1) + vn(q4) * t

                            def v_quant(t):
                                return v2c0 + 18 * t + 1

                            def v_ev2c(q4, t, j):
                                return v_base2c(q4, t) + (1 if q4 == 0 else 0) + j + 1

                            def v_gat(q4, t):
                                return v_base2c(q4, t) + vn(q4)

                            def s_dn(q4, t):
                                return s2c0 + 8 * q4 + t + 1

                            @blk.sync
                            def _(sync):
                                phase_join(sync)
                                for q4 in range(4):
                                    wsem = m_w2a if q4 % 2 == 0 else m_w2b
                                    if q4 >= 1:
                                        sync.wait_ge(_c_t, t_dn(q4 - 1, NT - 1))
                                    for ki in range(8):
                                        wsem.dma(sync.dma_start(
                                            out=wdh[:, :, ki, :],
                                            in_=wq_view("wd", 1024, D, q4, 512)[:, ki]
                                            .rearrange("r p c -> p r c")))
                                    wdh_vals.append(wsem.v)
                                    for t in range(NT):
                                        prt = prt0 if t % 2 == 0 else prt1
                                        psem = m_spa if t % 2 == 0 else m_spb
                                        if q4 == 0:
                                            if t >= 2:
                                                sync.wait_ge(_c_v, v_quant(t - 2))
                                        else:
                                            prev = (q4, t - 2) if t >= 2 else (q4 - 1, NT - 2 + t)
                                            sync.wait_ge(_c_t, t_tr2c(prev[0], prev[1], 15))
                                            if q4 == 1 and t < 2:
                                                sync.wait_ge(_m_oa, oa0 + 16 * 4)
                                                sync.wait_ge(_m_ob, ob0 + 16 * 4)
                                        srcten = prodd if q4 == 0 else pqd
                                        psem.dma(sync.dma_start(
                                            out=prt[:], in_=srcten[128 * t:128 * (t + 1), :]))
                                        sxt = sxt0 if t % 2 == 0 else sxt1
                                        xsem = m_xa if t % 2 == 0 else m_xb
                                        if 8 * q4 + t >= 2:
                                            pj = 8 * q4 + t - 2
                                            sync.wait_ge(_c_v, v_gat(pj // 8, pj % 8))
                                        xsem.dma(sync.dma_start(
                                            out=sxt[:, 0:512],
                                            in_=sd[128 * t:128 * (t + 1), 512 * q4:512 * (q4 + 1)]))
                                        xsem.dma(sync.dma_start(
                                            out=sxt[:, 512:1024],
                                            in_=xo[128 * t:128 * (t + 1), 512 * q4:512 * (q4 + 1)]))

                            @blk.vector
                            def _(vector):
                                vector = SafeEng(vector, h_v, vstate)
                                vector.wait_ge(_c_v, ph2b_v_end)
                                for q4 in range(4):
                                    for t in range(NT):
                                        prt = prt0 if t % 2 == 0 else prt1
                                        psem = m_spa if t % 2 == 0 else m_spb
                                        psem.wait(vector, prt_val(q4, t))
                                        PQ = pq_tm if q4 == 0 else prt
                                        if q4 == 0:
                                            if t >= 1:
                                                vector.wait_ge(_c_t, t_tr2c(q4, t - 1, 15))
                                                pqsem_p = m_oa if (t - 1) % 2 == 0 else m_ob
                                                pqsem_p.wait(vector, pqw_val(t - 1))
                                            for pc in range(4):
                                                sl = slice(2048 * pc, 2048 * (pc + 1))
                                                vector.tensor_scalar(out=tq2[:], in0=prt[:, sl],
                                                                     scalar1=invsp[:, t:t + 1],
                                                                     scalar2=MAGIC, op0=ALU.mult,
                                                                     op1=ALU.add)
                                                vector.tensor_scalar(out=tq2[:], in0=tq2[:],
                                                                     scalar1=MAGIC - 128.0,
                                                                     scalar2=MAGIC + 127.0,
                                                                     op0=ALU.max, op1=ALU.min)
                                                inst = vector.tensor_scalar(out=pq_tm[:, sl], in0=tq2[:],
                                                                            scalar1=-MAGIC, scalar2=None,
                                                                            op0=ALU.add)
                                            c_v.inc(inst)
                                        for j in range(16):
                                            vector.wait_ge(_c_t, t_tr2c(q4, t, j))
                                            ps = psT2c0 if j % 2 == 0 else psT2c1
                                            c_v.inc(vector.tensor_copy(
                                                out=pq_fm[:, 4 * j:4 * (j + 1), :]
                                                .rearrange("p a b -> p (a b)"), in_=ps[:]))
                                        sxt = sxt0 if t % 2 == 0 else sxt1
                                        xsem = m_xa if t % 2 == 0 else m_xb
                                        x1st = x1st0 if t % 2 == 0 else x1st1
                                        xqsem = m_qs[t % 2]
                                        vector.wait_ge(_c_s, s_dn(q4, t))
                                        xsem.wait(vector, sxt_val(q4, t))
                                        if 8 * q4 + t >= 2:
                                            pj = 8 * q4 + t - 2
                                            xqsem.wait(vector, x1w_val(pj // 8, pj % 8)
                                                       if (pj % 8) % 2 == t % 2 else xqsem.v)
                                        vector.tensor_tensor(out=x1st[:], in0=g0h[:],
                                                             in1=sxt[:, 0:512], op=ALU.mult)
                                        c_v.inc(vector.tensor_tensor(out=x1st[:], in0=x1st[:],
                                                                     in1=sxt[:, 512:1024], op=ALU.add))

                            @blk.tensor
                            def _(tensor):
                                for q4 in range(4):
                                    wsem = m_w2a if q4 % 2 == 0 else m_w2b
                                    wsem.wait(tensor, wdh_val(q4))
                                    for t in range(NT):
                                        prt = prt0 if t % 2 == 0 else prt1
                                        psem = m_spa if t % 2 == 0 else m_spb
                                        PQ = pq_tm if q4 == 0 else prt
                                        if q4 == 0:
                                            tensor.wait_ge(_c_v, v_quant(t))
                                        else:
                                            psem.wait(tensor, prt_val(q4, t))
                                        for j in range(16):
                                            gj = 16 * (8 * q4 + t) + j
                                            if gj >= 2:
                                                pj = gj - 2
                                                qt_, jj = divmod(pj, 16)
                                                tensor.wait_ge(_c_v, v_ev2c(qt_ // 8, qt_ % 8, jj))
                                            ps = psT2c0 if j % 2 == 0 else psT2c1
                                            for i2 in range(4):
                                                oc = 4 * j + i2
                                                inst = tensor.transpose(
                                                    ps[:, 128 * i2:128 * (i2 + 1)],
                                                    in_=PQ[:, 128 * oc:128 * (oc + 1)],
                                                    identity=c_idb[:])
                                                if i2 == 3:
                                                    c_t.inc(inst)
                                        tensor.wait_ge(_c_v, v_ev2c(q4, t, 15))
                                        if 8 * q4 + t >= 2:
                                            pj = 8 * q4 + t - 2
                                            tensor.wait_ge(_c_s, s_dn(pj // 8, pj % 8))
                                        psD = psD0 if (8 * q4 + t) % 2 == 0 else psD1
                                        for oc in range(64):
                                            inst = tensor.matmul(psD[:], lhsT=pq_fm[:, oc, :],
                                                                 rhs=wdh[:, oc // 8, oc % 8, :],
                                                                 start=(oc == 0), stop=(oc == 63))
                                            if oc == 63:
                                                c_t.inc(inst)

                            @blk.scalar
                            def _(scalar):
                                scalar = SafeEng(scalar, h_s, sstate)
                                for q4 in range(4):
                                    for t in range(NT):
                                        if q4 == 0:
                                            pqsem = m_oa if t % 2 == 0 else m_ob
                                            scalar.wait_ge(_c_v, v_quant(t))
                                            pqsem.dma(scalar.dma_start(
                                                out=pqd[128 * t:128 * (t + 1), :], in_=pq_tm[:]))
                                        scalar.wait_ge(_c_t, t_dn(q4, t))
                                        if 8 * q4 + t >= 1:
                                            pj = 8 * q4 + t - 1
                                            scalar.wait_ge(_c_v, v_gat(pj // 8, pj % 8))
                                        psD = psD0 if (8 * q4 + t) % 2 == 0 else psD1
                                        c_s.inc(scalar.activation(out=g0h[:], in_=psD[:], func=AF.Silu,
                                                                  scale=spd[:, t:t + 1]))
                                        if d_g0 is not None:
                                            scalar.wait_ge(_c_s, c_s.v)
                                            m_dbg.dma(scalar.dma_start(
                                                out=d_g0[128 * t:128 * (t + 1), 512 * q4:512 * (q4 + 1)],
                                                in_=g0h[:]))
                                            scalar.wait_ge(_m_dbg, m_dbg.v)
                                        x1st = x1st0 if t % 2 == 0 else x1st1
                                        xqsem = m_qs[t % 2]
                                        scalar.wait_ge(_c_v, v_gat(q4, t))
                                        xqsem.dma(scalar.dma_start(
                                            out=x1d[128 * t:128 * (t + 1), 512 * q4:512 * (q4 + 1)],
                                            in_=x1st[:]))
                                        if d_x1 is not None:
                                            m_dbg.dma(scalar.dma_start(
                                                out=d_x1[128 * t:128 * (t + 1), 512 * q4:512 * (q4 + 1)],
                                                in_=x1st[:]))
                                            scalar.wait_ge(_m_dbg, m_dbg.v)
                            ph2_x1_vals = [s_.v for s_ in m_qs]


        if upto not in ('ph1a', 'ph1b', 'ph1cload', 'ph1c', 'ph2a', 'ph2b', 'ph2c'):
            with contextlib.ExitStack() as es4:
                sx1 = es4.enter_context(nc.sbuf_tensor("sx1", [128, 8], F32))
                sx3 = es4.enter_context(nc.sbuf_tensor("sx3", [128, 8], F32))
                rmx1 = es4.enter_context(nc.sbuf_tensor("rmx1", [128, 8], F32))
                spd1 = es4.enter_context(nc.sbuf_tensor("spd1", [128, 8], F32))
                invsp1 = es4.enter_context(nc.sbuf_tensor("invsp1", [128, 8], F32))

                with contextlib.ExitStack() as es4ab:
                    xq1_fm = es4ab.enter_context(
                        nc.sbuf_tensor("xq1_fm", [128, DCK, 1024], BF16))

                    with contextlib.ExitStack() as es4a:
                        x1b = es4a.enter_context(nc.sbuf_tensor("x1b", [128, NT, D], F32))
                        xnt0 = es4a.enter_context(nc.sbuf_tensor("xnt0", [128, D], F32))
                        xnt1 = es4a.enter_context(nc.sbuf_tensor("xnt1", [128, D], F32))
                        tq4 = es4a.enter_context(nc.sbuf_tensor("tq4", [128, D], F32))
                        sqj4 = es4a.enter_context(nc.sbuf_tensor("sqj4", [128, D], F32))
                        xq1_tm = es4a.enter_context(nc.sbuf_tensor("xq1_tm", [128, NT, D], BF16))
                        psT4a = es4a.enter_context(nc.psum_tensor("psT4a", [128, 512], BF16))
                        psT4b = es4a.enter_context(nc.psum_tensor("psT4b", [128, 512], BF16))

                        v4a0, s4a0, t4a0 = c_v.v, c_s.v, c_t.v
                        oa4_0, ob4_0 = m_oa.v, m_ob.v
                        x1_val = []

                        @blk.sync
                        def _(sync):
                            phase_join(sync)
                            m_sy.dma(sync.dma_start(
                                out=x1b[:], in_=x1d.rearrange("(i p) d -> p i d", p=128)))
                            x1_val.append(m_sy.v)

                        def v4a(i, k):
                            return v4a0 + 7 * i + k

                        def s4aa(i, k):
                            return s4a0 + 3 * i + k

                        def xn1w_val(i):
                            return (oa4_0 if i % 2 == 0 else ob4_0) + 16 * (i // 2 + 1)

                        @blk.vector
                        def _(vector):
                            vector = SafeEng(vector, h_v, vstate)
                            m_sy.wait(vector, x1_val[0])
                            for i in range(NT):
                                xnt = xnt0 if i % 2 == 0 else xnt1
                                vector.reduce_sum(out=st_a[:, 0:1], in_=x1b[:, i, :], axis=AX)
                                c_v.inc(vector.tensor_copy(out=st_a[:, 1:2], in_=st_a[:, 0:1]))
                                vector.wait_ge(_c_s, s4aa(i, 1))
                                vector.tensor_scalar(out=st_a[:, 0:1], in0=st_a[:, 0:1],
                                                     scalar1=1.0 / D, scalar2=None, op0=ALU.mult)
                                vector.tensor_scalar(out=st_b[:, 0:1], in0=st_b[:, 0:1],
                                                     scalar1=1.0 / D, scalar2=None, op0=ALU.mult)
                                vector.tensor_tensor(out=st_c[:, 0:1], in0=st_a[:, 0:1],
                                                     in1=st_a[:, 0:1], op=ALU.mult)
                                vector.tensor_tensor(out=st_c[:, 0:1], in0=st_b[:, 0:1],
                                                     in1=st_c[:, 0:1], op=ALU.subtract)
                                vector.tensor_scalar(out=st_c[:, 0:1], in0=st_c[:, 0:1],
                                                     scalar1=LNEPS, scalar2=None, op0=ALU.add)
                                c_v.inc(vector.reciprocal(out=st_d[:, 0:1], in_=st_c[:, 0:1]))
                                vector.wait_ge(_c_s, s4aa(i, 2))
                                vector.tensor_tensor(out=st_f[:, 0:1], in0=st_e[:, 0:1],
                                                     in1=st_e[:, 0:1], op=ALU.mult)
                                vector.tensor_tensor(out=st_f[:, 0:1], in0=st_f[:, 0:1],
                                                     in1=st_c[:, 0:1], op=ALU.mult)
                                vector.tensor_scalar(out=st_f[:, 0:1], in0=st_f[:, 0:1],
                                                     scalar1=-0.5, scalar2=1.5, op0=ALU.mult,
                                                     op1=ALU.add)
                                vector.tensor_tensor(out=st_c[:, 0:1], in0=st_e[:, 0:1],
                                                     in1=st_f[:, 0:1], op=ALU.mult)
                                c_v.inc(vector.scalar_tensor_tensor(out=st_d[:, 0:1],
                                                                    in0=st_a[:, 0:1],
                                                                    scalar=-1.0, in1=st_c[:, 0:1],
                                                                    op0=ALU.mult, op1=ALU.mult))
                                vector.wait_ge(_c_s, s4aa(i, 3))
                                vector.reduce_max(out=st_a[:, 2:3], in_=xnt[:], axis=AX,
                                                  apply_absolute_value=True)
                                vector.tensor_scalar(out=st_a[:, 2:3], in0=st_a[:, 2:3],
                                                     scalar1=EPS, scalar2=None, op0=ALU.max)
                                c_v.inc(vector.reciprocal(out=st_a[:, 3:4], in_=st_a[:, 2:3]))
                                vector.tensor_scalar(out=st_a[:, 4:5], in0=st_a[:, 3:4],
                                                     scalar1=127.0, scalar2=None, op0=ALU.mult)
                                vector.tensor_scalar(out=st_a[:, 5:6], in0=st_a[:, 2:3],
                                                     scalar1=1.0 / 127.0, scalar2=None,
                                                     op0=ALU.mult)
                                vector.tensor_tensor(out=sx1[:, i:i + 1], in0=st_a[:, 5:6],
                                                     in1=q_deq[:, 3:4], op=ALU.mult)
                                vector.tensor_tensor(out=sx3[:, i:i + 1], in0=st_a[:, 5:6],
                                                     in1=q_deq[:, 4:5], op=ALU.mult)
                                vector.tensor_scalar(out=tq4[:], in0=xnt[:],
                                                     scalar1=st_a[:, 4:5], scalar2=MAGIC,
                                                     op0=ALU.mult, op1=ALU.add)
                                c_v.inc(vector.tensor_scalar(out=tq4[:], in0=tq4[:],
                                                             scalar1=MAGIC - 128.0,
                                                             scalar2=MAGIC + 127.0,
                                                             op0=ALU.max, op1=ALU.min))
                                c_v.inc(vector.tensor_scalar(out=xq1_tm[:, i, :], in0=tq4[:],
                                                             scalar1=-MAGIC, scalar2=None,
                                                             op0=ALU.add))
                                c_v.inc(vector.tensor_copy(out=st_a[:, 6:7], in_=st_a[:, 2:3]))

                        @blk.scalar
                        def _(scalar):
                            scalar = SafeEng(scalar, h_s, sstate)
                            m_sy.wait(scalar, x1_val[0])
                            for i in range(NT):
                                xnt = xnt0 if i % 2 == 0 else xnt1
                                xnsem = m_oa if i % 2 == 0 else m_ob
                                c_s.inc(scalar.activation(out=sqj4[:], in_=x1b[:, i, :],
                                                          func=AF.Square,
                                                          accum_out=st_b[:, 0:1]))
                                scalar.wait_ge(_c_v, v4a(i, 2))
                                c_s.inc(scalar.activation(out=st_e[:, 0:1], in_=st_d[:, 0:1],
                                                          func=AF.Sqrt))
                                scalar.wait_ge(_c_v, v4a(i, 3))
                                if i >= 2:
                                    xnsem.wait(scalar, xn1w_val(i - 2))
                                c_s.inc(scalar.activation(out=xnt[:], in_=x1b[:, i, :],
                                                          func=AF.Identity, scale=st_c[:, 0:1],
                                                          bias=st_d[:, 0:1]))
                                scalar.wait_ge(_c_s, c_s.v)
                                xnsem.dma(scalar.dma_start(
                                    out=xn1d[128 * i:128 * (i + 1), :], in_=xnt[:]))
                                if d_xn1 is not None:
                                    m_dbg.dma(scalar.dma_start(
                                        out=d_xn1[128 * i:128 * (i + 1), :], in_=xnt[:]))
                                    scalar.wait_ge(_m_dbg, m_dbg.v)
                                scalar.wait_ge(_c_v, v4a(i, 7) - 1)

                        def t4a(j):
                            return t4a0 + j + 1

                        def v4a_ev(j):
                            return v4a0 + 7 * NT + j + 1

                        @blk.tensor
                        def _(tensor):
                            for j in range(2 * DCK):
                                ck, half = j // 2, j % 2
                                if j == 0:
                                    tensor.wait_ge(_c_v, v4a0 + 7 * NT)
                                if j >= 2:
                                    tensor.wait_ge(_c_v, v4a_ev(j - 2))
                                ps = psT4a if j % 2 == 0 else psT4b
                                for i2 in range(4):
                                    i = 4 * half + i2
                                    inst = tensor.transpose(
                                        ps[:, 128 * i2:128 * (i2 + 1)],
                                        in_=xq1_tm[:, i, 128 * ck:128 * (ck + 1)],
                                        identity=c_idb[:])
                                    if i2 == 3:
                                        c_t.inc(inst)

                        @blk.vector
                        def _(vector):
                            vector = SafeEng(vector, h_v, vstate)
                            for j in range(2 * DCK):
                                ck, half = j // 2, j % 2
                                vector.wait_ge(_c_t, t4a(j))
                                ps = psT4a if j % 2 == 0 else psT4b
                                c_v.inc(vector.tensor_copy(
                                    out=xq1_fm[:, ck, 512 * half:512 * (half + 1)], in_=ps[:]))

                    with contextlib.ExitStack() as es4b:
                        w1s0 = es4b.enter_context(nc.sbuf_tensor("w1s0", [128, 8, 2, 512], BF16))
                        w1s1 = es4b.enter_context(nc.sbuf_tensor("w1s1", [128, 8, 2, 512], BF16))
                        w3s0 = es4b.enter_context(nc.sbuf_tensor("w3s0", [128, 8, 2, 512], BF16))
                        w3s1 = es4b.enter_context(nc.sbuf_tensor("w3s1", [128, 8, 2, 512], BF16))
                        gst1 = es4b.enter_context(nc.sbuf_tensor("gst1", [128, NT, 512], BF16))
                        pst40 = es4b.enter_context(nc.sbuf_tensor("pst40", [128, NT, 512], BF16))
                        pst41 = es4b.enter_context(nc.sbuf_tensor("pst41", [128, NT, 512], BF16))
                        psG40 = es4b.enter_context(nc.psum_tensor("psG40", [128, 512], F32))
                        psG41 = es4b.enter_context(nc.psum_tensor("psG41", [128, 512], F32))
                        psU40 = es4b.enter_context(nc.psum_tensor("psU40", [128, 512], F32))
                        psU41 = es4b.enter_context(nc.psum_tensor("psU41", [128, 512], F32))

                        NOS1 = I1 // 512
                        t4b0, s4b0, v4b0 = c_t.v, c_s.v, c_v.v
                        pwa_b, pwb_b = m_pwa.v, m_pwb.v
                        w1_vals = []

                        def t_g1(os, t):
                            return t4b0 + 16 * os + 2 * t + 1

                        def t_u1(os, t):
                            return t4b0 + 16 * os + 2 * t + 2

                        def s_si1(os, t):
                            return s4b0 + 8 * os + t + 1

                        def v_pr1(os, t):
                            return v4b0 + 1 + 16 * os + 2 * t + 1

                        def v_rm1(os, t):
                            return v4b0 + 1 + 16 * os + 2 * t + 2

                        @blk.sync
                        def _(sync):
                            phase_join(sync)
                            for os in range(NOS1):
                                w1s = w1s0 if os % 2 == 0 else w1s1
                                w3s = w3s0 if os % 2 == 0 else w3s1
                                wsem = m_w2a if os % 2 == 0 else m_w2b
                                if os >= 2:
                                    sync.wait_ge(_c_t, t_u1(os - 2, NT - 1))
                                for ki in range(2):
                                    wsem.dma(sync.dma_start(
                                        out=w1s[:, :, ki, :],
                                        in_=wq_view("w1", 256, I1, os, 512)[:, ki]
                                        .rearrange("r p c -> p r c")))
                                    wsem.dma(sync.dma_start(
                                        out=w3s[:, :, ki, :],
                                        in_=wq_view("w3", 256, I1, os, 512)[:, ki]
                                        .rearrange("r p c -> p r c")))
                                w1_vals.append(wsem.v)

                        @blk.tensor
                        def _(tensor):
                            for os in range(NOS1):
                                w1s = w1s0 if os % 2 == 0 else w1s1
                                w3s = w3s0 if os % 2 == 0 else w3s1
                                wsem = m_w2a if os % 2 == 0 else m_w2b
                                wsem.wait(tensor, w1_vals[os])
                                for t in range(NT):
                                    gt = 8 * os + t
                                    psG = psG40 if gt % 2 == 0 else psG41
                                    psU = psU40 if gt % 2 == 0 else psU41
                                    if gt >= 2:
                                        tensor.wait_ge(_c_s, s_si1((gt - 2) // 8, (gt - 2) % 8))
                                        tensor.wait_ge(_c_v, v_pr1((gt - 2) // 8, (gt - 2) % 8))
                                    for ck in range(DCK):
                                        inst = tensor.matmul(
                                            psG[:], lhsT=xq1_fm[:, ck, 128 * t:128 * (t + 1)],
                                            rhs=w1s[:, ck // 2, ck % 2, :],
                                            start=(ck == 0), stop=(ck == DCK - 1))
                                        if ck == DCK - 1:
                                            c_t.inc(inst)
                                    for ck in range(DCK):
                                        inst = tensor.matmul(
                                            psU[:], lhsT=xq1_fm[:, ck, 128 * t:128 * (t + 1)],
                                            rhs=w3s[:, ck // 2, ck % 2, :],
                                            start=(ck == 0), stop=(ck == DCK - 1))
                                        if ck == DCK - 1:
                                            c_t.inc(inst)

                        @blk.scalar
                        def _(scalar):
                            scalar = SafeEng(scalar, h_s, sstate)
                            for os in range(NOS1):
                                for t in range(NT):
                                    gt = 8 * os + t
                                    psG = psG40 if gt % 2 == 0 else psG41
                                    scalar.wait_ge(_c_t, t_g1(os, t))
                                    if os >= 1:
                                        scalar.wait_ge(_c_v, v_pr1(os - 1, t))
                                    c_s.inc(scalar.activation(out=gst1[:, t, :], in_=psG[:],
                                                              func=AF.Silu,
                                                              scale=sx1[:, t:t + 1]))

                        @blk.vector
                        def _(vector):
                            vector = SafeEng(vector, h_v, vstate)
                            c_v.inc(vector.tensor_scalar(out=rmx1[:], in0=vjk[:, 0:8],
                                                         scalar1=0.0, scalar2=None,
                                                         op0=ALU.mult))
                            for os in range(NOS1):
                                pst = pst40 if os % 2 == 0 else pst41
                                for t in range(NT):
                                    gt = 8 * os + t
                                    psU = psU40 if gt % 2 == 0 else psU41
                                    vector.wait_ge(_c_t, t_u1(os, t))
                                    vector.wait_ge(_c_s, s_si1(os, t))
                                    if os >= 2 and t == 0:
                                        vector.wait_ge(_m_pwa if os % 2 == 0 else _m_pwb,
                                                       (pwa_b if os % 2 == 0 else pwb_b)
                                                       + 16 * ((os - 2) // 2 + 1))
                                    c_v.inc(vector.scalar_tensor_tensor(out=pst[:, t, :],
                                                                        in0=psU[:],
                                                                        scalar=sx3[:, t:t + 1],
                                                                        in1=gst1[:, t, :],
                                                                        op0=ALU.mult,
                                                                        op1=ALU.mult))
                                    vector.reduce_max(out=st_a[:, 8:9], in_=pst[:, t, :],
                                                      axis=AX, apply_absolute_value=True)
                                    c_v.inc(vector.tensor_tensor(out=rmx1[:, t:t + 1],
                                                                 in0=rmx1[:, t:t + 1],
                                                                 in1=st_a[:, 8:9], op=ALU.max))

                        @blk.gpsimd
                        def _(gpsimd):
                            for os in range(NOS1):
                                pst = pst40 if os % 2 == 0 else pst41
                                psem = m_pwa if os % 2 == 0 else m_pwb
                                gpsimd.wait_ge(_c_v, v_rm1(os, NT - 1))
                                psem.dma(gpsimd.dma_start(
                                    out=hdd.rearrange("(i p) c -> p i c", p=128)
                                    [:, :, 512 * os:512 * (os + 1)],
                                    in_=pst[:]))

                        @blk.vector
                        def _(vector):
                            vector = SafeEng(vector, h_v, vstate)
                            vector.wait_ge(_c_v, v_rm1(NOS1 - 1, NT - 1))
                            vector.tensor_scalar(out=rmx1[:], in0=rmx1[:], scalar1=EPS,
                                                 scalar2=None, op0=ALU.max)
                            c_v.inc(vector.reciprocal(out=invsp1[:], in_=rmx1[:]))
                            vector.tensor_scalar(out=invsp1[:], in0=invsp1[:], scalar1=127.0,
                                                 scalar2=None, op0=ALU.mult)
                            vector.tensor_scalar(out=rmx1[:], in0=rmx1[:], scalar1=1.0 / 127.0,
                                                 scalar2=None, op0=ALU.mult)
                            c_v.inc(vector.tensor_scalar(out=spd1[:], in0=rmx1[:],
                                                         scalar1=q_deq[:, 5:6], scalar2=None,
                                                         op0=ALU.mult))
                        ph4b_v_end = c_v.v

                with contextlib.ExitStack() as es4c:
                    wdh2 = es4c.enter_context(nc.sbuf_tensor("wdh2", [128, 8, 6, 512], BF16))
                    hrt0 = es4c.enter_context(nc.sbuf_tensor("hrt0", [128, I1P], BF16))
                    hrt1 = es4c.enter_context(nc.sbuf_tensor("hrt1", [128, I1P], BF16))
                    tq42 = es4c.enter_context(nc.sbuf_tensor("tq42", [128, 2048], F32))
                    hq_tm = es4c.enter_context(nc.sbuf_tensor("hq_tm", [128, I1P], BF16))
                    hq_fm = es4c.enter_context(nc.sbuf_tensor("hq_fm", [128, 48, 128], BF16))
                    g1h = es4c.enter_context(nc.sbuf_tensor("g1h", [128, 512], F32))
                    sxt40 = es4c.enter_context(nc.sbuf_tensor("sxt40", [128, 1536], F32))
                    sxt41 = es4c.enter_context(nc.sbuf_tensor("sxt41", [128, 1536], F32))
                    x2st0 = es4c.enter_context(nc.sbuf_tensor("x2st0", [128, 512], F32))
                    x2st1 = es4c.enter_context(nc.sbuf_tensor("x2st1", [128, 512], F32))
                    s1h = es4c.enter_context(nc.sbuf_tensor("s1h", [128, 512], F32))
                    psT4c0 = es4c.enter_context(nc.psum_tensor("psT4c0", [128, 512], BF16))
                    psT4c1 = es4c.enter_context(nc.psum_tensor("psT4c1", [128, 512], BF16))
                    psD40 = es4c.enter_context(nc.psum_tensor("psD40", [128, 512], F32))
                    psD41 = es4c.enter_context(nc.psum_tensor("psD41", [128, 512], F32))

                    t4c0, v4c0, s4c0 = c_t.v, c_v.v, c_s.v
                    spa0_, spb0_ = m_spa.v, m_spb.v
                    oa0_, ob0_ = m_oa.v, m_ob.v
                    xa0__, xb0__ = m_xa.v, m_xb.v
                    q0__ = [s_.v for s_ in m_qs]
                    wdh2_vals = []
                    _cum4 = [2, 4, 7, 10]

                    def prt1_val(q4, t):
                        return (spa0_ if t % 2 == 0 else spb0_) + 16 * (4 * q4 + t // 2 + 1)

                    def hqw_val(t):
                        return (oa0_ if t % 2 == 0 else ob0_) + 16 * (t // 2 + 1)

                    def sxt4_val(q4, t):
                        return (xa0__ if t % 2 == 0 else xb0__) \
                            + 16 * (10 * q4 + _cum4[t // 2])

                    def t_tr4c(q4, t, j):
                        return t4c0 + 13 * (8 * q4 + t) + j + 1

                    def t_dn4(q4, t):
                        return t4c0 + 13 * (8 * q4 + t) + 13

                    def vn4(q4):
                        return 14 if q4 == 0 else 13

                    def v_base4c(q4, t):
                        return v4c0 + (14 * 8 if q4 >= 1 else 0) + 13 * 8 * max(0, q4 - 1) \
                            + vn4(q4) * t

                    def v_q4(t):
                        return v4c0 + 14 * t + 1

                    def v_ev4c(q4, t, j):
                        return v_base4c(q4, t) + (1 if q4 == 0 else 0) + j + 1

                    def v_gat4(q4, t):
                        return v_base4c(q4, t) + vn4(q4)

                    def s_dn4(q4, t):
                        return s4c0 + 8 * q4 + t + 1

                    @blk.sync
                    def _(sync):
                        phase_join(sync)
                        for q4 in range(4):
                            wsem = m_w2a if q4 % 2 == 0 else m_w2b
                            if q4 >= 1:
                                sync.wait_ge(_c_t, t_dn4(q4 - 1, NT - 1))
                            for ki in range(6):
                                wsem.dma(sync.dma_start(
                                    out=wdh2[:, :, ki, :],
                                    in_=wq_view("w2", 768, D, q4, 512)[:, ki]
                                    .rearrange("r p c -> p r c")))
                            wdh2_vals.append(wsem.v)
                            for t in range(NT):
                                prt = hrt0 if t % 2 == 0 else hrt1
                                psem = m_spa if t % 2 == 0 else m_spb
                                if q4 == 0:
                                    if t >= 2:
                                        sync.wait_ge(_c_v, v_q4(t - 2))
                                else:
                                    prev = (q4, t - 2) if t >= 2 else (q4 - 1, NT - 2 + t)
                                    sync.wait_ge(_c_t, t_tr4c(prev[0], prev[1], 11))
                                    if q4 == 1 and t < 2:
                                        sync.wait_ge(_m_oa, oa0_ + 16 * 4)
                                        sync.wait_ge(_m_ob, ob0_ + 16 * 4)
                                if q4 == 0:
                                    psem.dma(sync.dma_start(
                                        out=prt[:, 0:I1],
                                        in_=hdd[128 * t:128 * (t + 1), :]))
                                else:
                                    psem.dma(sync.dma_start(
                                        out=prt[:, 0:I1P],
                                        in_=hqd[128 * t:128 * (t + 1), :]))
                                sxt = sxt40 if t % 2 == 0 else sxt41
                                xsem = m_xa if t % 2 == 0 else m_xb
                                if 8 * q4 + t >= 2:
                                    pj = 8 * q4 + t - 2
                                    sync.wait_ge(_c_v, v_gat4(pj // 8, pj % 8))
                                xsem.dma(sync.dma_start(
                                    out=sxt[:, 0:512],
                                    in_=x1d[128 * t:128 * (t + 1), 512 * q4:512 * (q4 + 1)]))
                                xsem.dma(sync.dma_start(
                                    out=sxt[:, 512:1024],
                                    in_=xn1d[128 * t:128 * (t + 1), 512 * q4:512 * (q4 + 1)]))
                                if t >= 4:
                                    xsem.dma(sync.dma_start(
                                        out=sxt[:, 1024:1536],
                                        in_=xn1d[128 * (t - 4):128 * (t - 3),
                                                 512 * q4:512 * (q4 + 1)]))

                    @blk.vector
                    def _(vector):
                        vector = SafeEng(vector, h_v, vstate)
                        vector.wait_ge(_c_v, ph4b_v_end)
                        for q4 in range(4):
                            for t in range(NT):
                                prt = hrt0 if t % 2 == 0 else hrt1
                                psem = m_spa if t % 2 == 0 else m_spb
                                psem.wait(vector, prt1_val(q4, t))
                                if q4 == 0:
                                    if t >= 1:
                                        vector.wait_ge(_c_t, t_tr4c(q4, t - 1, 11))
                                        pq_p = m_oa if (t - 1) % 2 == 0 else m_ob
                                        pq_p.wait(vector, hqw_val(t - 1))
                                    for pc in range(3):
                                        w = min(2048, I1 - 2048 * pc)
                                        sl = slice(2048 * pc, 2048 * pc + w)
                                        vector.tensor_scalar(out=tq42[:, 0:w], in0=prt[:, sl],
                                                             scalar1=invsp1[:, t:t + 1],
                                                             scalar2=MAGIC, op0=ALU.mult,
                                                             op1=ALU.add)
                                        vector.tensor_scalar(out=tq42[:, 0:w], in0=tq42[:, 0:w],
                                                             scalar1=MAGIC - 128.0,
                                                             scalar2=MAGIC + 127.0,
                                                             op0=ALU.max, op1=ALU.min)
                                        vector.tensor_scalar(out=hq_tm[:, sl],
                                                             in0=tq42[:, 0:w],
                                                             scalar1=-MAGIC,
                                                             scalar2=None, op0=ALU.add)
                                    c_v.inc(vector.tensor_scalar(out=hq_tm[:, I1:I1P],
                                                                 in0=hq_tm[:, 0:I1P - I1],
                                                                 scalar1=0.0, scalar2=None,
                                                                 op0=ALU.mult))
                                for j in range(12):
                                    vector.wait_ge(_c_t, t_tr4c(q4, t, j))
                                    ps = psT4c0 if j % 2 == 0 else psT4c1
                                    c_v.inc(vector.tensor_copy(
                                        out=hq_fm[:, 4 * j:4 * (j + 1), :]
                                        .rearrange("p a b -> p (a b)"), in_=ps[:]))
                                sxt = sxt40 if t % 2 == 0 else sxt41
                                xsem = m_xa if t % 2 == 0 else m_xb
                                x2st = x2st0 if t % 2 == 0 else x2st1
                                xqsem = m_qs[t % 2]
                                vector.wait_ge(_c_s, s_dn4(q4, t))
                                xsem.wait(vector, sxt4_val(q4, t))
                                if 8 * q4 + t >= 2:
                                    pj = 8 * q4 + t - 2
                                    xqsem.wait(vector, q0__[t % 2]
                                               + 16 * (4 * (pj // 8) + (pj % 8) // 2 + 1))
                                cs = slice(512 * q4, 512 * (q4 + 1))
                                vector.tensor_tensor(out=s1h[:], in0=sxt[:, 512:1024],
                                                     in1=kb0[:, cs], op=ALU.mult)
                                if t >= 4:
                                    vector.tensor_tensor(out=tq42[:, 0:512],
                                                         in0=sxt[:, 1024:1536],
                                                         in1=kb1[:, cs], op=ALU.mult)
                                    vector.tensor_tensor(out=s1h[:], in0=s1h[:],
                                                         in1=tq42[:, 0:512], op=ALU.add)
                                vector.tensor_tensor(out=x2st[:], in0=s1h[:], in1=g1h[:],
                                                     op=ALU.mult)
                                c_v.inc(vector.tensor_tensor(out=x2st[:], in0=x2st[:],
                                                             in1=sxt[:, 0:512], op=ALU.add))

                    @blk.tensor
                    def _(tensor):
                        for q4 in range(4):
                            wsem = m_w2a if q4 % 2 == 0 else m_w2b
                            wsem.wait(tensor, wdh2_vals[q4])
                            for t in range(NT):
                                prt = hrt0 if t % 2 == 0 else hrt1
                                psem = m_spa if t % 2 == 0 else m_spb
                                PQ = hq_tm if q4 == 0 else prt
                                if q4 == 0:
                                    tensor.wait_ge(_c_v, v_q4(t))
                                else:
                                    psem.wait(tensor, prt1_val(q4, t))
                                for j in range(12):
                                    gj = 12 * (8 * q4 + t) + j
                                    if gj >= 2:
                                        pj = gj - 2
                                        qt_, jj = divmod(pj, 12)
                                        tensor.wait_ge(_c_v, v_ev4c(qt_ // 8, qt_ % 8, jj))
                                    ps = psT4c0 if j % 2 == 0 else psT4c1
                                    for i2 in range(4):
                                        oc = 4 * j + i2
                                        inst = tensor.transpose(
                                            ps[:, 128 * i2:128 * (i2 + 1)],
                                            in_=PQ[:, 128 * oc:128 * (oc + 1)],
                                            identity=c_idb[:])
                                        if i2 == 3:
                                            c_t.inc(inst)
                                tensor.wait_ge(_c_v, v_ev4c(q4, t, 11))
                                if 8 * q4 + t >= 2:
                                    pj = 8 * q4 + t - 2
                                    tensor.wait_ge(_c_s, s_dn4(pj // 8, pj % 8))
                                psD = psD40 if (8 * q4 + t) % 2 == 0 else psD41
                                for oc in range(48):
                                    inst = tensor.matmul(psD[:], lhsT=hq_fm[:, oc, :],
                                                         rhs=wdh2[:, oc // 6, oc % 6, :],
                                                         start=(oc == 0), stop=(oc == 47))
                                    if oc == 47:
                                        c_t.inc(inst)

                    @blk.scalar
                    def _(scalar):
                        scalar = SafeEng(scalar, h_s, sstate)
                        for q4 in range(4):
                            for t in range(NT):
                                if q4 == 0:
                                    pqsem = m_oa if t % 2 == 0 else m_ob
                                    scalar.wait_ge(_c_v, v_q4(t))
                                    pqsem.dma(scalar.dma_start(
                                        out=hqd[128 * t:128 * (t + 1), :], in_=hq_tm[:]))
                                scalar.wait_ge(_c_t, t_dn4(q4, t))
                                if 8 * q4 + t >= 1:
                                    pj = 8 * q4 + t - 1
                                    scalar.wait_ge(_c_v, v_gat4(pj // 8, pj % 8))
                                psD = psD40 if (8 * q4 + t) % 2 == 0 else psD41
                                c_s.inc(scalar.activation(out=g1h[:], in_=psD[:],
                                                          func=AF.Silu,
                                                          scale=spd1[:, t:t + 1]))
                                if d_g1 is not None:
                                    scalar.wait_ge(_c_s, c_s.v)
                                    m_dbg.dma(scalar.dma_start(
                                        out=d_g1[128 * t:128 * (t + 1),
                                                 512 * q4:512 * (q4 + 1)],
                                        in_=g1h[:]))
                                    scalar.wait_ge(_m_dbg, m_dbg.v)
                                x2st = x2st0 if t % 2 == 0 else x2st1
                                xqsem = m_qs[t % 2]
                                scalar.wait_ge(_c_v, v_gat4(q4, t))
                                xqsem.dma(scalar.dma_start(
                                    out=out[128 * t:128 * (t + 1), 512 * q4:512 * (q4 + 1)],
                                    in_=x2st[:]))

        @blk.sync
        def _(sync):
            sync.wait_ge(_m_out, m_out.v)
            sync.wait_ge(_m_gp, m_gp.v)
            sync.wait_ge(_m_sc, m_sc.v)
            sync.wait_ge(_m_cc, m_cc.v)
            sync.wait_ge(_m_dbg, m_dbg.v)
            for s in m_qs:
                sync.wait_ge(s.sem, s.v)
            sync.wait_ge(_m_oa, m_oa.v)
            sync.wait_ge(_m_ob, m_ob.v)
            sync.wait_ge(_m_kv, m_kv.v)
            sync.wait_ge(_m_zc, m_zc.v)
            sync.wait_ge(_m_pwa, m_pwa.v)
            sync.wait_ge(_m_pwb, m_pwb.v)
            sync.wait_ge(_m_xa, m_xa.v)
            sync.wait_ge(_m_xb, m_xb.v)
            sync.wait_ge(_m_wa, m_wa.v)
            sync.wait_ge(_m_wb, m_wb.v)
    return nc, dbg


def make_inmaps(inputs):
    g = {k: np.asarray(v) for k, v in inputs.items()}
    x = np.ascontiguousarray(g["L0_attn_x"], dtype=np.float32)
    WQ = np.asarray(g["L0_attn_W_Q"], np.float32)
    WK = np.asarray(g["L0_attn_W_K"], np.float32)
    WV = np.asarray(g["L0_attn_W_V"], np.float32)
    WO = np.asarray(g["L0_attn_W_O"], np.float32)
    wgT = np.asarray(g["L0_ffn_w_gate"], np.float32).T
    wuT = np.asarray(g["L0_ffn_w_up"], np.float32).T
    wdT = np.asarray(g["L0_ffn_w_down"], np.float32).T
    w1T = np.asarray(g["L1_ffn_w1"], np.float32).T
    w3T = np.asarray(g["L1_ffn_w3"], np.float32).T
    w2T = np.asarray(g["L1_ffn_w2"], np.float32).T
    w2Tp = np.zeros((I1P, D), np.float32)
    w2Tp[:I1] = w2T
    wo_t = np.ascontiguousarray(WO.transpose(0, 2, 1))

    trim = np.triu(np.full((128, 128), -1e9, np.float32), k=1)
    idm = np.eye(128, dtype=np.float32)
    onescv = np.ones((128, 1), np.float32)
    onesrv = np.ones((1, 128), np.float32)

    def chunked_vec(v):
        return np.ascontiguousarray(np.asarray(v, np.float32).reshape(DCK, 128).T)

    def chunked_mat(m):
        return np.ascontiguousarray(
            np.asarray(m, np.float32).reshape(DCK, 128, 8).transpose(1, 0, 2).reshape(128, DCK * 8))

    ldt = chunked_vec(g["L1_attn_log_dt"])
    dpar = chunked_vec(g["L1_attn_D_param"])
    arl = chunked_mat(g["L1_attn_A_real_log"])
    aim = chunked_mat(g["L1_attn_A_imag"])
    cre = chunked_mat(g["L1_attn_C_re"])
    cim = chunked_mat(g["L1_attn_C_im"])

    in_maps = []
    for c in range(8):
        b = c // 4
        h0 = 4 * (c % 4)
        m = {
            "xa": np.ascontiguousarray(x[b]),
            "xo": np.ascontiguousarray(x[:, SEQ8 * c:SEQ8 * (c + 1), :]).reshape(TOK, D),
            "wq_t": np.ascontiguousarray(WQ[h0:h0 + 4].transpose(0, 2, 1)),
            "wk_t": np.ascontiguousarray(WK[h0:h0 + 4].transpose(0, 2, 1)),
            "wv_t": np.ascontiguousarray(WV[h0:h0 + 4].transpose(0, 2, 1)),
            "wo_t": wo_t,
            "wg_s": np.ascontiguousarray(wgT[256 * c:256 * (c + 1)]),
            "wu_s": np.ascontiguousarray(wuT[256 * c:256 * (c + 1)]),
            "wd_s": np.ascontiguousarray(wdT[1024 * c:1024 * (c + 1)]),
            "w1_s": np.ascontiguousarray(w1T[256 * c:256 * (c + 1)]),
            "w3_s": np.ascontiguousarray(w3T[256 * c:256 * (c + 1)]),
            "w2_s": np.ascontiguousarray(w2Tp[768 * c:768 * (c + 1)]),
            "tri": trim, "ident": idm, "onesc": onescv, "onesr": onesrv,
            "ldt_in": ldt, "dpar_in": dpar, "arl_in": arl, "aim_in": aim,
            "cre_in": cre, "cim_in": cim,
            "zsel_in": (np.arange(128, dtype=np.int32)[None, :] + 128 * c + 1024 * np.arange(8, dtype=np.int32)[:, None]).reshape(1024, 1).astype(np.int32).copy(),
            "jnk_in": np.zeros((128, 16), np.float32),
        }
        in_maps.append(m)
    return in_maps


def kernel(**inputs):
    nc, _ = build()
    in_maps = make_inmaps(inputs)
    res = run_bass_kernel_spmd(nc, in_maps, core_ids=list(range(8)))
    y = np.zeros((B, S, D), np.float32)
    for c in range(8):
        y[:, SEQ8 * c:SEQ8 * (c + 1), :] = res.results[c]["out"].reshape(B, SEQ8, D)
    return y


# version 03
# speedup vs baseline: 1.0203x; 1.0203x over previous
import contextlib
import numpy as np
import ml_dtypes
import concourse.bass as bass
import concourse.mybir as mybir
from concourse.bass_utils import run_bass_kernel_spmd

F32 = mybir.dt.float32
BF16 = mybir.dt.bfloat16
AF = mybir.ActivationFunctionType
ALU = mybir.AluOpType
AX = mybir.AxisListType.X

B, S, D, NH, E = 2, 4096, 2048, 16, 128
SEQ8 = S // 8
TOK = 2 * SEQ8
NT = TOK // 128
DCK = D // 128
NG = S // 512
I0, I1, I1P = 8192, 5632, 6144
EPS = 1e-5
LNEPS = 1e-5
MAGIC = float(1.5 * 2 ** 23)
SCALE_ATT = 1.0 / float(np.sqrt(128.0))
TWO_PI = float(2.0 * np.pi)
HALF_PI = float(0.5 * np.pi)
INV_2PI = float(1.0 / (2.0 * np.pi))

WSLICES = [
    ("wg", D // 8, I0, D * I0),
    ("wu", D // 8, I0, D * I0),
    ("wd", I0 // 8, D, I0 * D),
    ("w1", D // 8, I1, D * I1),
    ("w3", D // 8, I1, D * I1),
    ("w2", I1P // 8, D, I1 * D),
]
WQ_RANK = sum(r * c for _, r, c, _ in WSLICES)
WOFF = {}
_o = 0
for _nm, _r, _c, _ in WSLICES:
    WOFF[_nm] = _o
    _o += _r * _c
WCOL = {nm: i for i, (nm, *_) in enumerate(WSLICES)}


EDGE_REG = {}


def _inst_key(inst):
    try:
        return inst.ins.name
    except AttributeError:
        return id(inst)


class Ctr:
    def __init__(self, sem):
        self.sem = sem
        self.v = 0

    def inc(self, inst, n=1):
        self.v += n
        inst.then_inc(self.sem, n)
        EDGE_REG[_inst_key(inst)] = (self.sem, self.v)
        return inst

    def dma(self, inst):
        return self.inc(inst, 16)

    def wait(self, eng, v=None):
        eng.wait_ge(self.sem, self.v if v is None else v)


def cdiv(a, b):
    return (a + b - 1) // b


def _ap_tname(a):
    try:
        return a.tensor.name
    except AttributeError:
        return None


class SafeEng:

    def __init__(self, eng, hctr, state):
        self.e = eng
        self.h = hctr
        self.st = state
        self.w = state["w"]
        self.r = state["r"]
        self.lv = state["lv"]

    def _resolve_pending(self):
        inst = self.st.get("pend")
        if inst is not None:
            k = _inst_key(inst)
            if k not in EDGE_REG:
                self.h.inc(inst)
                EDGE_REG[k] = (self.h.sem, self.h.v)
            self.st["pend"] = None

    def wait_ge(self, sem, v):
        self.e.wait_ge(sem, v)

    def nop(self, **kw):
        return self.e.nop(**kw)

    def dma_start(self, **kw):
        return self.e.dma_start(**kw)

    def _edge_of(self, rec):
        if rec[1] is None:
            rec[1] = EDGE_REG[_inst_key(rec[0])]
        return rec[1]

    def _need(self, needs, rec):
        sem, val = self._edge_of(rec)
        k = id(sem)
        if self.lv.get(k, 0) < val:
            needs[k] = (sem, max(val, needs[k][1] if k in needs else 0))

    def _apply(self, needs):
        for k, (sem, val) in needs.items():
            self.e.wait_ge(sem, val)
            self.lv[k] = val

    def flush(self):
        self._resolve_pending()
        needs = {}
        for recs in self.w.values():
            for rec in recs:
                self._need(needs, rec)
        self._apply(needs)

    def _pre(self, reads, writes):
        self._resolve_pending()
        rn = {n for n in (_ap_tname(r) for r in reads) if n}
        wn = {n for n in (_ap_tname(w) for w in writes) if n}
        needs = {}
        for n in rn | wn:
            for rec in self.w.get(n, ()):
                self._need(needs, rec)
        for n in wn:
            for rec in self.r.get(n, ()):
                self._need(needs, rec)
        self._apply(needs)
        return rn, wn

    def _post(self, inst, rn, wn):
        self.st["pend"] = inst
        for n in wn:
            self.w[n] = [[inst, None]]
            self.r[n] = []
        for n in rn:
            if n not in wn:
                self.r.setdefault(n, []).append([inst, None])
        return inst

    def memset(self, ap, const):
        rn, wn = self._pre([], [ap])
        return self._post(self.e.memset(ap, const), rn, wn)

    def memzero(self, ap):
        rn, wn = self._pre([], [ap])
        return self._post(self.e.memzero(ap), rn, wn)

    def tensor_copy(self, out, in_):
        rn, wn = self._pre([in_], [out])
        return self._post(self.e.tensor_copy(out=out, in_=in_), rn, wn)

    def tensor_tensor(self, out, in0, in1, op):
        rn, wn = self._pre([in0, in1], [out])
        return self._post(self.e.tensor_tensor(out=out, in0=in0, in1=in1, op=op), rn, wn)

    def tensor_scalar(self, out, in0, scalar1, scalar2, op0, op1=None, accum_out=None):
        rn, wn = self._pre([in0, scalar1, scalar2], [out, accum_out])
        kw = {}
        if op1 is not None:
            kw["op1"] = op1
        if accum_out is not None:
            kw["accum_out"] = accum_out
        return self._post(self.e.tensor_scalar(out=out, in0=in0, scalar1=scalar1,
                                               scalar2=scalar2, op0=op0, **kw), rn, wn)

    def scalar_tensor_tensor(self, out, in0, scalar, in1, op0, op1, accum_out=None):
        rn, wn = self._pre([in0, scalar, in1], [out, accum_out])
        kw = {"accum_out": accum_out} if accum_out is not None else {}
        return self._post(self.e.scalar_tensor_tensor(out=out, in0=in0, scalar=scalar,
                                                      in1=in1, op0=op0, op1=op1, **kw),
                          rn, wn)

    def reduce_sum(self, out, in_, axis, apply_absolute_value=None):
        rn, wn = self._pre([in_], [out])
        kw = {}
        if apply_absolute_value is not None:
            kw["apply_absolute_value"] = apply_absolute_value
        return self._post(self.e.reduce_sum(out=out, in_=in_, axis=axis, **kw), rn, wn)

    def reduce_max(self, out, in_, axis, apply_absolute_value=None):
        rn, wn = self._pre([in_], [out])
        kw = {}
        if apply_absolute_value is not None:
            kw["apply_absolute_value"] = apply_absolute_value
        return self._post(self.e.reduce_max(out=out, in_=in_, axis=axis, **kw), rn, wn)

    def tensor_reduce(self, out, in_, axis, op):
        rn, wn = self._pre([in_], [out])
        return self._post(self.e.tensor_reduce(out=out, in_=in_, axis=axis, op=op), rn, wn)

    def reciprocal(self, out, in_):
        rn, wn = self._pre([in_], [out])
        return self._post(self.e.reciprocal(out=out, in_=in_), rn, wn)

    def activation(self, out, in_, func, bias=None, scale=None, accum_out=None):
        rn, wn = self._pre([in_, bias, scale], [out, accum_out])
        kw = {}
        if bias is not None:
            kw["bias"] = bias
        if scale is not None:
            kw["scale"] = scale
        if accum_out is not None:
            kw["accum_out"] = accum_out
        return self._post(self.e.activation(out=out, in_=in_, func=func, **kw), rn, wn)

    def copy(self, out, in_):
        rn, wn = self._pre([in_], [out])
        return self._post(self.e.copy(out=out, in_=in_), rn, wn)

    def add(self, out, in_, add):
        rn, wn = self._pre([in_, add], [out])
        return self._post(self.e.add(out=out, in_=in_, add=add), rn, wn)

    def mul(self, out, in_, mul):
        rn, wn = self._pre([in_, mul], [out])
        return self._post(self.e.mul(out=out, in_=in_, mul=mul), rn, wn)


def build(debug=(), upto='all', skip_cc=()):
    EDGE_REG.clear()
    nc = bass.Bass("TRN2", target_bir_lowering=False)

    xa = nc.dram_tensor("xa", [S, D], F32, kind="ExternalInput").ap()
    xo = nc.dram_tensor("xo", [TOK, D], F32, kind="ExternalInput").ap()
    wq_t = nc.dram_tensor("wq_t", [4, D, E], F32, kind="ExternalInput").ap()
    wk_t = nc.dram_tensor("wk_t", [4, D, E], F32, kind="ExternalInput").ap()
    wv_t = nc.dram_tensor("wv_t", [4, D, E], F32, kind="ExternalInput").ap()
    wo_t = nc.dram_tensor("wo_t", [NH, E, D], F32, kind="ExternalInput").ap()
    wsl = {nm: nc.dram_tensor(nm + "_s", [r, c], F32, kind="ExternalInput").ap()
           for nm, r, c, _ in WSLICES}
    tri = nc.dram_tensor("tri", [128, 128], F32, kind="ExternalInput").ap()
    ident = nc.dram_tensor("ident", [128, 128], F32, kind="ExternalInput").ap()
    onesc = nc.dram_tensor("onesc", [128, 1], F32, kind="ExternalInput").ap()
    onesr = nc.dram_tensor("onesr", [1, 128], F32, kind="ExternalInput").ap()
    ldt_in = nc.dram_tensor("ldt_in", [128, DCK], F32, kind="ExternalInput").ap()
    dpar_in = nc.dram_tensor("dpar_in", [128, DCK], F32, kind="ExternalInput").ap()
    arl_in = nc.dram_tensor("arl_in", [128, DCK * 8], F32, kind="ExternalInput").ap()
    aim_in = nc.dram_tensor("aim_in", [128, DCK * 8], F32, kind="ExternalInput").ap()
    cre_in = nc.dram_tensor("cre_in", [128, DCK * 8], F32, kind="ExternalInput").ap()
    cim_in = nc.dram_tensor("cim_in", [128, DCK * 8], F32, kind="ExternalInput").ap()
    zsel_in = nc.dram_tensor("zsel_in", [1024, 1], mybir.dt.int32, kind="ExternalInput").ap()
    jnk_in = nc.dram_tensor("jnk_in", [128, 16], F32, kind="ExternalInput").ap()

    out = nc.dram_tensor("out", [TOK, D], F32, kind="ExternalOutput").ap()

    ar_in = nc.dram_tensor("ar_in", [1, 8], F32, kind="Internal").ap()
    ar_out = nc.dram_tensor("ar_out", [1, 8], F32, kind="Internal").ap()
    wqag_in = nc.dram_tensor("wqag_in", [WQ_RANK], BF16, kind="Internal").ap()
    wqag_out = nc.dram_tensor("wqag_out", [8, WQ_RANK], BF16, kind="Internal").ap()
    qd = nc.dram_tensor("qd", [4, E, S], BF16, kind="Internal").ap()
    kd = nc.dram_tensor("kd", [4, E, S], BF16, kind="Internal").ap()
    vd = nc.dram_tensor("vd", [4, E, S], BF16, kind="Internal").ap()
    zag_in = nc.dram_tensor("zag_in", [NG, E, 4, 512], BF16, kind="Internal").ap()
    zag_out = nc.dram_tensor("zag_out", [8, NG, E, 4, 512], BF16, kind="Internal").ap()
    sd = nc.dram_tensor("sd", [TOK, D], F32, kind="Internal").ap()
    prodd = nc.dram_tensor("prodd", [TOK, I0], BF16, kind="Internal").ap()
    pqd = nc.dram_tensor("pqd", [TOK, I0], BF16, kind="Internal").ap()
    hdd = nc.dram_tensor("hdd", [TOK, I1], BF16, kind="Internal").ap()
    hqd = nc.dram_tensor("hqd", [TOK, I1P], BF16, kind="Internal").ap()
    x1d = nc.dram_tensor("x1d", [TOK, D], F32, kind="Internal").ap()
    xn1d = nc.dram_tensor("xn1d", [TOK, D], F32, kind="Internal").ap()
    ktmp_d = nc.dram_tensor("ktmp_d", [32, 128], F32, kind="Internal").ap()

    dbg = {}

    def dbg_ap(name, shape, dtype=F32):
        if name in debug:
            dbg[name] = nc.dram_tensor("dbg_" + name, shape, dtype, kind="ExternalOutput").ap()
            return dbg[name]
        return None

    d_wsc = dbg_ap("wsc", [1, 8])
    d_kvec = dbg_ap("kvec", [1, 2 * D])
    d_qkv = dbg_ap("qkv", [3, 4, E, S], BF16)
    d_xna = dbg_ap("xna", [S, D], BF16)
    d_aw = dbg_ap("aw", [128, 4 * DCK, E], BF16)
    d_xnf = dbg_ap("xnf", [128, DCK, 512], BF16)
    d_z = dbg_ap("z", [8, NG, E, 4, 512], BF16)
    d_zb = dbg_ap("zb", [128, 32, 128], BF16)
    d_f0 = dbg_ap("f0", [128, 320], F32)
    d_f0p = dbg_ap("f0", [128, 128], BF16) if False else None
    d_f0b = dbg_ap("f0b", [128, 128], BF16)
    d_qt = dbg_ap("qt", [2, 128, S], BF16)
    d_qt0 = dbg_ap("qt0", [2, 128, S], BF16)
    d_pb = dbg_ap("pb", [128, S], BF16)
    d_ssb = dbg_ap("ssb", [128, S], F32)
    d_s = dbg_ap("s", [TOK, D])
    d_xn = dbg_ap("xn", [TOK, D])
    d_g0 = dbg_ap("g0", [TOK, D])
    d_x1 = dbg_ap("x1", [TOK, D])
    d_xn1 = dbg_ap("xn1", [TOK, D])
    d_g1 = dbg_ap("g1", [TOK, D])

    CSUB = 2048
    wchunks = []
    for nm, r, c_, _ in WSLICES:
        for r0 in range(0, r, 128):
            nr = min(128, r - r0)
            for c0 in range(0, c_, CSUB):
                wchunks.append((nm, r0, nr, c0, min(CSUB, c_ - c0)))
    NWCH = len(wchunks)

    with contextlib.ExitStack() as es:
        c_tri = es.enter_context(nc.sbuf_tensor("c_tri", [128, 128], F32))
        c_idb = es.enter_context(nc.sbuf_tensor("c_idb", [128, 128], BF16))
        c_idf = es.enter_context(nc.sbuf_tensor("c_idf", [128, 128], F32))
        c_onesr = es.enter_context(nc.sbuf_tensor("c_onesr", [1, 128], F32))
        c_onesc = es.enter_context(nc.sbuf_tensor("c_onesc", [128, 1], F32))
        acc6 = es.enter_context(nc.sbuf_tensor("acc6", [128, 8], F32))
        wsc = es.enter_context(nc.sbuf_tensor("wsc", [1, 8], F32))
        sc8a = es.enter_context(nc.sbuf_tensor("sc8a", [1, 8], F32))
        sc8b = es.enter_context(nc.sbuf_tensor("sc8b", [1, 8], F32))
        q_inv = es.enter_context(nc.sbuf_tensor("q_inv", [128, 8], F32))
        q_deq = es.enter_context(nc.sbuf_tensor("q_deq", [128, 8], F32))
        kb0 = es.enter_context(nc.sbuf_tensor("kb0", [128, D], F32))
        kb1 = es.enter_context(nc.sbuf_tensor("kb1", [128, D], F32))
        st_a = es.enter_context(nc.sbuf_tensor("st_a", [128, 64], F32))
        st_b = es.enter_context(nc.sbuf_tensor("st_b", [128, 64], F32))
        st_c = es.enter_context(nc.sbuf_tensor("st_c", [128, 64], F32))
        st_d = es.enter_context(nc.sbuf_tensor("st_d", [128, 64], F32))
        st_e = es.enter_context(nc.sbuf_tensor("st_e", [128, 64], F32))
        st_f = es.enter_context(nc.sbuf_tensor("st_f", [128, 64], F32))
        vjk = es.enter_context(nc.sbuf_tensor("vjk", [128, 8], F32))
        sjk = es.enter_context(nc.sbuf_tensor("sjk", [128, 8], F32))
        zsel_t = [es.enter_context(nc.sbuf_tensor(f"zsel{r}", [128, 1], mybir.dt.int32))
                  for r in range(8)]
        _m_sy = es.enter_context(nc.semaphore("m_sy"))
        _m_sc = es.enter_context(nc.semaphore("m_sc"))
        _m_gp = es.enter_context(nc.semaphore("m_gp"))
        _m_xa = es.enter_context(nc.semaphore("m_xa"))
        _m_xb = es.enter_context(nc.semaphore("m_xb"))
        _m_wa = es.enter_context(nc.semaphore("m_wa"))
        _m_wb = es.enter_context(nc.semaphore("m_wb"))
        _m_kv = es.enter_context(nc.semaphore("m_kv"))
        _m_w2a = es.enter_context(nc.semaphore("m_w2a"))
        _m_w2b = es.enter_context(nc.semaphore("m_w2b"))
        _m_spa = es.enter_context(nc.semaphore("m_spa"))
        _m_spb = es.enter_context(nc.semaphore("m_spb"))
        _m_out = es.enter_context(nc.semaphore("m_out"))
        _m_cc = es.enter_context(nc.semaphore("m_cc"))
        _m_dbg = es.enter_context(nc.semaphore("m_dbg"))
        _m_q0 = es.enter_context(nc.semaphore("m_q0"))
        _m_q1 = es.enter_context(nc.semaphore("m_q1"))
        _m_q2 = es.enter_context(nc.semaphore("m_q2"))
        _m_q3 = es.enter_context(nc.semaphore("m_q3"))
        _m_oa = es.enter_context(nc.semaphore("m_oa"))
        _m_ob = es.enter_context(nc.semaphore("m_ob"))
        _c_v = es.enter_context(nc.semaphore("c_v"))
        _c_s = es.enter_context(nc.semaphore("c_s"))
        _c_t = es.enter_context(nc.semaphore("c_t"))
        _c_g = es.enter_context(nc.semaphore("c_g"))
        _h_v = es.enter_context(nc.semaphore("h_v"))
        _m_zc = es.enter_context(nc.semaphore("m_zc"))
        _m_pwa = es.enter_context(nc.semaphore("m_pwa"))
        _m_pwb = es.enter_context(nc.semaphore("m_pwb"))
        _h_s = es.enter_context(nc.semaphore("h_s"))
        blk = es.enter_context(nc.Block())
        m_sy, m_sc, m_gp = Ctr(_m_sy), Ctr(_m_sc), Ctr(_m_gp)
        m_xa, m_xb = Ctr(_m_xa), Ctr(_m_xb)
        m_wa, m_wb = Ctr(_m_wa), Ctr(_m_wb)
        m_kv = Ctr(_m_kv)
        m_w2a, m_w2b = Ctr(_m_w2a), Ctr(_m_w2b)
        m_spa, m_spb = Ctr(_m_spa), Ctr(_m_spb)
        m_out, m_cc = Ctr(_m_out), Ctr(_m_cc)
        m_dbg = Ctr(_m_dbg)
        m_qs = [Ctr(_m_q0), Ctr(_m_q1), Ctr(_m_q2), Ctr(_m_q3)]
        h_v, h_s = Ctr(_h_v), Ctr(_h_s)
        m_zc = Ctr(_m_zc)
        m_pwa, m_pwb = Ctr(_m_pwa), Ctr(_m_pwb)
        vstate = {"w": {}, "r": {}, "lv": {}, "pend": None}
        sstate = {"w": {}, "r": {}, "lv": {}, "pend": None}

        def phase_join(eng):
            for ctr in (c_t, c_v, c_s, h_v, h_s, m_sy, m_sc, m_gp, m_xa, m_xb,
                        m_wa, m_wb, m_kv, m_zc, m_w2a, m_w2b, m_spa, m_spb,
                        m_oa, m_ob, m_pwa, m_pwb, m_dbg, *m_qs):
                if ctr.v:
                    eng.wait_ge(ctr.sem, ctr.v)
        m_oa, m_ob = Ctr(_m_oa), Ctr(_m_ob)
        c_v, c_s, c_t, c_g = Ctr(_c_v), Ctr(_c_s), Ctr(_c_t), Ctr(_c_g)

        with contextlib.ExitStack() as es01:
            awq = es01.enter_context(nc.sbuf_tensor("awq", [128, 4 * DCK, E], BF16))
            awk = es01.enter_context(nc.sbuf_tensor("awk", [128, 4 * DCK, E], BF16))
            awv = es01.enter_context(nc.sbuf_tensor("awv", [128, 4 * DCK, E], BF16))
            with contextlib.ExitStack() as es0:
                wsb0 = es0.enter_context(nc.sbuf_tensor("wsb0", [128, CSUB], F32))
                wsb1 = es0.enter_context(nc.sbuf_tensor("wsb1", [128, CSUB], F32))
                wqb0 = es0.enter_context(nc.sbuf_tensor("wqb0", [128, CSUB], BF16))
                wqb1 = es0.enter_context(nc.sbuf_tensor("wqb1", [128, CSUB], BF16))
                s4 = es0.enter_context(nc.sbuf_tensor("s4", [128, 16 * DCK * 8], F32))
                ktr = es0.enter_context(nc.sbuf_tensor("ktr", [32, 128], F32))
                kvec = es0.enter_context(nc.sbuf_tensor("kvec", [1, 2 * D], F32))
                p0a = es0.enter_context(nc.psum_tensor("p0a", [128, 512], F32))
                p0b = es0.enter_context(nc.psum_tensor("p0b", [128, 512], F32))
                NP = DCK * 8

                def pl(i):
                    return s4[:, i * NP:(i + 1) * NP]

                wbufs = [(wsb0, m_wa), (wsb1, m_wb)]
                wqbufs = [wqb0, wqb1]

                vA0 = c_v.v
                passA_vals = []

                @blk.sync
                def _(sync):
                    m_sy.dma(sync.dma_start(out=vjk[:], in_=jnk_in[:, 0:8]))
                    m_sy.dma(sync.dma_start(out=sjk[:], in_=jnk_in[:, 8:16]))
                    m_sy.dma(sync.dma_start(out=c_tri[:], in_=tri))
                    m_sy.dma(sync.dma_start(out=c_idf[:], in_=ident))
                    m_sy.dma(sync.dma_start(out=c_onesr[:], in_=onesr))
                    m_sy.dma(sync.dma_start(out=c_onesc[:], in_=onesc))
                    for i, (nm, r0, nr, c0, ncl) in enumerate(wchunks):
                        buf, sem = wbufs[i % 2]
                        if i >= 2:
                            sync.wait_ge(_c_v, vA0 + 1 + (i - 2) + 1)
                        sem.dma(sync.dma_start(out=buf[0:nr, 0:ncl],
                                               in_=wsl[nm][r0:r0 + nr, c0:c0 + ncl]))
                        passA_vals.append(sem.v)

                @blk.gpsimd
                def _(gpsimd):
                    for r in range(8):
                        m_gp.dma(gpsimd.dma_start(out=zsel_t[r][:],
                                                  in_=zsel_in[128 * r:128 * (r + 1), :]))
                    m_gp.dma(gpsimd.dma_start(out=c_idb[:], in_=ident))
                    m_gp.dma(gpsimd.dma_start(
                        out=awq[:], in_=wq_t.rearrange("u (ck p) e -> p (u ck) e", p=128)))
                    m_gp.dma(gpsimd.dma_start(
                        out=awk[:], in_=wk_t.rearrange("u (ck p) e -> p (u ck) e", p=128)))
                    m_gp.dma(gpsimd.dma_start(
                        out=awv[:], in_=wv_t.rearrange("u (ck p) e -> p (u ck) e", p=128)))
                    m_gp.dma(gpsimd.dma_start(out=pl(0)[:, 0:DCK], in_=ldt_in))
                    m_gp.dma(gpsimd.dma_start(out=pl(0)[:, DCK:2 * DCK], in_=dpar_in))
                    m_gp.dma(gpsimd.dma_start(out=pl(1), in_=arl_in))
                    m_gp.dma(gpsimd.dma_start(out=pl(2), in_=aim_in))
                    m_gp.dma(gpsimd.dma_start(out=pl(3), in_=cre_in))
                    m_gp.dma(gpsimd.dma_start(out=pl(4), in_=cim_in))
                gp_loaded = m_gp.v

                @blk.vector
                def _(vector):
                    vector = SafeEng(vector, h_v, vstate)
                    vector.wait_ge(_m_sy, 96)
                    for i, (nm, r0, nr, c0, ncl) in enumerate(wchunks):
                        buf, sem = wbufs[i % 2]
                        sem.wait(vector, passA_vals[i])
                        if i == 0:
                            c_v.inc(vector.tensor_scalar(out=acc6[:], in0=vjk[:, 0:8],
                                                         scalar1=0.0, scalar2=None,
                                                         op0=ALU.mult))
                        col = WCOL[nm]
                        vector.reduce_sum(out=st_a[0:nr, 0:1], in_=buf[0:nr, 0:ncl],
                                          axis=AX, apply_absolute_value=True)
                        c_v.inc(vector.tensor_tensor(out=acc6[0:nr, col:col + 1],
                                                     in0=acc6[0:nr, col:col + 1],
                                                     in1=st_a[0:nr, 0:1], op=ALU.add))
                vA_done = c_v.v

                @blk.tensor
                def _(tensor):
                    c_v.wait(tensor, vA_done)
                    m_sy.wait(tensor, 96)
                    c_t.inc(tensor.matmul(p0a[0:1, 0:8], lhsT=c_onesc[:], rhs=acc6[:, 0:8],
                                          start=True, stop=True))
                t_abs = c_t.v

                @blk.scalar
                def _(scalar):
                    scalar = SafeEng(scalar, h_s, sstate)
                    c_t.wait(scalar, t_abs)
                    c_s.inc(scalar.copy(out=sc8a[0:1, 0:8], in_=p0a[0:1, 0:8]))
                    scalar.wait_ge(_c_s, c_s.v)
                    m_sc.dma(scalar.dma_start(out=ar_in, in_=sc8a[0:1, 0:8]))
                sc_ar = m_sc.v

                @blk.gpsimd
                def _(gpsimd):
                    m_sc.wait(gpsimd, sc_ar)
                    m_cc.inc(gpsimd.collective_compute(
                        "AllReduce", ALU.add, ins=[ar_in], outs=[ar_out],
                        replica_groups=[[0, 1, 2, 3, 4, 5, 6, 7]]))
                    gpsimd.wait_ge(_m_cc, m_cc.v)
                    m_gp.dma(gpsimd.dma_start(out=wsc[:], in_=ar_out))
                    if d_wsc is not None:
                        m_gp.dma(gpsimd.dma_start(out=d_wsc, in_=ar_out))
                gp_wsc = m_gp.v

                @blk.vector
                def _(vector):
                    vector = SafeEng(vector, h_v, vstate)
                    m_gp.wait(vector, gp_wsc)
                    for j, (_, _, _, cnt) in enumerate(WSLICES):
                        vector.tensor_scalar(out=sc8a[0:1, j:j + 1], in0=wsc[0:1, j:j + 1],
                                             scalar1=1.0 / float(cnt), scalar2=None, op0=ALU.mult)
                    for j in (0, 1, 2):
                        vector.tensor_scalar(out=sc8a[0:1, j:j + 1], in0=sc8a[0:1, j:j + 1],
                                             scalar1=EPS, scalar2=None, op0=ALU.add)
                    for j in (3, 4, 5):
                        vector.tensor_scalar(out=sc8a[0:1, j:j + 1], in0=sc8a[0:1, j:j + 1],
                                             scalar1=EPS, scalar2=None, op0=ALU.max)
                    vector.tensor_scalar(out=sc8a[0:1, 6:8], in0=wsc[0:1, 6:8],
                                         scalar1=0.0, scalar2=1.0, op0=ALU.mult, op1=ALU.add)
                    c_v.inc(vector.reciprocal(out=sc8b[0:1, 0:8], in_=sc8a[0:1, 0:8]))
                v_scales = c_v.v

                @blk.tensor
                def _(tensor):
                    c_v.wait(tensor, v_scales)
                    tensor.matmul(p0b[:, 0:8], lhsT=c_onesr[:], rhs=sc8a[0:1, 0:8],
                                  start=True, stop=True)
                    c_t.inc(tensor.matmul(p0b[:, 8:16], lhsT=c_onesr[:], rhs=sc8b[0:1, 0:8],
                                          start=True, stop=True))
                t_bcast = c_t.v

                @blk.vector
                def _(vector):
                    vector = SafeEng(vector, h_v, vstate)
                    c_t.wait(vector, t_bcast)
                    vector.tensor_copy(out=q_deq[:], in_=p0b[:, 0:8])
                    c_v.inc(vector.tensor_copy(out=q_inv[:], in_=p0b[:, 8:16]))
                v_bcast = c_v.v

                @blk.scalar
                def _(scalar):
                    scalar = SafeEng(scalar, h_s, sstate)
                    m_gp.wait(scalar, gp_wsc)
                    scalar.activation(out=st_e[:, 0:DCK], in_=pl(0)[:, 0:DCK], func=AF.Exp)
                    c_s.inc(scalar.activation(out=pl(5), in_=pl(1), func=AF.Exp))
                s_s4a = c_s.v

                @blk.vector
                def _(vector):
                    vector = SafeEng(vector, h_v, vstate)
                    c_s.wait(vector, s_s4a)
                    for n in range(8):
                        vector.tensor_copy(out=pl(6)[:, n::8], in_=st_e[:, 0:DCK])
                    vector.tensor_scalar(out=pl(5), in0=pl(5), scalar1=-1.0, scalar2=None,
                                         op0=ALU.mult)
                    vector.tensor_tensor(out=pl(7), in0=pl(2), in1=pl(6), op=ALU.mult)
                    vector.tensor_tensor(out=pl(6), in0=pl(5), in1=pl(6), op=ALU.mult)
                    vector.tensor_scalar(out=pl(8), in0=pl(7), scalar1=INV_2PI, scalar2=MAGIC,
                                         op0=ALU.mult, op1=ALU.add)
                    vector.tensor_scalar(out=pl(8), in0=pl(8), scalar1=-MAGIC, scalar2=None,
                                         op0=ALU.add)
                    c_v.inc(vector.scalar_tensor_tensor(out=pl(8), in0=pl(8), scalar=-TWO_PI,
                                                        in1=pl(7), op0=ALU.mult, op1=ALU.add))
                v_s4a = c_v.v

                @blk.scalar
                def _(scalar):
                    scalar = SafeEng(scalar, h_s, sstate)
                    c_v.wait(scalar, v_s4a)
                    scalar.activation(out=pl(9), in_=pl(6), func=AF.Exp)
                    c_s.inc(scalar.activation(out=pl(10), in_=pl(8), func=AF.Sin))
                s_s4b = c_s.v

                @blk.vector
                def _(vector):
                    vector = SafeEng(vector, h_v, vstate)
                    c_s.wait(vector, s_s4b)
                    vector.tensor_scalar(out=pl(11), in0=pl(8), scalar1=HALF_PI, scalar2=None,
                                         op0=ALU.add)
                    vector.tensor_scalar(out=pl(12), in0=pl(11), scalar1=INV_2PI, scalar2=MAGIC,
                                         op0=ALU.mult, op1=ALU.add)
                    vector.tensor_scalar(out=pl(12), in0=pl(12), scalar1=-MAGIC, scalar2=None,
                                         op0=ALU.add)
                    c_v.inc(vector.scalar_tensor_tensor(out=pl(11), in0=pl(12), scalar=-TWO_PI,
                                                        in1=pl(11), op0=ALU.mult, op1=ALU.add))
                v_s4b = c_v.v

                @blk.scalar
                def _(scalar):
                    scalar = SafeEng(scalar, h_s, sstate)
                    c_v.wait(scalar, v_s4b)
                    c_s.inc(scalar.activation(out=pl(11), in_=pl(11), func=AF.Sin))
                s_s4c = c_s.v

                @blk.vector
                def _(vector):
                    vector = SafeEng(vector, h_v, vstate)
                    c_s.wait(vector, s_s4c)
                    vector.tensor_tensor(out=pl(12), in0=pl(9), in1=pl(11), op=ALU.mult)
                    vector.tensor_tensor(out=pl(9), in0=pl(9), in1=pl(10), op=ALU.mult)
                    vector.tensor_scalar(out=pl(10), in0=pl(12), scalar1=-1.0, scalar2=None,
                                         op0=ALU.add)
                    vector.tensor_tensor(out=pl(13), in0=pl(5), in1=pl(5), op=ALU.mult)
                    vector.tensor_tensor(out=pl(14), in0=pl(2), in1=pl(2), op=ALU.mult)
                    vector.tensor_tensor(out=pl(13), in0=pl(13), in1=pl(14), op=ALU.add)
                    vector.reciprocal(out=pl(13), in_=pl(13))
                    vector.tensor_tensor(out=pl(14), in0=pl(3), in1=pl(10), op=ALU.mult)
                    vector.tensor_tensor(out=pl(15), in0=pl(4), in1=pl(9), op=ALU.mult)
                    vector.tensor_tensor(out=pl(14), in0=pl(14), in1=pl(15), op=ALU.subtract)
                    vector.tensor_tensor(out=pl(15), in0=pl(3), in1=pl(9), op=ALU.mult)
                    vector.tensor_tensor(out=pl(10), in0=pl(4), in1=pl(10), op=ALU.mult)
                    vector.tensor_tensor(out=pl(15), in0=pl(15), in1=pl(10), op=ALU.add)
                    vector.tensor_tensor(out=pl(10), in0=pl(14), in1=pl(5), op=ALU.mult)
                    vector.tensor_tensor(out=pl(11), in0=pl(15), in1=pl(2), op=ALU.mult)
                    vector.tensor_tensor(out=pl(10), in0=pl(10), in1=pl(11), op=ALU.add)
                    vector.tensor_tensor(out=pl(10), in0=pl(10), in1=pl(13), op=ALU.mult)
                    vector.tensor_tensor(out=pl(11), in0=pl(15), in1=pl(5), op=ALU.mult)
                    vector.tensor_tensor(out=pl(15), in0=pl(14), in1=pl(2), op=ALU.mult)
                    vector.tensor_tensor(out=pl(11), in0=pl(11), in1=pl(15), op=ALU.subtract)
                    vector.tensor_tensor(out=pl(11), in0=pl(11), in1=pl(13), op=ALU.mult)
                    vector.tensor_reduce(out=st_e[:, 16:16 + DCK],
                                         in_=pl(10).rearrange("p (c n) -> p c n", n=8),
                                         axis=AX, op=ALU.add)
                    vector.tensor_tensor(out=pl(14), in0=pl(10), in1=pl(12), op=ALU.mult)
                    vector.tensor_tensor(out=pl(15), in0=pl(11), in1=pl(9), op=ALU.mult)
                    vector.tensor_tensor(out=pl(14), in0=pl(14), in1=pl(15), op=ALU.subtract)
                    vector.tensor_reduce(out=st_e[:, 32:32 + DCK],
                                         in_=pl(14).rearrange("p (c n) -> p c n", n=8),
                                         axis=AX, op=ALU.add)
                    vector.tensor_scalar(out=st_f[:, 0:DCK], in0=st_e[:, 16:16 + DCK],
                                         scalar1=2.0, scalar2=None, op0=ALU.mult)
                    vector.tensor_tensor(out=st_f[:, 0:DCK], in0=st_f[:, 0:DCK],
                                         in1=pl(0)[:, DCK:2 * DCK], op=ALU.add)
                    c_v.inc(vector.tensor_scalar(out=st_f[:, DCK:2 * DCK],
                                                 in0=st_e[:, 32:32 + DCK],
                                                 scalar1=2.0, scalar2=None, op0=ALU.mult))
                v_s4c = c_v.v

                @blk.tensor
                def _(tensor):
                    c_v.wait(tensor, v_s4c)
                    c_t.inc(tensor.matmul(p0a[0:32, 0:128], lhsT=st_f[:, 0:32],
                                          rhs=c_idf[:], start=True, stop=True))
                t_ktr = c_t.v

                @blk.vector
                def _(vector):
                    vector = SafeEng(vector, h_v, vstate)
                    c_t.wait(vector, t_ktr)
                    c_v.inc(vector.tensor_copy(out=ktr[:], in_=p0a[0:32, 0:128]))
                v_ktr = c_v.v

                @blk.gpsimd
                def _(gpsimd):
                    c_v.wait(gpsimd, v_ktr)
                    m_gp.dma(gpsimd.dma_start(out=ktmp_d, in_=ktr[:]))
                    gpsimd.wait_ge(_m_gp, m_gp.v)
                    m_gp.dma(gpsimd.dma_start(out=kvec[0:1, :],
                                              in_=ktmp_d.rearrange("a b -> (a b)")[None, :]))
                    if d_kvec is not None:
                        gpsimd.wait_ge(_m_gp, m_gp.v)
                        m_gp.dma(gpsimd.dma_start(out=d_kvec, in_=kvec[0:1, :]))
                gp_kvec = m_gp.v

                t_kb0 = c_t.v
                v_kb0 = c_v.v

                @blk.tensor
                def _(tensor):
                    m_gp.wait(tensor, gp_kvec)
                    for q in range(4):
                        if q >= 1:
                            tensor.wait_ge(_c_v, v_kb0 + q)
                        tensor.matmul(p0a[:, 0:512], lhsT=c_onesr[:],
                                      rhs=kvec[0:1, 512 * q:512 * (q + 1)],
                                      start=True, stop=True)
                        c_t.inc(tensor.matmul(p0b[:, 0:512], lhsT=c_onesr[:],
                                              rhs=kvec[0:1, D + 512 * q:D + 512 * (q + 1)],
                                              start=True, stop=True))

                @blk.vector
                def _(vector):
                    vector = SafeEng(vector, h_v, vstate)
                    for q in range(4):
                        vector.wait_ge(_c_t, t_kb0 + q + 1)
                        vector.tensor_copy(out=kb0[:, 512 * q:512 * (q + 1)], in_=p0a[:, 0:512])
                        c_v.inc(vector.tensor_copy(out=kb1[:, 512 * q:512 * (q + 1)],
                                                   in_=p0b[:, 0:512]))
                v_kb = c_v.v

                vB0 = c_v.v
                syB0 = m_sy.v
                wq_views = {nm: wqag_in[WOFF[nm]:WOFF[nm] + r * c_].rearrange("(r c) -> r c", r=r)
                            for nm, r, c_, _ in WSLICES}
                passB_in_vals = []

                @blk.scalar
                def _(scalar):
                    scalar = SafeEng(scalar, h_s, sstate)
                    c_v.wait(scalar, v_bcast)
                    for i, (nm, r0, nr, c0, ncl) in enumerate(wchunks):
                        buf, sem = wbufs[i % 2]
                        if i >= 2:
                            scalar.wait_ge(_c_v, vB0 + (i - 1))
                        sem.dma(scalar.dma_start(out=buf[0:nr, 0:ncl],
                                                 in_=wsl[nm][r0:r0 + nr, c0:c0 + ncl]))
                        passB_in_vals.append(sem.v)

                @blk.vector
                def _(vector):
                    vector = SafeEng(vector, h_v, vstate)
                    for i, (nm, r0, nr, c0, ncl) in enumerate(wchunks):
                        buf, sem = wbufs[i % 2]
                        qb = wqbufs[i % 2]
                        col = WCOL[nm]
                        sem.wait(vector, passB_in_vals[i])
                        if i >= 2:
                            osem = m_oa if i % 2 == 0 else m_ob
                            vector.wait_ge(osem.sem, 16 * ((i - 2) // 2 + 1))
                        vector.tensor_scalar(out=buf[0:nr, 0:ncl], in0=buf[0:nr, 0:ncl],
                                             scalar1=q_inv[0:nr, col:col + 1], scalar2=MAGIC,
                                             op0=ALU.mult, op1=ALU.add)
                        vector.tensor_scalar(out=buf[0:nr, 0:ncl], in0=buf[0:nr, 0:ncl],
                                             scalar1=MAGIC - 1.0, scalar2=MAGIC + 1.0,
                                             op0=ALU.max, op1=ALU.min)
                        c_v.inc(vector.tensor_scalar(out=qb[0:nr, 0:ncl], in0=buf[0:nr, 0:ncl],
                                                     scalar1=-MAGIC, scalar2=None, op0=ALU.add))

                @blk.sync
                def _(sync):
                    for i, (nm, r0, nr, c0, ncl) in enumerate(wchunks):
                        qb = wqbufs[i % 2]
                        osem = m_oa if i % 2 == 0 else m_ob
                        sync.wait_ge(_c_v, vB0 + (i + 1))
                        osem.dma(sync.dma_start(out=wq_views[nm][r0:r0 + nr, c0:c0 + ncl],
                                                in_=qb[0:nr, 0:ncl]))
                syB_done = None

                @blk.gpsimd
                def _(gpsimd):
                    gpsimd.wait_ge(_m_oa, m_oa.v)
                    gpsimd.wait_ge(_m_ob, m_ob.v)
                    if "wq" not in skip_cc:
                        m_cc.inc(gpsimd.collective_compute(
                            "AllGather", ALU.bypass, ins=[wqag_in], outs=[wqag_out],
                            replica_groups=[[0, 1, 2, 3, 4, 5, 6, 7]]))
                cc_wq = m_cc.v

            with contextlib.ExitStack() as es1a:
                xab0 = es1a.enter_context(nc.sbuf_tensor("xab0", [128, 4, D], F32))
                xab1 = es1a.enter_context(nc.sbuf_tensor("xab1", [128, 4, D], F32))
                xnb = es1a.enter_context(nc.sbuf_tensor("xnb", [128, 4, D], BF16))
                sqj = es1a.enter_context(nc.sbuf_tensor("sqj", [128, D], F32))
                xnf = es1a.enter_context(nc.sbuf_tensor("xnf", [128, DCK, 512], BF16))
                qst = es1a.enter_context(nc.sbuf_tensor("qst", [128, 4, 512], BF16))
                psA = es1a.enter_context(nc.psum_tensor("psA", [128, 512], F32))
                psB = es1a.enter_context(nc.psum_tensor("psB", [128, 512], F32))
                psT0 = es1a.enter_context(nc.psum_tensor("psT0", [128, 512], BF16))
                psT1 = es1a.enter_context(nc.psum_tensor("psT1", [128, 512], BF16))
                xbufs = [(xab0, m_xa), (xab1, m_xb)]
                v0, s0, t0 = c_v.v, c_s.v, c_t.v
                sc0, sy0 = m_sc.v, m_sy.v
                aws = {0: awq, 1: awk, 2: awv}
                wdst = {0: qd, 1: kd, 2: vd}

                def v_vsum(g):
                    return v0 + 19 * g + 1

                def v_vst1(g):
                    return v0 + 19 * g + 2

                def v_vst2(g):
                    return v0 + 19 * g + 3

                def v_evac(j):
                    return v0 + 19 * (j // 16) + 3 + (j % 16) + 1

                def s_ssq(g):
                    return s0 + 18 * g + 1

                def s_ssq2(g):
                    return s0 + 18 * g + 2

                def s_sxn(g, tt):
                    return s0 + 18 * g + 2 + tt + 1

                def s_qevac(j):
                    return s0 + 18 * (j // 12) + 6 + (j % 12) + 1

                def t_tr(j):
                    return t0 + 28 * (j // 16) + (j % 16) + 1

                def t_qkv(j):
                    return t0 + 28 * (j // 12) + 16 + (j % 12) + 1

                @blk.sync
                def _(sync):
                    sync.wait_ge(_m_oa, m_oa.v)
                    sync.wait_ge(_m_ob, m_ob.v)
                    sync.wait_ge(_m_wa, m_wa.v)
                    sync.wait_ge(_m_wb, m_wb.v)
                    sync.wait_ge(_m_gp, m_gp.v)
                    for g in range(NG):
                        buf, sem = xbufs[g % 2]
                        if g >= 2:
                            sync.wait_ge(_c_s, s_sxn(g - 2, 3))
                        sem.dma(sync.dma_start(
                            out=buf[:],
                            in_=xa[512 * g:512 * (g + 1), :]
                            .rearrange("(tt p) d -> p tt d", p=128)))

                @blk.vector
                def _(vector):
                    vector = SafeEng(vector, h_v, vstate)
                    vector.wait_ge(_m_oa, m_oa.v)
                    vector.wait_ge(_m_ob, m_ob.v)
                    for g in range(NG):
                        buf, sem = xbufs[g % 2]
                        cc0 = 8 * (g % 2)
                        sem.wait(vector, 16 * (g // 2 + 1))
                        for tt in range(4):
                            vector.reduce_sum(out=st_a[:, cc0 + tt:cc0 + tt + 1],
                                              in_=buf[:, tt, :], axis=AX)
                        c_v.inc(vector.tensor_copy(out=st_a[:, cc0 + 4:cc0 + 5],
                                                   in_=st_a[:, cc0:cc0 + 1]))
                        vector.wait_ge(_c_s, s_ssq(g))
                        a4 = st_a[:, cc0:cc0 + 4]
                        b4 = st_b[:, cc0:cc0 + 4]
                        c4 = st_c[:, cc0:cc0 + 4]
                        d4 = st_d[:, cc0:cc0 + 4]
                        e4 = st_e[:, cc0:cc0 + 4]
                        f4 = st_f[:, cc0:cc0 + 4]
                        vector.tensor_scalar(out=a4, in0=a4, scalar1=1.0 / D, scalar2=None,
                                             op0=ALU.mult)
                        vector.tensor_scalar(out=b4, in0=b4, scalar1=1.0 / D, scalar2=None,
                                             op0=ALU.mult)
                        vector.tensor_tensor(out=c4, in0=a4, in1=a4, op=ALU.mult)
                        vector.tensor_tensor(out=c4, in0=b4, in1=c4, op=ALU.subtract)
                        vector.tensor_scalar(out=c4, in0=c4, scalar1=LNEPS, scalar2=None,
                                             op0=ALU.add)
                        c_v.inc(vector.reciprocal(out=d4, in_=c4))
                        vector.wait_ge(_c_s, s_ssq2(g))
                        vector.tensor_tensor(out=f4, in0=e4, in1=e4, op=ALU.mult)
                        vector.tensor_tensor(out=f4, in0=f4, in1=c4, op=ALU.mult)
                        vector.tensor_scalar(out=f4, in0=f4, scalar1=-0.5, scalar2=1.5,
                                             op0=ALU.mult, op1=ALU.add)
                        vector.tensor_tensor(out=c4, in0=e4, in1=f4, op=ALU.mult)
                        c_v.inc(vector.scalar_tensor_tensor(out=d4, in0=a4, scalar=-1.0,
                                                            in1=c4, op0=ALU.mult,
                                                            op1=ALU.mult))
                        for ck in range(DCK):
                            j = 16 * g + ck
                            vector.wait_ge(_c_t, t_tr(j))
                            ps = psT0 if j % 2 == 0 else psT1
                            c_v.inc(vector.tensor_copy(out=xnf[:, ck, :], in_=ps[:]))

                @blk.scalar
                def _(scalar):
                    scalar = SafeEng(scalar, h_s, sstate)
                    scalar.wait_ge(_m_oa, m_oa.v)
                    scalar.wait_ge(_m_ob, m_ob.v)
                    for g in range(NG):
                        buf, sem = xbufs[g % 2]
                        cc0 = 8 * (g % 2)
                        sem.wait(scalar, 16 * (g // 2 + 1))
                        for tt in range(4):
                            acc = st_b[:, cc0 + tt:cc0 + tt + 1]
                            inst = scalar.activation(out=sqj[:], in_=buf[:, tt, :],
                                                     func=AF.Square, accum_out=acc)
                            if tt == 3:
                                c_s.inc(inst)
                        scalar.wait_ge(_c_v, v_vst1(g))
                        c_s.inc(scalar.activation(out=st_e[:, cc0:cc0 + 4],
                                                  in_=st_d[:, cc0:cc0 + 4], func=AF.Sqrt))
                        scalar.wait_ge(_c_v, v_vst2(g))
                        if g >= 1:
                            scalar.wait_ge(_c_t, t_tr(16 * (g - 1) + 15))
                        for tt in range(4):
                            c_s.inc(scalar.activation(
                                out=xnb[:, tt, :], in_=buf[:, tt, :], func=AF.Identity,
                                scale=st_c[:, cc0 + tt:cc0 + tt + 1],
                                bias=st_d[:, cc0 + tt:cc0 + tt + 1]))
                        if d_xna is not None:
                            scalar.wait_ge(_c_s, c_s.v)
                            m_dbg.dma(scalar.dma_start(
                                out=d_xna[512 * g:512 * (g + 1), :]
                                .rearrange("(tt p) d -> p tt d", p=128),
                                in_=xnb[:]))
                            scalar.wait_ge(_m_dbg, m_dbg.v)
                        for m in range(12):
                            j = 12 * g + m
                            u, which = m % 4, m // 4
                            slot = m_qs[j % 4]
                            scalar.wait_ge(_c_t, t_qkv(j))
                            if j >= 4:
                                scalar.wait_ge(slot.sem, slot.v)
                            ps = psA if j % 2 == 0 else psB
                            c_s.inc(scalar.activation(out=qst[:, j % 4, :], in_=ps[:],
                                                      func=AF.Identity))
                            scalar.wait_ge(_c_s, c_s.v)
                            slot.dma(scalar.dma_start(
                                out=wdst[which][u, :, 512 * g:512 * (g + 1)],
                                in_=qst[:, j % 4, :]))

                @blk.tensor
                def _(tensor):
                    m_gp.wait(tensor)
                    for g in range(NG):
                        for ck in range(DCK):
                            j = 16 * g + ck
                            if ck == 0:
                                tensor.wait_ge(_c_s, s_sxn(g, 3))
                            if j >= 2:
                                tensor.wait_ge(_c_v, v_evac(j - 2))
                            ps = psT0 if j % 2 == 0 else psT1
                            for tt in range(4):
                                inst = tensor.transpose(ps[:, 128 * tt:128 * (tt + 1)],
                                                        in_=xnb[:, tt, 128 * ck:128 * (ck + 1)],
                                                        identity=c_idb[:])
                                if tt == 3:
                                    c_t.inc(inst)
                        tensor.wait_ge(_c_v, v_evac(16 * g + 15))
                        for m in range(12):
                            j = 12 * g + m
                            u, which = m % 4, m // 4
                            if j >= 2:
                                tensor.wait_ge(_c_s, s_qevac(j - 2))
                            ps = psA if j % 2 == 0 else psB
                            aw = aws[which]
                            for ck in range(DCK):
                                inst = tensor.matmul(ps[:], lhsT=aw[:, u * DCK + ck, :],
                                                     rhs=xnf[:, ck, :],
                                                     start=(ck == 0), stop=(ck == DCK - 1))
                                if ck == DCK - 1:
                                    c_t.inc(inst)
                ph1a_done_vals = [s.v for s in m_qs]

                if d_aw is not None:
                    @blk.gpsimd
                    def _(gpsimd):
                        for s, v in zip(m_qs, ph1a_done_vals):
                            gpsimd.wait_ge(s.sem, v)
                        m_dbg.dma(gpsimd.dma_start(out=d_aw, in_=awq[:]))
                        m_dbg.dma(gpsimd.dma_start(out=d_xnf, in_=xnf[:]))
                if d_qkv is not None:
                    @blk.gpsimd
                    def _(gpsimd):
                        for s, v in zip(m_qs, ph1a_done_vals):
                            gpsimd.wait_ge(s.sem, v)
                        m_gp.dma(gpsimd.dma_start(out=dbg["qkv"][0], in_=qd))
                        m_gp.dma(gpsimd.dma_start(out=dbg["qkv"][1], in_=kd))
                        m_gp.dma(gpsimd.dma_start(out=dbg["qkv"][2], in_=vd))


        if upto != 'ph1a':
            with contextlib.ExitStack() as es1b:
                wo_b = es1b.enter_context(nc.sbuf_tensor("wo_b", [128, NH, D], BF16))
                Kt = es1b.enter_context(nc.sbuf_tensor("Kt", [128, S], BF16))
                Vf = es1b.enter_context(nc.sbuf_tensor("Vf", [128, S], BF16))
                Qt = es1b.enter_context(nc.sbuf_tensor("Qt", [128, S], BF16))
                Vt = es1b.enter_context(nc.sbuf_tensor("Vt", [128, 32, 128], BF16))
                Ssb = es1b.enter_context(nc.sbuf_tensor("Ssb", [128, S], F32))
                Pb = es1b.enter_context(nc.sbuf_tensor("Pb", [128, S], BF16))
                Ptb = es1b.enter_context(nc.sbuf_tensor("Ptb", [128, 2, 4, 128], BF16))
                zb = es1b.enter_context(nc.sbuf_tensor("zb", [128, 32, 128], BF16))
                zfst = es1b.enter_context(nc.sbuf_tensor("zfst", [128, NG, 512], BF16))
                zc_b = es1b.enter_context(nc.sbuf_tensor("zc_b", [128, 32, 512], BF16))
                sst0 = es1b.enter_context(nc.sbuf_tensor("sst0", [128, D], F32))
                sst1 = es1b.enter_context(nc.sbuf_tensor("sst1", [128, D], F32))
                psS0 = es1b.enter_context(nc.psum_tensor("psS0", [128, 512], F32))
                psS1 = es1b.enter_context(nc.psum_tensor("psS1", [128, 512], F32))
                psPt0 = es1b.enter_context(nc.psum_tensor("psPt0", [128, 512], BF16))
                psPt1 = es1b.enter_context(nc.psum_tensor("psPt1", [128, 512], BF16))
                psZ = es1b.enter_context(nc.psum_tensor("psZ", [128, 128], F32))
                psO0 = es1b.enter_context(nc.psum_tensor("psO0", [128, 512], F32))
                psO1 = es1b.enter_context(nc.psum_tensor("psO1", [128, 512], F32))

                @blk.gpsimd
                def _(gpsimd):
                    for s_, v_ in zip(m_qs, ph1a_done_vals):
                        gpsimd.wait_ge(s_.sem, v_)
                    m_gp.dma(gpsimd.dma_start(
                        out=wo_b[:], in_=wo_t.rearrange("h p d -> p h d")))
                gp_wo = m_gp.v

                def nchunks(t):
                    return cdiv(t + 1, 4)

                UT_T = 8 + sum(nchunks(t) + 2 * nchunks(t) for t in range(32)) + 8
                UT_V = 8 + sum(nchunks(t) for t in range(32)) + 32 * 3 + \
                    sum(nchunks(t) for t in range(32)) + 8
                UT_S = sum(nchunks(t) for t in range(32)) + 32
                t1b0, v1b0, s1b0 = c_t.v, c_v.v, c_s.v

                def tb(u):
                    return t1b0 + u * UT_T

                def vb(u):
                    return v1b0 + u * UT_V

                def sb(u):
                    return s1b0 + u * UT_S

                NCH = [nchunks(t) for t in range(32)]
                SNCH = [0]
                for t in range(32):
                    SNCH.append(SNCH[-1] + NCH[t])
                TOT = SNCH[32]

                def t_vtr(u, vbi):
                    return tb(u) + vbi + 1

                def t_score(u, t, ch):
                    return tb(u) + 8 + SNCH[t] + 2 * SNCH[max(t - 1, 0)] + ch + 1

                def t_ptr(u, t, jb):
                    return tb(u) + 8 + SNCH[min(t + 2, 32)] + 2 * SNCH[t] + 2 * jb + 1

                def t_pv(u, t, jb):
                    return tb(u) + 8 + SNCH[min(t + 2, 32)] + 2 * SNCH[t] + 2 * jb + 2

                def t_ztr(u, qg):
                    return tb(u) + 8 + SNCH[32] * 3 + qg + 1

                def t_unit_end(u):
                    return tb(u) + UT_T

                def v_vtr(u, vbi):
                    return vb(u) + vbi + 1

                def v_head_t(u, t):
                    return vb(u) + 8 + (SNCH[t] + t) \
                        + (SNCH[max(t - 1, 0)] + 2 * max(t - 1, 0))

                def v_max(u, t, ch):
                    return v_head_t(u, t) + ch + 1

                def v_rowmax(u, t):
                    return v_head_t(u, t) + NCH[t] + 1

                def v_tail_t(u, t):
                    return vb(u) + 8 + (SNCH[min(t + 2, 32)] + min(t + 2, 32)) \
                        + (SNCH[t] + 2 * t)

                def v_rl(u, t):
                    return v_tail_t(u, t) + 1

                def v_ptev(u, t, jb):
                    return v_tail_t(u, t) + 1 + jb + 1

                def v_zev(u, t):
                    return v_tail_t(u, t) + 1 + NCH[t] + 1

                def v_zfm(u, qg):
                    return vb(u) + 8 + SNCH[32] * 2 + 32 * 3 + qg + 1

                def s_ssb(u, t, ch):
                    return sb(u) + SNCH[t] + t + ch + 1

                def s_exp(u, t):
                    return sb(u) + SNCH[t + 1] + t + 1

                kv_vals = []

                @blk.sync
                def _(sync):
                    for s_, v_ in zip(m_qs, ph1a_done_vals):
                        sync.wait_ge(s_.sem, v_)
                    for u in range(4):
                        if u >= 1:
                            sync.wait_ge(_c_t, t_unit_end(u - 1))
                        m_kv.dma(sync.dma_start(out=Kt[:], in_=kd[u]))
                        m_kv.dma(sync.dma_start(out=Vf[:], in_=vd[u]))
                        m_kv.dma(sync.dma_start(out=Qt[:], in_=qd[u]))
                        kv_vals.append(m_kv.v)

                @blk.tensor
                def _(tensor):
                    m_gp.wait(tensor)
                    for u in range(4):
                        m_kv.wait(tensor, kv_vals[u])
                        if u >= 1:
                            tensor.wait_ge(_c_s, sb(u))
                            tensor.wait_ge(_c_v, vb(u))
                        for vbi in range(8):
                            if vbi >= 2:
                                tensor.wait_ge(_c_v, v_vtr(u, vbi - 2))
                            ps = psPt0 if vbi % 2 == 0 else psPt1
                            for i2 in range(4):
                                j = 4 * vbi + i2
                                inst = tensor.transpose(ps[:, 128 * i2:128 * (i2 + 1)],
                                                        in_=Vf[:, 128 * j:128 * (j + 1)],
                                                        identity=c_idb[:])
                                if i2 == 3:
                                    c_t.inc(inst)
                        for st in range(33):
                            if st < 32:
                                t = st
                                for ch in range(NCH[t]):
                                    gch = SNCH[t] + ch
                                    if gch >= 2:
                                        tp = t
                                        chp = ch - 2
                                        while chp < 0:
                                            tp -= 1
                                            chp += NCH[tp]
                                        tensor.wait_ge(_c_v, v_max(u, tp, chp))
                                        tensor.wait_ge(_c_s, s_ssb(u, tp, chp))
                                    w = min(512, 128 * (t + 1) - 512 * ch)
                                    ps = psS0 if gch % 2 == 0 else psS1
                                    c_t.inc(tensor.matmul(
                                        ps[:, 0:w], lhsT=Qt[:, 128 * t:128 * (t + 1)],
                                        rhs=Kt[:, 512 * ch:512 * ch + w],
                                        start=True, stop=True))
                            if st >= 1:
                                t = st - 1
                                tensor.wait_ge(_c_s, s_exp(u, t))
                                if t >= 1:
                                    tensor.wait_ge(_c_v, v_zev(u, t - 1))
                                for jb in range(NCH[t]):
                                    nj = min(4, (t + 1) - 4 * jb)
                                    if jb >= 2:
                                        tensor.wait_ge(_c_v, v_ptev(u, t, jb - 2))
                                    ps = psPt0 if jb % 2 == 0 else psPt1
                                    for i2 in range(nj):
                                        j = 4 * jb + i2
                                        inst = tensor.transpose(
                                            ps[:, 128 * i2:128 * (i2 + 1)],
                                            in_=Pb[:, 128 * j:128 * (j + 1)],
                                            identity=c_idb[:])
                                        if i2 == nj - 1:
                                            c_t.inc(inst)
                                    tensor.wait_ge(_c_v, v_ptev(u, t, jb))
                                    for i2 in range(nj):
                                        j = 4 * jb + i2
                                        inst = tensor.matmul(psZ[:], lhsT=Ptb[:, jb % 2, i2, :],
                                                             rhs=Vt[:, j, :],
                                                             start=(j == 0), stop=(j == t))
                                        if i2 == nj - 1:
                                            c_t.inc(inst)
                        for qg in range(NG):
                            if qg >= 2:
                                tensor.wait_ge(_c_v, v_zfm(u, qg - 2))
                            ps = psPt0 if qg % 2 == 0 else psPt1
                            for i2 in range(4):
                                qt = 4 * qg + i2
                                inst = tensor.transpose(ps[:, 128 * i2:128 * (i2 + 1)],
                                                        in_=zb[:, qt, :],
                                                        identity=c_idb[:])
                                if i2 == 3:
                                    c_t.inc(inst)
                        assert c_t.v == t_unit_end(u), (u, c_t.v, t_unit_end(u))

                @blk.vector
                def _(vector):
                    vector = SafeEng(vector, h_v, vstate)
                    for u in range(4):
                        for vbi in range(8):
                            vector.wait_ge(_c_t, t_vtr(u, vbi))
                            ps = psPt0 if vbi % 2 == 0 else psPt1
                            c_v.inc(vector.tensor_copy(
                                out=Vt[:, 4 * vbi:4 * (vbi + 1), :],
                                in_=ps[:].rearrange("p (a b) -> p a b", a=4)))
                        for st in range(33):
                            if st < 32:
                                t = st
                                p4 = 4 * (t % 2)
                                diag_ch = NCH[t] - 1
                                diag_off = 128 * t - 512 * diag_ch
                                for ch in range(NCH[t]):
                                    gch = SNCH[t] + ch
                                    vector.wait_ge(_c_t, t_score(u, t, ch))
                                    ps = psS0 if gch % 2 == 0 else psS1
                                    w = min(512, 128 * (t + 1) - 512 * ch)
                                    if ch == diag_ch:
                                        vector.tensor_tensor(
                                            out=ps[:, diag_off:diag_off + 128],
                                            in0=ps[:, diag_off:diag_off + 128],
                                            in1=c_tri[:], op=ALU.add)
                                    c_v.inc(vector.reduce_max(out=st_a[:, ch:ch + 1],
                                                              in_=ps[:, 0:w], axis=AX))
                                vector.reduce_max(out=st_b[:, p4:p4 + 1],
                                                  in_=st_a[:, 0:NCH[t]], axis=AX)
                                c_v.inc(vector.tensor_scalar(out=st_b[:, p4 + 1:p4 + 2],
                                                             in0=st_b[:, p4:p4 + 1],
                                                             scalar1=-SCALE_ATT, scalar2=None,
                                                             op0=ALU.mult))
                            if st >= 1:
                                t = st - 1
                                p4 = 4 * (t % 2)
                                vector.wait_ge(_c_s, s_exp(u, t))
                                c_v.inc(vector.reciprocal(out=st_b[:, p4 + 3:p4 + 4],
                                                          in_=st_b[:, p4 + 2:p4 + 3]))
                                for jb in range(NCH[t]):
                                    nj = min(4, (t + 1) - 4 * jb)
                                    vector.wait_ge(_c_t, t_ptr(u, t, jb))
                                    ps = psPt0 if jb % 2 == 0 else psPt1
                                    c_v.inc(vector.tensor_copy(
                                        out=Ptb[:, jb % 2, 0:nj, :],
                                        in_=ps[:, 0:128 * nj]
                                        .rearrange("p (a b) -> p a b", a=nj)))
                                vector.wait_ge(_c_t, t_pv(u, t, NCH[t] - 1))
                                c_v.inc(vector.tensor_scalar(out=zb[:, t, :], in0=psZ[:],
                                                             scalar1=st_b[:, p4 + 3:p4 + 4],
                                                             scalar2=None, op0=ALU.mult))
                        for qg in range(NG):
                            vector.wait_ge(_c_t, t_ztr(u, qg))
                            if u >= 1 and qg == 0:
                                vector.wait_ge(m_qs[u - 1].sem,
                                               ph1a_done_vals[u - 1] + 16)
                            ps = psPt0 if qg % 2 == 0 else psPt1
                            c_v.inc(vector.tensor_copy(out=zfst[:, qg, :], in_=ps[:]))
                        assert c_v.v == vb(u) + UT_V, (u, c_v.v)

                @blk.scalar
                def _(scalar):
                    scalar = SafeEng(scalar, h_s, sstate)
                    for u in range(4):
                        if d_qt0 is not None and u == 3:
                            scalar.wait_ge(_m_kv, kv_vals[3])
                            m_dbg.dma(scalar.dma_start(out=d_qt0[0], in_=Qt[:]))
                            m_dbg.dma(scalar.dma_start(out=d_qt0[1], in_=Kt[:]))
                        for st in range(33):
                            if st >= 1:
                                t = st - 1
                                p4 = 4 * (t % 2)
                                scalar.wait_ge(_c_v, v_rowmax(u, t))
                                if t >= 1:
                                    scalar.wait_ge(_c_t, t_ptr(u, t - 1, NCH[t - 1] - 1))
                                c_s.inc(scalar.activation(out=Pb[:, 0:128 * (t + 1)],
                                                          in_=Ssb[:, 0:128 * (t + 1)],
                                                          func=AF.Exp, scale=SCALE_ATT,
                                                          bias=st_b[:, p4 + 1:p4 + 2],
                                                          accum_out=st_b[:, p4 + 2:p4 + 3]))
                            if st < 32:
                                t = st
                                for ch in range(NCH[t]):
                                    gch = SNCH[t] + ch
                                    scalar.wait_ge(_c_v, v_max(u, t, ch))
                                    ps = psS0 if gch % 2 == 0 else psS1
                                    w = min(512, 128 * (t + 1) - 512 * ch)
                                    c_s.inc(scalar.activation(
                                        out=Ssb[:, 512 * ch:512 * ch + w],
                                        in_=ps[:, 0:w], func=AF.Identity))
                        scalar.wait_ge(_c_v, v_zfm(u, NG - 1))
                        m_qs[u].dma(scalar.dma_start(
                            out=zag_in[:, :, u, :].rearrange("g e q -> e g q"),
                            in_=zfst[:]))
                z_write_vals = [s_.v for s_ in m_qs]

                @blk.gpsimd
                def _(gpsimd):
                    for s_, v_ in zip(m_qs, z_write_vals):
                        gpsimd.wait_ge(s_.sem, v_)
                    if "z" not in skip_cc:
                        m_cc.inc(gpsimd.collective_compute(
                            "AllGather", ALU.bypass, ins=[zag_in], outs=[zag_out],
                            replica_groups=[[0, 1, 2, 3, 4, 5, 6, 7]]))
                cc_z = m_cc.v

                if d_z is not None:
                    @blk.gpsimd
                    def _(gpsimd):
                        gpsimd.wait_ge(_m_cc, cc_z)
                        m_dbg.dma(gpsimd.dma_start(out=d_z, in_=zag_out))

                if d_qt is not None:
                    @blk.gpsimd
                    def _(gpsimd):
                        gpsimd.wait_ge(_m_cc, cc_z)
                        m_dbg.dma(gpsimd.dma_start(out=d_qt[0], in_=qd[3]))
                        m_dbg.dma(gpsimd.dma_start(out=d_qt[1], in_=kd[3]))

                if upto != 'ph1b':
                    zc_val = []

                    @blk.gpsimd
                    def _(gpsimd):
                        gpsimd.wait_ge(_m_cc, cc_z)
                        from concourse.bass_core_rust import IndirectOffsetOnAxis
                        zflat = zag_out.rearrange("r g e u q -> (r g e u q)") \
                            .rearrange("(x y) -> x y", y=4 * 512)
                        for r in range(8):
                            inst = gpsimd.indirect_dma_start(
                                out=zc_b[:, 4 * r:4 * (r + 1), :].rearrange("p a b -> p (a b)"),
                                out_offset=None, in_=zflat,
                                in_offset=IndirectOffsetOnAxis(zsel_t[r][:], axis=0))
                            m_zc.dma(inst)
                            gpsimd.wait_ge(_m_zc, m_zc.v)
                        zc_val.append(m_zc.v)

                    if upto != 'ph1cload':
                        t1c0, v1c0 = c_t.v, c_v.v

                        def t_oproj(i, dc):
                            return t1c0 + 4 * i + dc + 1

                        def v_oev(i, dc):
                            return v1c0 + 4 * i + dc + 1

                        @blk.tensor
                        def _(tensor):
                            m_zc.wait(tensor, zc_val[0])
                            m_gp.wait(tensor, gp_wo)
                            for i in range(NT):
                                bb, k = i // 4, i % 4
                                for dc in range(4):
                                    gdc = 4 * i + dc
                                    if gdc >= 2:
                                        tensor.wait_ge(_c_v, v_oev((gdc - 2) // 4, (gdc - 2) % 4))
                                    ps = psO0 if gdc % 2 == 0 else psO1
                                    for h in range(NH):
                                        blk_i = 4 * (4 * bb + h // 4) + (h % 4)
                                        inst = tensor.matmul(ps[:], lhsT=zc_b[:, blk_i, 128 * k:128 * (k + 1)],
                                                             rhs=wo_b[:, h, 512 * dc:512 * (dc + 1)],
                                                             start=(h == 0), stop=(h == NH - 1))
                                        if h == NH - 1:
                                            c_t.inc(inst)

                        oa_b, ob_b = m_oa.v, m_ob.v

                        @blk.vector
                        def _(vector):
                            vector = SafeEng(vector, h_v, vstate)
                            for i in range(NT):
                                sst = sst0 if i % 2 == 0 else sst1
                                osem = m_oa if i % 2 == 0 else m_ob
                                if i >= 2:
                                    vector.wait_ge(osem.sem,
                                                   (oa_b if i % 2 == 0 else ob_b)
                                                   + 16 * ((i - 2) // 2 + 1))
                                for dc in range(4):
                                    gdc = 4 * i + dc
                                    vector.wait_ge(_c_t, t_oproj(i, dc))
                                    ps = psO0 if gdc % 2 == 0 else psO1
                                    c_v.inc(vector.tensor_copy(out=sst[:, 512 * dc:512 * (dc + 1)],
                                                               in_=ps[:]))

                        @blk.scalar
                        def _(scalar):
                            scalar = SafeEng(scalar, h_s, sstate)
                            for i in range(NT):
                                sst = sst0 if i % 2 == 0 else sst1
                                osem = m_oa if i % 2 == 0 else m_ob
                                scalar.wait_ge(_c_v, v_oev(i, 3))
                                osem.dma(scalar.dma_start(out=sd[128 * i:128 * (i + 1), :],
                                                          in_=sst[:]))
                        ph1c_s_vals = (m_oa.v, m_ob.v)

                        if d_s is not None:
                            @blk.gpsimd
                            def _(gpsimd):
                                gpsimd.wait_ge(_m_oa, ph1c_s_vals[0])
                                gpsimd.wait_ge(_m_ob, ph1c_s_vals[1])
                                m_dbg.dma(gpsimd.dma_start(out=d_s, in_=sd))


        if upto not in ('ph1a', 'ph1b', 'ph1cload', 'ph1c'):
            with contextlib.ExitStack() as es2:
                xq_fm = es2.enter_context(nc.sbuf_tensor("xq_fm", [128, DCK, 1024], BF16))
                sxg = es2.enter_context(nc.sbuf_tensor("sxg", [128, 8], F32))
                sxu = es2.enter_context(nc.sbuf_tensor("sxu", [128, 8], F32))
                rmx = es2.enter_context(nc.sbuf_tensor("rmx", [128, 8], F32))
                spd = es2.enter_context(nc.sbuf_tensor("spd", [128, 8], F32))
                invsp = es2.enter_context(nc.sbuf_tensor("invsp", [128, 8], F32))

                with contextlib.ExitStack() as es2a:
                    xob = es2a.enter_context(nc.sbuf_tensor("xob", [128, NT, D], F32))
                    xnt = es2a.enter_context(nc.sbuf_tensor("xnt", [128, D], F32))
                    tq = es2a.enter_context(nc.sbuf_tensor("tq", [128, D], F32))
                    sqj2 = es2a.enter_context(nc.sbuf_tensor("sqj2", [128, D], F32))
                    xq_tm = es2a.enter_context(nc.sbuf_tensor("xq_tm", [128, NT, D], BF16))
                    psT2a = es2a.enter_context(nc.psum_tensor("psT2a", [128, 512], BF16))
                    psT2b = es2a.enter_context(nc.psum_tensor("psT2b", [128, 512], BF16))

                    v2a0, s2a0, t2a0 = c_v.v, c_s.v, c_t.v
                    xo_val = []

                    @blk.sync
                    def _(sync):
                        sync.wait_ge(_c_t, c_t.v)
                        sync.wait_ge(_c_v, c_v.v)
                        sync.wait_ge(_c_s, c_s.v)
                        sync.wait_ge(_h_v, h_v.v)
                        sync.wait_ge(_h_s, h_s.v)
                        for s_ in m_qs:
                            sync.wait_ge(s_.sem, s_.v)
                        sync.wait_ge(_m_oa, m_oa.v)
                        sync.wait_ge(_m_ob, m_ob.v)
                        sync.wait_ge(_m_kv, m_kv.v)
                        sync.wait_ge(_m_dbg, m_dbg.v)
                        m_sy.dma(sync.dma_start(
                            out=xob[:], in_=xo.rearrange("(i p) d -> p i d", p=128)))
                        xo_val.append(m_sy.v)

                    def v2a(i, k):
                        return v2a0 + 7 * i + k

                    def s2a(i, k):
                        return s2a0 + 3 * i + k

                    @blk.vector
                    def _(vector):
                        vector = SafeEng(vector, h_v, vstate)
                        m_sy.wait(vector, xo_val[0])
                        for i in range(NT):
                            cc0 = 0
                            vector.reduce_sum(out=st_a[:, 0:1], in_=xob[:, i, :], axis=AX)
                            c_v.inc(vector.tensor_copy(out=st_a[:, 1:2], in_=st_a[:, 0:1]))
                            vector.wait_ge(_c_s, s2a(i, 1))
                            vector.tensor_scalar(out=st_a[:, 0:1], in0=st_a[:, 0:1],
                                                 scalar1=1.0 / D, scalar2=None, op0=ALU.mult)
                            vector.tensor_scalar(out=st_b[:, 0:1], in0=st_b[:, 0:1],
                                                 scalar1=1.0 / D, scalar2=None, op0=ALU.mult)
                            vector.tensor_tensor(out=st_c[:, 0:1], in0=st_a[:, 0:1],
                                                 in1=st_a[:, 0:1], op=ALU.mult)
                            vector.tensor_tensor(out=st_c[:, 0:1], in0=st_b[:, 0:1],
                                                 in1=st_c[:, 0:1], op=ALU.subtract)
                            vector.tensor_scalar(out=st_c[:, 0:1], in0=st_c[:, 0:1],
                                                 scalar1=LNEPS, scalar2=None, op0=ALU.add)
                            c_v.inc(vector.reciprocal(out=st_d[:, 0:1], in_=st_c[:, 0:1]))
                            vector.wait_ge(_c_s, s2a(i, 2))
                            vector.tensor_tensor(out=st_f[:, 0:1], in0=st_e[:, 0:1],
                                                 in1=st_e[:, 0:1], op=ALU.mult)
                            vector.tensor_tensor(out=st_f[:, 0:1], in0=st_f[:, 0:1],
                                                 in1=st_c[:, 0:1], op=ALU.mult)
                            vector.tensor_scalar(out=st_f[:, 0:1], in0=st_f[:, 0:1],
                                                 scalar1=-0.5, scalar2=1.5, op0=ALU.mult,
                                                 op1=ALU.add)
                            vector.tensor_tensor(out=st_c[:, 0:1], in0=st_e[:, 0:1],
                                                 in1=st_f[:, 0:1], op=ALU.mult)
                            c_v.inc(vector.scalar_tensor_tensor(out=st_d[:, 0:1], in0=st_a[:, 0:1],
                                                                scalar=-1.0, in1=st_c[:, 0:1],
                                                                op0=ALU.mult, op1=ALU.mult))
                            vector.wait_ge(_c_s, s2a(i, 3))
                            vector.reduce_max(out=st_a[:, 2:3], in_=xnt[:], axis=AX,
                                              apply_absolute_value=True)
                            vector.tensor_scalar(out=st_a[:, 2:3], in0=st_a[:, 2:3],
                                                 scalar1=1.0 / 127.0, scalar2=EPS,
                                                 op0=ALU.mult, op1=ALU.add)
                            c_v.inc(vector.reciprocal(out=st_a[:, 3:4], in_=st_a[:, 2:3]))
                            vector.tensor_tensor(out=sxg[:, i:i + 1], in0=st_a[:, 2:3],
                                                 in1=q_deq[:, 0:1], op=ALU.mult)
                            vector.tensor_tensor(out=sxu[:, i:i + 1], in0=st_a[:, 2:3],
                                                 in1=q_deq[:, 1:2], op=ALU.mult)
                            vector.tensor_scalar(out=tq[:], in0=xnt[:], scalar1=st_a[:, 3:4],
                                                 scalar2=MAGIC, op0=ALU.mult, op1=ALU.add)
                            c_v.inc(vector.tensor_scalar(out=tq[:], in0=tq[:],
                                                         scalar1=MAGIC - 128.0,
                                                         scalar2=MAGIC + 127.0,
                                                         op0=ALU.max, op1=ALU.min))
                            c_v.inc(vector.tensor_scalar(out=xq_tm[:, i, :], in0=tq[:],
                                                         scalar1=-MAGIC, scalar2=None,
                                                         op0=ALU.add))
                            c_v.inc(vector.tensor_copy(out=st_a[:, 4:5], in_=st_a[:, 2:3]))

                    @blk.scalar
                    def _(scalar):
                        scalar = SafeEng(scalar, h_s, sstate)
                        m_sy.wait(scalar, xo_val[0])
                        for i in range(NT):
                            c_s.inc(scalar.activation(out=sqj2[:], in_=xob[:, i, :],
                                                      func=AF.Square, accum_out=st_b[:, 0:1]))
                            scalar.wait_ge(_c_v, v2a(i, 2))
                            c_s.inc(scalar.activation(out=st_e[:, 0:1], in_=st_d[:, 0:1],
                                                      func=AF.Sqrt))
                            scalar.wait_ge(_c_v, v2a(i, 3))
                            c_s.inc(scalar.activation(out=xnt[:], in_=xob[:, i, :],
                                                      func=AF.Identity, scale=st_c[:, 0:1],
                                                      bias=st_d[:, 0:1]))
                            if d_xn is not None:
                                scalar.wait_ge(_c_s, c_s.v)
                                m_dbg.dma(scalar.dma_start(
                                    out=d_xn[128 * i:128 * (i + 1), :], in_=xnt[:]))
                                scalar.wait_ge(_m_dbg, m_dbg.v)
                            scalar.wait_ge(_c_v, v2a(i, 7) - 1)
                    def t2a(j):
                        return t2a0 + j + 1

                    def v2a_ev(j):
                        return v2a0 + 7 * NT + j + 1

                    @blk.tensor
                    def _(tensor):
                        for j in range(2 * DCK):
                            ck, half = j // 2, j % 2
                            if j == 0:
                                tensor.wait_ge(_c_v, v2a0 + 7 * NT)
                            if j >= 2:
                                tensor.wait_ge(_c_v, v2a_ev(j - 2))
                            ps = psT2a if j % 2 == 0 else psT2b
                            for i2 in range(4):
                                i = 4 * half + i2
                                inst = tensor.transpose(ps[:, 128 * i2:128 * (i2 + 1)],
                                                        in_=xq_tm[:, i, 128 * ck:128 * (ck + 1)],
                                                        identity=c_idb[:])
                                if i2 == 3:
                                    c_t.inc(inst)

                    @blk.vector
                    def _(vector):
                        vector = SafeEng(vector, h_v, vstate)
                        for j in range(2 * DCK):
                            ck, half = j // 2, j % 2
                            vector.wait_ge(_c_t, t2a(j))
                            ps = psT2a if j % 2 == 0 else psT2b
                            c_v.inc(vector.tensor_copy(out=xq_fm[:, ck, 512 * half:512 * (half + 1)],
                                                       in_=ps[:]))
                    ph2a_v_end = c_v.v
                    ph2a_t_end = c_t.v

                if upto != 'ph2a':
                    with contextlib.ExitStack() as es2b:
                        wgs0 = es2b.enter_context(nc.sbuf_tensor("wgs0", [128, 8, 2, 512], BF16))
                        wgs1 = es2b.enter_context(nc.sbuf_tensor("wgs1", [128, 8, 2, 512], BF16))
                        wus0 = es2b.enter_context(nc.sbuf_tensor("wus0", [128, 8, 2, 512], BF16))
                        wus1 = es2b.enter_context(nc.sbuf_tensor("wus1", [128, 8, 2, 512], BF16))
                        gst = es2b.enter_context(nc.sbuf_tensor("gst", [128, NT, 512], BF16))
                        pst0 = es2b.enter_context(nc.sbuf_tensor("pst0", [128, NT, 512], BF16))
                        pst1 = es2b.enter_context(nc.sbuf_tensor("pst1", [128, NT, 512], BF16))
                        psG0 = es2b.enter_context(nc.psum_tensor("psG0", [128, 512], F32))
                        psG1 = es2b.enter_context(nc.psum_tensor("psG1", [128, 512], F32))
                        psU0 = es2b.enter_context(nc.psum_tensor("psU0", [128, 512], F32))
                        psU1 = es2b.enter_context(nc.psum_tensor("psU1", [128, 512], F32))

                        NOS = I0 // 512
                        t2b0, s2b0, v2b0 = c_t.v, c_s.v, c_v.v
                        spa_b, spb_b = m_pwa.v, m_pwb.v
                        wg_vals, wu_vals = [], []

                        def t_gate(os, t):
                            return t2b0 + 16 * os + 2 * t + 1

                        def t_up(os, t):
                            return t2b0 + 16 * os + 2 * t + 2

                        def s_silu(os, t):
                            return s2b0 + 8 * os + t + 1

                        def v_prod(os, t):
                            return v2b0 + 1 + 16 * os + 2 * t + 1

                        def v_rmx(os, t):
                            return v2b0 + 1 + 16 * os + 2 * t + 2

                        def wq_view(nm, rpr, cols, os, width):
                            base = WOFF[nm]
                            ckin = rpr // 128
                            return wqag_out[:, base:base + rpr * cols] \
                                .rearrange("r (ki p c) -> r ki p c", ki=ckin, p=128) \
                                [:, :, :, width * os:width * (os + 1)]

                        @blk.sync
                        def _(sync):
                            sync.wait_ge(_m_cc, cc_wq)
                            phase_join(sync)
                            for os in range(NOS):
                                wgs = wgs0 if os % 2 == 0 else wgs1
                                wus = wus0 if os % 2 == 0 else wus1
                                wsem = m_w2a if os % 2 == 0 else m_w2b
                                if os >= 2:
                                    sync.wait_ge(_c_t, t_up(os - 2, NT - 1))
                                for ki in range(2):
                                    wsem.dma(sync.dma_start(
                                        out=wgs[:, :, ki, :],
                                        in_=wq_view("wg", 256, I0, os, 512)[:, ki]
                                        .rearrange("r p c -> p r c")))
                                    wsem.dma(sync.dma_start(
                                        out=wus[:, :, ki, :],
                                        in_=wq_view("wu", 256, I0, os, 512)[:, ki]
                                        .rearrange("r p c -> p r c")))
                                wg_vals.append(wsem.v)

                        @blk.tensor
                        def _(tensor):
                            for os in range(NOS):
                                wgs = wgs0 if os % 2 == 0 else wgs1
                                wus = wus0 if os % 2 == 0 else wus1
                                wsem = m_w2a if os % 2 == 0 else m_w2b
                                wsem.wait(tensor, wg_vals[os])
                                for t in range(NT):
                                    gt = 8 * os + t
                                    psG = psG0 if gt % 2 == 0 else psG1
                                    psU = psU0 if gt % 2 == 0 else psU1
                                    if gt >= 2:
                                        tensor.wait_ge(_c_s, s_silu((gt - 2) // 8, (gt - 2) % 8))
                                        tensor.wait_ge(_c_v, v_prod((gt - 2) // 8, (gt - 2) % 8))
                                    for ck in range(DCK):
                                        inst = tensor.matmul(psG[:], lhsT=xq_fm[:, ck, 128 * t:128 * (t + 1)],
                                                             rhs=wgs[:, ck // 2, ck % 2, :],
                                                             start=(ck == 0), stop=(ck == DCK - 1))
                                        if ck == DCK - 1:
                                            c_t.inc(inst)
                                    for ck in range(DCK):
                                        inst = tensor.matmul(psU[:], lhsT=xq_fm[:, ck, 128 * t:128 * (t + 1)],
                                                             rhs=wus[:, ck // 2, ck % 2, :],
                                                             start=(ck == 0), stop=(ck == DCK - 1))
                                        if ck == DCK - 1:
                                            c_t.inc(inst)

                        @blk.scalar
                        def _(scalar):
                            scalar = SafeEng(scalar, h_s, sstate)
                            for os in range(NOS):
                                for t in range(NT):
                                    gt = 8 * os + t
                                    psG = psG0 if gt % 2 == 0 else psG1
                                    scalar.wait_ge(_c_t, t_gate(os, t))
                                    if os >= 1:
                                        scalar.wait_ge(_c_v, v_prod(os - 1, t))
                                    c_s.inc(scalar.activation(out=gst[:, t, :], in_=psG[:],
                                                              func=AF.Silu, scale=sxg[:, t:t + 1]))

                        @blk.vector
                        def _(vector):
                            vector = SafeEng(vector, h_v, vstate)
                            c_v.inc(vector.tensor_scalar(out=rmx[:], in0=vjk[:, 0:8],
                                                 scalar1=0.0, scalar2=None, op0=ALU.mult))
                            for os in range(NOS):
                                pst = pst0 if os % 2 == 0 else pst1
                                psem = m_spa if os % 2 == 0 else m_spb
                                for t in range(NT):
                                    gt = 8 * os + t
                                    psU = psU0 if gt % 2 == 0 else psU1
                                    vector.wait_ge(_c_t, t_up(os, t))
                                    vector.wait_ge(_c_s, s_silu(os, t))
                                    if os >= 2 and t == 0:
                                        vector.wait_ge(_m_pwa if os % 2 == 0 else _m_pwb,
                                                       (spa_b if os % 2 == 0 else spb_b) + 16 * ((os - 2) // 2 + 1))
                                    c_v.inc(vector.scalar_tensor_tensor(out=pst[:, t, :], in0=psU[:],
                                                                        scalar=sxu[:, t:t + 1],
                                                                        in1=gst[:, t, :],
                                                                        op0=ALU.mult, op1=ALU.mult))
                                    vector.reduce_max(out=st_a[:, 8:9], in_=pst[:, t, :], axis=AX,
                                                      apply_absolute_value=True)
                                    c_v.inc(vector.tensor_tensor(out=rmx[:, t:t + 1],
                                                                 in0=rmx[:, t:t + 1],
                                                                 in1=st_a[:, 8:9], op=ALU.max))

                        @blk.gpsimd
                        def _(gpsimd):
                            for os in range(NOS):
                                pst = pst0 if os % 2 == 0 else pst1
                                psem = m_pwa if os % 2 == 0 else m_pwb
                                gpsimd.wait_ge(_c_v, v_rmx(os, NT - 1))
                                psem.dma(gpsimd.dma_start(
                                    out=prodd.rearrange("(i p) c -> p i c", p=128)
                                    [:, :, 512 * os:512 * (os + 1)],
                                    in_=pst[:]))
                        prod_w_vals = (m_spa.v, m_spb.v)

                        @blk.vector
                        def _(vector):
                            vector = SafeEng(vector, h_v, vstate)
                            vector.wait_ge(_c_v, v_rmx(NOS - 1, NT - 1))
                            vector.tensor_scalar(out=rmx[:], in0=rmx[:], scalar1=1.0 / 127.0,
                                                 scalar2=EPS, op0=ALU.mult, op1=ALU.add)
                            c_v.inc(vector.reciprocal(out=invsp[:], in_=rmx[:]))
                            c_v.inc(vector.tensor_scalar(out=spd[:], in0=rmx[:],
                                                         scalar1=q_deq[:, 2:3], scalar2=None,
                                                         op0=ALU.mult))
                        ph2b_v_end = c_v.v


                    if upto != 'ph2b':
                        with contextlib.ExitStack() as es2c:
                            wdh = es2c.enter_context(nc.sbuf_tensor("wdh", [128, 8, 8, 512], BF16))
                            prt0 = es2c.enter_context(nc.sbuf_tensor("prt0", [128, I0], BF16))
                            prt1 = es2c.enter_context(nc.sbuf_tensor("prt1", [128, I0], BF16))
                            tq2 = es2c.enter_context(nc.sbuf_tensor("tq2", [128, 2048], F32))
                            pq_tm = es2c.enter_context(nc.sbuf_tensor("pq_tm", [128, I0], BF16))
                            pq_fm = es2c.enter_context(nc.sbuf_tensor("pq_fm", [128, 64, 128], BF16))
                            g0h = es2c.enter_context(nc.sbuf_tensor("g0h", [128, 512], F32))
                            sxt0 = es2c.enter_context(nc.sbuf_tensor("sxt0", [128, 1024], F32))
                            sxt1 = es2c.enter_context(nc.sbuf_tensor("sxt1", [128, 1024], F32))
                            x1st0 = es2c.enter_context(nc.sbuf_tensor("x1st0", [128, 512], F32))
                            x1st1 = es2c.enter_context(nc.sbuf_tensor("x1st1", [128, 512], F32))
                            psT2c0 = es2c.enter_context(nc.psum_tensor("psT2c0", [128, 512], BF16))
                            psT2c1 = es2c.enter_context(nc.psum_tensor("psT2c1", [128, 512], BF16))
                            psD0 = es2c.enter_context(nc.psum_tensor("psD0", [128, 512], F32))
                            psD1 = es2c.enter_context(nc.psum_tensor("psD1", [128, 512], F32))

                            t2c0, v2c0, s2c0 = c_t.v, c_v.v, c_s.v
                            w2a0, w2b0 = m_w2a.v, m_w2b.v
                            spa0, spb0 = m_spa.v, m_spb.v
                            oa0, ob0 = m_oa.v, m_ob.v
                            xa0_, xb0_ = m_xa.v, m_xb.v
                            q0_ = [s_.v for s_ in m_qs]

                            wdh_vals = []

                            def wdh_val(q4):
                                return wdh_vals[q4]

                            def prt_val(q4, t):
                                return (spa0 if t % 2 == 0 else spb0) + 16 * (4 * q4 + t // 2 + 1)

                            def pqw_val(t):
                                return (oa0 if t % 2 == 0 else ob0) + 16 * (t // 2 + 1)

                            def sxt_val(q4, t):
                                return (xa0_ if t % 2 == 0 else xb0_) + 32 * (4 * q4 + t // 2 + 1)

                            def x1w_val(q4, t):
                                return q0_[t % 2] + 16 * (4 * q4 + t // 2 + 1)

                            def t_tr2c(q4, t, j):
                                return t2c0 + 17 * (8 * q4 + t) + j + 1

                            def t_dn(q4, t):
                                return t2c0 + 17 * (8 * q4 + t) + 17

                            def vn(q4):
                                return 18 if q4 == 0 else 17

                            def v_base2c(q4, t):
                                return v2c0 + (18 * 8 if q4 >= 1 else 0) + 17 * 8 * max(0, q4 - 1) + vn(q4) * t

                            def v_quant(t):
                                return v2c0 + 18 * t + 1

                            def v_ev2c(q4, t, j):
                                return v_base2c(q4, t) + (1 if q4 == 0 else 0) + j + 1

                            def v_gat(q4, t):
                                return v_base2c(q4, t) + vn(q4)

                            def s_dn(q4, t):
                                return s2c0 + 8 * q4 + t + 1

                            @blk.sync
                            def _(sync):
                                phase_join(sync)
                                for q4 in range(4):
                                    wsem = m_w2a if q4 % 2 == 0 else m_w2b
                                    if q4 >= 1:
                                        sync.wait_ge(_c_t, t_dn(q4 - 1, NT - 1))
                                    for ki in range(8):
                                        wsem.dma(sync.dma_start(
                                            out=wdh[:, :, ki, :],
                                            in_=wq_view("wd", 1024, D, q4, 512)[:, ki]
                                            .rearrange("r p c -> p r c")))
                                    wdh_vals.append(wsem.v)
                                    for t in range(NT):
                                        prt = prt0 if t % 2 == 0 else prt1
                                        psem = m_spa if t % 2 == 0 else m_spb
                                        if q4 == 0:
                                            if t >= 2:
                                                sync.wait_ge(_c_v, v_quant(t - 2))
                                        else:
                                            prev = (q4, t - 2) if t >= 2 else (q4 - 1, NT - 2 + t)
                                            sync.wait_ge(_c_t, t_tr2c(prev[0], prev[1], 15))
                                            if q4 == 1 and t < 2:
                                                sync.wait_ge(_m_oa, oa0 + 16 * 4)
                                                sync.wait_ge(_m_ob, ob0 + 16 * 4)
                                        srcten = prodd if q4 == 0 else pqd
                                        psem.dma(sync.dma_start(
                                            out=prt[:], in_=srcten[128 * t:128 * (t + 1), :]))
                                        sxt = sxt0 if t % 2 == 0 else sxt1
                                        xsem = m_xa if t % 2 == 0 else m_xb
                                        if 8 * q4 + t >= 2:
                                            pj = 8 * q4 + t - 2
                                            sync.wait_ge(_c_v, v_gat(pj // 8, pj % 8))
                                        xsem.dma(sync.dma_start(
                                            out=sxt[:, 0:512],
                                            in_=sd[128 * t:128 * (t + 1), 512 * q4:512 * (q4 + 1)]))
                                        xsem.dma(sync.dma_start(
                                            out=sxt[:, 512:1024],
                                            in_=xo[128 * t:128 * (t + 1), 512 * q4:512 * (q4 + 1)]))

                            @blk.vector
                            def _(vector):
                                vector = SafeEng(vector, h_v, vstate)
                                vector.wait_ge(_c_v, ph2b_v_end)
                                for q4 in range(4):
                                    for t in range(NT):
                                        prt = prt0 if t % 2 == 0 else prt1
                                        psem = m_spa if t % 2 == 0 else m_spb
                                        psem.wait(vector, prt_val(q4, t))
                                        PQ = pq_tm if q4 == 0 else prt
                                        if q4 == 0:
                                            if t >= 1:
                                                vector.wait_ge(_c_t, t_tr2c(q4, t - 1, 15))
                                                pqsem_p = m_oa if (t - 1) % 2 == 0 else m_ob
                                                pqsem_p.wait(vector, pqw_val(t - 1))
                                            for pc in range(4):
                                                sl = slice(2048 * pc, 2048 * (pc + 1))
                                                vector.tensor_scalar(out=tq2[:], in0=prt[:, sl],
                                                                     scalar1=invsp[:, t:t + 1],
                                                                     scalar2=MAGIC, op0=ALU.mult,
                                                                     op1=ALU.add)
                                                vector.tensor_scalar(out=tq2[:], in0=tq2[:],
                                                                     scalar1=MAGIC - 128.0,
                                                                     scalar2=MAGIC + 127.0,
                                                                     op0=ALU.max, op1=ALU.min)
                                                inst = vector.tensor_scalar(out=pq_tm[:, sl], in0=tq2[:],
                                                                            scalar1=-MAGIC, scalar2=None,
                                                                            op0=ALU.add)
                                            c_v.inc(inst)
                                        for j in range(16):
                                            vector.wait_ge(_c_t, t_tr2c(q4, t, j))
                                            ps = psT2c0 if j % 2 == 0 else psT2c1
                                            c_v.inc(vector.tensor_copy(
                                                out=pq_fm[:, 4 * j:4 * (j + 1), :]
                                                .rearrange("p a b -> p (a b)"), in_=ps[:]))
                                        sxt = sxt0 if t % 2 == 0 else sxt1
                                        xsem = m_xa if t % 2 == 0 else m_xb
                                        x1st = x1st0 if t % 2 == 0 else x1st1
                                        xqsem = m_qs[t % 2]
                                        vector.wait_ge(_c_s, s_dn(q4, t))
                                        xsem.wait(vector, sxt_val(q4, t))
                                        if 8 * q4 + t >= 2:
                                            pj = 8 * q4 + t - 2
                                            xqsem.wait(vector, x1w_val(pj // 8, pj % 8)
                                                       if (pj % 8) % 2 == t % 2 else xqsem.v)
                                        vector.tensor_tensor(out=x1st[:], in0=g0h[:],
                                                             in1=sxt[:, 0:512], op=ALU.mult)
                                        c_v.inc(vector.tensor_tensor(out=x1st[:], in0=x1st[:],
                                                                     in1=sxt[:, 512:1024], op=ALU.add))

                            @blk.tensor
                            def _(tensor):
                                for q4 in range(4):
                                    wsem = m_w2a if q4 % 2 == 0 else m_w2b
                                    wsem.wait(tensor, wdh_val(q4))
                                    for t in range(NT):
                                        prt = prt0 if t % 2 == 0 else prt1
                                        psem = m_spa if t % 2 == 0 else m_spb
                                        PQ = pq_tm if q4 == 0 else prt
                                        if q4 == 0:
                                            tensor.wait_ge(_c_v, v_quant(t))
                                        else:
                                            psem.wait(tensor, prt_val(q4, t))
                                        for j in range(16):
                                            gj = 16 * (8 * q4 + t) + j
                                            if gj >= 2:
                                                pj = gj - 2
                                                qt_, jj = divmod(pj, 16)
                                                tensor.wait_ge(_c_v, v_ev2c(qt_ // 8, qt_ % 8, jj))
                                            ps = psT2c0 if j % 2 == 0 else psT2c1
                                            for i2 in range(4):
                                                oc = 4 * j + i2
                                                inst = tensor.transpose(
                                                    ps[:, 128 * i2:128 * (i2 + 1)],
                                                    in_=PQ[:, 128 * oc:128 * (oc + 1)],
                                                    identity=c_idb[:])
                                                if i2 == 3:
                                                    c_t.inc(inst)
                                        tensor.wait_ge(_c_v, v_ev2c(q4, t, 15))
                                        if 8 * q4 + t >= 2:
                                            pj = 8 * q4 + t - 2
                                            tensor.wait_ge(_c_s, s_dn(pj // 8, pj % 8))
                                        psD = psD0 if (8 * q4 + t) % 2 == 0 else psD1
                                        for oc in range(64):
                                            inst = tensor.matmul(psD[:], lhsT=pq_fm[:, oc, :],
                                                                 rhs=wdh[:, oc // 8, oc % 8, :],
                                                                 start=(oc == 0), stop=(oc == 63))
                                            if oc == 63:
                                                c_t.inc(inst)

                            @blk.scalar
                            def _(scalar):
                                scalar = SafeEng(scalar, h_s, sstate)
                                for q4 in range(4):
                                    for t in range(NT):
                                        if q4 == 0:
                                            pqsem = m_oa if t % 2 == 0 else m_ob
                                            scalar.wait_ge(_c_v, v_quant(t))
                                            pqsem.dma(scalar.dma_start(
                                                out=pqd[128 * t:128 * (t + 1), :], in_=pq_tm[:]))
                                        scalar.wait_ge(_c_t, t_dn(q4, t))
                                        if 8 * q4 + t >= 1:
                                            pj = 8 * q4 + t - 1
                                            scalar.wait_ge(_c_v, v_gat(pj // 8, pj % 8))
                                        psD = psD0 if (8 * q4 + t) % 2 == 0 else psD1
                                        c_s.inc(scalar.activation(out=g0h[:], in_=psD[:], func=AF.Silu,
                                                                  scale=spd[:, t:t + 1]))
                                        if d_g0 is not None:
                                            scalar.wait_ge(_c_s, c_s.v)
                                            m_dbg.dma(scalar.dma_start(
                                                out=d_g0[128 * t:128 * (t + 1), 512 * q4:512 * (q4 + 1)],
                                                in_=g0h[:]))
                                            scalar.wait_ge(_m_dbg, m_dbg.v)
                                        x1st = x1st0 if t % 2 == 0 else x1st1
                                        xqsem = m_qs[t % 2]
                                        scalar.wait_ge(_c_v, v_gat(q4, t))
                                        xqsem.dma(scalar.dma_start(
                                            out=x1d[128 * t:128 * (t + 1), 512 * q4:512 * (q4 + 1)],
                                            in_=x1st[:]))
                                        if d_x1 is not None:
                                            m_dbg.dma(scalar.dma_start(
                                                out=d_x1[128 * t:128 * (t + 1), 512 * q4:512 * (q4 + 1)],
                                                in_=x1st[:]))
                                            scalar.wait_ge(_m_dbg, m_dbg.v)
                            ph2_x1_vals = [s_.v for s_ in m_qs]


        if upto not in ('ph1a', 'ph1b', 'ph1cload', 'ph1c', 'ph2a', 'ph2b', 'ph2c'):
            with contextlib.ExitStack() as es4:
                sx1 = es4.enter_context(nc.sbuf_tensor("sx1", [128, 8], F32))
                sx3 = es4.enter_context(nc.sbuf_tensor("sx3", [128, 8], F32))
                rmx1 = es4.enter_context(nc.sbuf_tensor("rmx1", [128, 8], F32))
                spd1 = es4.enter_context(nc.sbuf_tensor("spd1", [128, 8], F32))
                invsp1 = es4.enter_context(nc.sbuf_tensor("invsp1", [128, 8], F32))

                with contextlib.ExitStack() as es4ab:
                    xq1_fm = es4ab.enter_context(
                        nc.sbuf_tensor("xq1_fm", [128, DCK, 1024], BF16))

                    with contextlib.ExitStack() as es4a:
                        x1b = es4a.enter_context(nc.sbuf_tensor("x1b", [128, NT, D], F32))
                        xnt0 = es4a.enter_context(nc.sbuf_tensor("xnt0", [128, D], F32))
                        xnt1 = es4a.enter_context(nc.sbuf_tensor("xnt1", [128, D], F32))
                        tq4 = es4a.enter_context(nc.sbuf_tensor("tq4", [128, D], F32))
                        sqj4 = es4a.enter_context(nc.sbuf_tensor("sqj4", [128, D], F32))
                        xq1_tm = es4a.enter_context(nc.sbuf_tensor("xq1_tm", [128, NT, D], BF16))
                        psT4a = es4a.enter_context(nc.psum_tensor("psT4a", [128, 512], BF16))
                        psT4b = es4a.enter_context(nc.psum_tensor("psT4b", [128, 512], BF16))

                        v4a0, s4a0, t4a0 = c_v.v, c_s.v, c_t.v
                        oa4_0, ob4_0 = m_oa.v, m_ob.v
                        x1_val = []

                        @blk.sync
                        def _(sync):
                            phase_join(sync)
                            m_sy.dma(sync.dma_start(
                                out=x1b[:], in_=x1d.rearrange("(i p) d -> p i d", p=128)))
                            x1_val.append(m_sy.v)

                        def v4a(i, k):
                            return v4a0 + 7 * i + k

                        def s4aa(i, k):
                            return s4a0 + 3 * i + k

                        def xn1w_val(i):
                            return (oa4_0 if i % 2 == 0 else ob4_0) + 16 * (i // 2 + 1)

                        @blk.vector
                        def _(vector):
                            vector = SafeEng(vector, h_v, vstate)
                            m_sy.wait(vector, x1_val[0])
                            for i in range(NT):
                                xnt = xnt0 if i % 2 == 0 else xnt1
                                vector.reduce_sum(out=st_a[:, 0:1], in_=x1b[:, i, :], axis=AX)
                                c_v.inc(vector.tensor_copy(out=st_a[:, 1:2], in_=st_a[:, 0:1]))
                                vector.wait_ge(_c_s, s4aa(i, 1))
                                vector.tensor_scalar(out=st_a[:, 0:1], in0=st_a[:, 0:1],
                                                     scalar1=1.0 / D, scalar2=None, op0=ALU.mult)
                                vector.tensor_scalar(out=st_b[:, 0:1], in0=st_b[:, 0:1],
                                                     scalar1=1.0 / D, scalar2=None, op0=ALU.mult)
                                vector.tensor_tensor(out=st_c[:, 0:1], in0=st_a[:, 0:1],
                                                     in1=st_a[:, 0:1], op=ALU.mult)
                                vector.tensor_tensor(out=st_c[:, 0:1], in0=st_b[:, 0:1],
                                                     in1=st_c[:, 0:1], op=ALU.subtract)
                                vector.tensor_scalar(out=st_c[:, 0:1], in0=st_c[:, 0:1],
                                                     scalar1=LNEPS, scalar2=None, op0=ALU.add)
                                c_v.inc(vector.reciprocal(out=st_d[:, 0:1], in_=st_c[:, 0:1]))
                                vector.wait_ge(_c_s, s4aa(i, 2))
                                vector.tensor_tensor(out=st_f[:, 0:1], in0=st_e[:, 0:1],
                                                     in1=st_e[:, 0:1], op=ALU.mult)
                                vector.tensor_tensor(out=st_f[:, 0:1], in0=st_f[:, 0:1],
                                                     in1=st_c[:, 0:1], op=ALU.mult)
                                vector.tensor_scalar(out=st_f[:, 0:1], in0=st_f[:, 0:1],
                                                     scalar1=-0.5, scalar2=1.5, op0=ALU.mult,
                                                     op1=ALU.add)
                                vector.tensor_tensor(out=st_c[:, 0:1], in0=st_e[:, 0:1],
                                                     in1=st_f[:, 0:1], op=ALU.mult)
                                c_v.inc(vector.scalar_tensor_tensor(out=st_d[:, 0:1],
                                                                    in0=st_a[:, 0:1],
                                                                    scalar=-1.0, in1=st_c[:, 0:1],
                                                                    op0=ALU.mult, op1=ALU.mult))
                                vector.wait_ge(_c_s, s4aa(i, 3))
                                vector.reduce_max(out=st_a[:, 2:3], in_=xnt[:], axis=AX,
                                                  apply_absolute_value=True)
                                vector.tensor_scalar(out=st_a[:, 2:3], in0=st_a[:, 2:3],
                                                     scalar1=EPS, scalar2=None, op0=ALU.max)
                                c_v.inc(vector.reciprocal(out=st_a[:, 3:4], in_=st_a[:, 2:3]))
                                vector.tensor_scalar(out=st_a[:, 4:5], in0=st_a[:, 3:4],
                                                     scalar1=127.0, scalar2=None, op0=ALU.mult)
                                vector.tensor_scalar(out=st_a[:, 5:6], in0=st_a[:, 2:3],
                                                     scalar1=1.0 / 127.0, scalar2=None,
                                                     op0=ALU.mult)
                                vector.tensor_tensor(out=sx1[:, i:i + 1], in0=st_a[:, 5:6],
                                                     in1=q_deq[:, 3:4], op=ALU.mult)
                                vector.tensor_tensor(out=sx3[:, i:i + 1], in0=st_a[:, 5:6],
                                                     in1=q_deq[:, 4:5], op=ALU.mult)
                                vector.tensor_scalar(out=tq4[:], in0=xnt[:],
                                                     scalar1=st_a[:, 4:5], scalar2=MAGIC,
                                                     op0=ALU.mult, op1=ALU.add)
                                c_v.inc(vector.tensor_scalar(out=tq4[:], in0=tq4[:],
                                                             scalar1=MAGIC - 128.0,
                                                             scalar2=MAGIC + 127.0,
                                                             op0=ALU.max, op1=ALU.min))
                                c_v.inc(vector.tensor_scalar(out=xq1_tm[:, i, :], in0=tq4[:],
                                                             scalar1=-MAGIC, scalar2=None,
                                                             op0=ALU.add))
                                c_v.inc(vector.tensor_copy(out=st_a[:, 6:7], in_=st_a[:, 2:3]))

                        @blk.scalar
                        def _(scalar):
                            scalar = SafeEng(scalar, h_s, sstate)
                            m_sy.wait(scalar, x1_val[0])
                            for i in range(NT):
                                xnt = xnt0 if i % 2 == 0 else xnt1
                                xnsem = m_oa if i % 2 == 0 else m_ob
                                c_s.inc(scalar.activation(out=sqj4[:], in_=x1b[:, i, :],
                                                          func=AF.Square,
                                                          accum_out=st_b[:, 0:1]))
                                scalar.wait_ge(_c_v, v4a(i, 2))
                                c_s.inc(scalar.activation(out=st_e[:, 0:1], in_=st_d[:, 0:1],
                                                          func=AF.Sqrt))
                                scalar.wait_ge(_c_v, v4a(i, 3))
                                if i >= 2:
                                    xnsem.wait(scalar, xn1w_val(i - 2))
                                c_s.inc(scalar.activation(out=xnt[:], in_=x1b[:, i, :],
                                                          func=AF.Identity, scale=st_c[:, 0:1],
                                                          bias=st_d[:, 0:1]))
                                scalar.wait_ge(_c_s, c_s.v)
                                xnsem.dma(scalar.dma_start(
                                    out=xn1d[128 * i:128 * (i + 1), :], in_=xnt[:]))
                                if d_xn1 is not None:
                                    m_dbg.dma(scalar.dma_start(
                                        out=d_xn1[128 * i:128 * (i + 1), :], in_=xnt[:]))
                                    scalar.wait_ge(_m_dbg, m_dbg.v)
                                scalar.wait_ge(_c_v, v4a(i, 7) - 1)

                        def t4a(j):
                            return t4a0 + j + 1

                        def v4a_ev(j):
                            return v4a0 + 7 * NT + j + 1

                        @blk.tensor
                        def _(tensor):
                            for j in range(2 * DCK):
                                ck, half = j // 2, j % 2
                                if j == 0:
                                    tensor.wait_ge(_c_v, v4a0 + 7 * NT)
                                if j >= 2:
                                    tensor.wait_ge(_c_v, v4a_ev(j - 2))
                                ps = psT4a if j % 2 == 0 else psT4b
                                for i2 in range(4):
                                    i = 4 * half + i2
                                    inst = tensor.transpose(
                                        ps[:, 128 * i2:128 * (i2 + 1)],
                                        in_=xq1_tm[:, i, 128 * ck:128 * (ck + 1)],
                                        identity=c_idb[:])
                                    if i2 == 3:
                                        c_t.inc(inst)

                        @blk.vector
                        def _(vector):
                            vector = SafeEng(vector, h_v, vstate)
                            for j in range(2 * DCK):
                                ck, half = j // 2, j % 2
                                vector.wait_ge(_c_t, t4a(j))
                                ps = psT4a if j % 2 == 0 else psT4b
                                c_v.inc(vector.tensor_copy(
                                    out=xq1_fm[:, ck, 512 * half:512 * (half + 1)], in_=ps[:]))

                    with contextlib.ExitStack() as es4b:
                        w1s0 = es4b.enter_context(nc.sbuf_tensor("w1s0", [128, 8, 2, 512], BF16))
                        w1s1 = es4b.enter_context(nc.sbuf_tensor("w1s1", [128, 8, 2, 512], BF16))
                        w3s0 = es4b.enter_context(nc.sbuf_tensor("w3s0", [128, 8, 2, 512], BF16))
                        w3s1 = es4b.enter_context(nc.sbuf_tensor("w3s1", [128, 8, 2, 512], BF16))
                        gst1 = es4b.enter_context(nc.sbuf_tensor("gst1", [128, NT, 512], BF16))
                        pst40 = es4b.enter_context(nc.sbuf_tensor("pst40", [128, NT, 512], BF16))
                        pst41 = es4b.enter_context(nc.sbuf_tensor("pst41", [128, NT, 512], BF16))
                        psG40 = es4b.enter_context(nc.psum_tensor("psG40", [128, 512], F32))
                        psG41 = es4b.enter_context(nc.psum_tensor("psG41", [128, 512], F32))
                        psU40 = es4b.enter_context(nc.psum_tensor("psU40", [128, 512], F32))
                        psU41 = es4b.enter_context(nc.psum_tensor("psU41", [128, 512], F32))

                        NOS1 = I1 // 512
                        t4b0, s4b0, v4b0 = c_t.v, c_s.v, c_v.v
                        pwa_b, pwb_b = m_pwa.v, m_pwb.v
                        w1_vals = []

                        def t_g1(os, t):
                            return t4b0 + 16 * os + 2 * t + 1

                        def t_u1(os, t):
                            return t4b0 + 16 * os + 2 * t + 2

                        def s_si1(os, t):
                            return s4b0 + 8 * os + t + 1

                        def v_pr1(os, t):
                            return v4b0 + 1 + 16 * os + 2 * t + 1

                        def v_rm1(os, t):
                            return v4b0 + 1 + 16 * os + 2 * t + 2

                        @blk.sync
                        def _(sync):
                            phase_join(sync)
                            for os in range(NOS1):
                                w1s = w1s0 if os % 2 == 0 else w1s1
                                w3s = w3s0 if os % 2 == 0 else w3s1
                                wsem = m_w2a if os % 2 == 0 else m_w2b
                                if os >= 2:
                                    sync.wait_ge(_c_t, t_u1(os - 2, NT - 1))
                                for ki in range(2):
                                    wsem.dma(sync.dma_start(
                                        out=w1s[:, :, ki, :],
                                        in_=wq_view("w1", 256, I1, os, 512)[:, ki]
                                        .rearrange("r p c -> p r c")))
                                    wsem.dma(sync.dma_start(
                                        out=w3s[:, :, ki, :],
                                        in_=wq_view("w3", 256, I1, os, 512)[:, ki]
                                        .rearrange("r p c -> p r c")))
                                w1_vals.append(wsem.v)

                        @blk.tensor
                        def _(tensor):
                            for os in range(NOS1):
                                w1s = w1s0 if os % 2 == 0 else w1s1
                                w3s = w3s0 if os % 2 == 0 else w3s1
                                wsem = m_w2a if os % 2 == 0 else m_w2b
                                wsem.wait(tensor, w1_vals[os])
                                for t in range(NT):
                                    gt = 8 * os + t
                                    psG = psG40 if gt % 2 == 0 else psG41
                                    psU = psU40 if gt % 2 == 0 else psU41
                                    if gt >= 2:
                                        tensor.wait_ge(_c_s, s_si1((gt - 2) // 8, (gt - 2) % 8))
                                        tensor.wait_ge(_c_v, v_pr1((gt - 2) // 8, (gt - 2) % 8))
                                    for ck in range(DCK):
                                        inst = tensor.matmul(
                                            psG[:], lhsT=xq1_fm[:, ck, 128 * t:128 * (t + 1)],
                                            rhs=w1s[:, ck // 2, ck % 2, :],
                                            start=(ck == 0), stop=(ck == DCK - 1))
                                        if ck == DCK - 1:
                                            c_t.inc(inst)
                                    for ck in range(DCK):
                                        inst = tensor.matmul(
                                            psU[:], lhsT=xq1_fm[:, ck, 128 * t:128 * (t + 1)],
                                            rhs=w3s[:, ck // 2, ck % 2, :],
                                            start=(ck == 0), stop=(ck == DCK - 1))
                                        if ck == DCK - 1:
                                            c_t.inc(inst)

                        @blk.scalar
                        def _(scalar):
                            scalar = SafeEng(scalar, h_s, sstate)
                            for os in range(NOS1):
                                for t in range(NT):
                                    gt = 8 * os + t
                                    psG = psG40 if gt % 2 == 0 else psG41
                                    scalar.wait_ge(_c_t, t_g1(os, t))
                                    if os >= 1:
                                        scalar.wait_ge(_c_v, v_pr1(os - 1, t))
                                    c_s.inc(scalar.activation(out=gst1[:, t, :], in_=psG[:],
                                                              func=AF.Silu,
                                                              scale=sx1[:, t:t + 1]))

                        @blk.vector
                        def _(vector):
                            vector = SafeEng(vector, h_v, vstate)
                            c_v.inc(vector.tensor_scalar(out=rmx1[:], in0=vjk[:, 0:8],
                                                         scalar1=0.0, scalar2=None,
                                                         op0=ALU.mult))
                            for os in range(NOS1):
                                pst = pst40 if os % 2 == 0 else pst41
                                for t in range(NT):
                                    gt = 8 * os + t
                                    psU = psU40 if gt % 2 == 0 else psU41
                                    vector.wait_ge(_c_t, t_u1(os, t))
                                    vector.wait_ge(_c_s, s_si1(os, t))
                                    if os >= 2 and t == 0:
                                        vector.wait_ge(_m_pwa if os % 2 == 0 else _m_pwb,
                                                       (pwa_b if os % 2 == 0 else pwb_b)
                                                       + 16 * ((os - 2) // 2 + 1))
                                    c_v.inc(vector.scalar_tensor_tensor(out=pst[:, t, :],
                                                                        in0=psU[:],
                                                                        scalar=sx3[:, t:t + 1],
                                                                        in1=gst1[:, t, :],
                                                                        op0=ALU.mult,
                                                                        op1=ALU.mult))
                                    vector.reduce_max(out=st_a[:, 8:9], in_=pst[:, t, :],
                                                      axis=AX, apply_absolute_value=True)
                                    c_v.inc(vector.tensor_tensor(out=rmx1[:, t:t + 1],
                                                                 in0=rmx1[:, t:t + 1],
                                                                 in1=st_a[:, 8:9], op=ALU.max))

                        @blk.gpsimd
                        def _(gpsimd):
                            for os in range(NOS1):
                                pst = pst40 if os % 2 == 0 else pst41
                                psem = m_pwa if os % 2 == 0 else m_pwb
                                gpsimd.wait_ge(_c_v, v_rm1(os, NT - 1))
                                psem.dma(gpsimd.dma_start(
                                    out=hdd.rearrange("(i p) c -> p i c", p=128)
                                    [:, :, 512 * os:512 * (os + 1)],
                                    in_=pst[:]))

                        @blk.vector
                        def _(vector):
                            vector = SafeEng(vector, h_v, vstate)
                            vector.wait_ge(_c_v, v_rm1(NOS1 - 1, NT - 1))
                            vector.tensor_scalar(out=rmx1[:], in0=rmx1[:], scalar1=EPS,
                                                 scalar2=None, op0=ALU.max)
                            c_v.inc(vector.reciprocal(out=invsp1[:], in_=rmx1[:]))
                            vector.tensor_scalar(out=invsp1[:], in0=invsp1[:], scalar1=127.0,
                                                 scalar2=None, op0=ALU.mult)
                            vector.tensor_scalar(out=rmx1[:], in0=rmx1[:], scalar1=1.0 / 127.0,
                                                 scalar2=None, op0=ALU.mult)
                            c_v.inc(vector.tensor_scalar(out=spd1[:], in0=rmx1[:],
                                                         scalar1=q_deq[:, 5:6], scalar2=None,
                                                         op0=ALU.mult))
                        ph4b_v_end = c_v.v

                with contextlib.ExitStack() as es4c:
                    wdh2 = es4c.enter_context(nc.sbuf_tensor("wdh2", [128, 8, 6, 512], BF16))
                    hrt0 = es4c.enter_context(nc.sbuf_tensor("hrt0", [128, I1P], BF16))
                    hrt1 = es4c.enter_context(nc.sbuf_tensor("hrt1", [128, I1P], BF16))
                    tq42 = es4c.enter_context(nc.sbuf_tensor("tq42", [128, 2048], F32))
                    hq_tm = es4c.enter_context(nc.sbuf_tensor("hq_tm", [128, I1P], BF16))
                    hq_fm = es4c.enter_context(nc.sbuf_tensor("hq_fm", [128, 48, 128], BF16))
                    g1h = es4c.enter_context(nc.sbuf_tensor("g1h", [128, 512], F32))
                    sxt40 = es4c.enter_context(nc.sbuf_tensor("sxt40", [128, 1536], F32))
                    sxt41 = es4c.enter_context(nc.sbuf_tensor("sxt41", [128, 1536], F32))
                    x2st0 = es4c.enter_context(nc.sbuf_tensor("x2st0", [128, 512], F32))
                    x2st1 = es4c.enter_context(nc.sbuf_tensor("x2st1", [128, 512], F32))
                    s1h = es4c.enter_context(nc.sbuf_tensor("s1h", [128, 512], F32))
                    psT4c0 = es4c.enter_context(nc.psum_tensor("psT4c0", [128, 512], BF16))
                    psT4c1 = es4c.enter_context(nc.psum_tensor("psT4c1", [128, 512], BF16))
                    psD40 = es4c.enter_context(nc.psum_tensor("psD40", [128, 512], F32))
                    psD41 = es4c.enter_context(nc.psum_tensor("psD41", [128, 512], F32))

                    t4c0, v4c0, s4c0 = c_t.v, c_v.v, c_s.v
                    spa0_, spb0_ = m_spa.v, m_spb.v
                    oa0_, ob0_ = m_oa.v, m_ob.v
                    xa0__, xb0__ = m_xa.v, m_xb.v
                    q0__ = [s_.v for s_ in m_qs]
                    wdh2_vals = []
                    _cum4 = [2, 4, 7, 10]

                    def prt1_val(q4, t):
                        return (spa0_ if t % 2 == 0 else spb0_) + 16 * (4 * q4 + t // 2 + 1)

                    def hqw_val(t):
                        return (oa0_ if t % 2 == 0 else ob0_) + 16 * (t // 2 + 1)

                    def sxt4_val(q4, t):
                        return (xa0__ if t % 2 == 0 else xb0__) \
                            + 16 * (10 * q4 + _cum4[t // 2])

                    def t_tr4c(q4, t, j):
                        return t4c0 + 13 * (8 * q4 + t) + j + 1

                    def t_dn4(q4, t):
                        return t4c0 + 13 * (8 * q4 + t) + 13

                    def vn4(q4):
                        return 14 if q4 == 0 else 13

                    def v_base4c(q4, t):
                        return v4c0 + (14 * 8 if q4 >= 1 else 0) + 13 * 8 * max(0, q4 - 1) \
                            + vn4(q4) * t

                    def v_q4(t):
                        return v4c0 + 14 * t + 1

                    def v_ev4c(q4, t, j):
                        return v_base4c(q4, t) + (1 if q4 == 0 else 0) + j + 1

                    def v_gat4(q4, t):
                        return v_base4c(q4, t) + vn4(q4)

                    def s_dn4(q4, t):
                        return s4c0 + 8 * q4 + t + 1

                    @blk.sync
                    def _(sync):
                        phase_join(sync)
                        for q4 in range(4):
                            wsem = m_w2a if q4 % 2 == 0 else m_w2b
                            if q4 >= 1:
                                sync.wait_ge(_c_t, t_dn4(q4 - 1, NT - 1))
                            for ki in range(6):
                                wsem.dma(sync.dma_start(
                                    out=wdh2[:, :, ki, :],
                                    in_=wq_view("w2", 768, D, q4, 512)[:, ki]
                                    .rearrange("r p c -> p r c")))
                            wdh2_vals.append(wsem.v)
                            for t in range(NT):
                                prt = hrt0 if t % 2 == 0 else hrt1
                                psem = m_spa if t % 2 == 0 else m_spb
                                if q4 == 0:
                                    if t >= 2:
                                        sync.wait_ge(_c_v, v_q4(t - 2))
                                else:
                                    prev = (q4, t - 2) if t >= 2 else (q4 - 1, NT - 2 + t)
                                    sync.wait_ge(_c_t, t_tr4c(prev[0], prev[1], 11))
                                    if q4 == 1 and t < 2:
                                        sync.wait_ge(_m_oa, oa0_ + 16 * 4)
                                        sync.wait_ge(_m_ob, ob0_ + 16 * 4)
                                if q4 == 0:
                                    psem.dma(sync.dma_start(
                                        out=prt[:, 0:I1],
                                        in_=hdd[128 * t:128 * (t + 1), :]))
                                else:
                                    psem.dma(sync.dma_start(
                                        out=prt[:, 0:I1P],
                                        in_=hqd[128 * t:128 * (t + 1), :]))
                                sxt = sxt40 if t % 2 == 0 else sxt41
                                xsem = m_xa if t % 2 == 0 else m_xb
                                if 8 * q4 + t >= 2:
                                    pj = 8 * q4 + t - 2
                                    sync.wait_ge(_c_v, v_gat4(pj // 8, pj % 8))
                                xsem.dma(sync.dma_start(
                                    out=sxt[:, 0:512],
                                    in_=x1d[128 * t:128 * (t + 1), 512 * q4:512 * (q4 + 1)]))
                                xsem.dma(sync.dma_start(
                                    out=sxt[:, 512:1024],
                                    in_=xn1d[128 * t:128 * (t + 1), 512 * q4:512 * (q4 + 1)]))
                                if t >= 4:
                                    xsem.dma(sync.dma_start(
                                        out=sxt[:, 1024:1536],
                                        in_=xn1d[128 * (t - 4):128 * (t - 3),
                                                 512 * q4:512 * (q4 + 1)]))

                    @blk.vector
                    def _(vector):
                        vector = SafeEng(vector, h_v, vstate)
                        vector.wait_ge(_c_v, ph4b_v_end)
                        for q4 in range(4):
                            for t in range(NT):
                                prt = hrt0 if t % 2 == 0 else hrt1
                                psem = m_spa if t % 2 == 0 else m_spb
                                psem.wait(vector, prt1_val(q4, t))
                                if q4 == 0:
                                    if t >= 1:
                                        vector.wait_ge(_c_t, t_tr4c(q4, t - 1, 11))
                                        pq_p = m_oa if (t - 1) % 2 == 0 else m_ob
                                        pq_p.wait(vector, hqw_val(t - 1))
                                    for pc in range(3):
                                        w = min(2048, I1 - 2048 * pc)
                                        sl = slice(2048 * pc, 2048 * pc + w)
                                        vector.tensor_scalar(out=tq42[:, 0:w], in0=prt[:, sl],
                                                             scalar1=invsp1[:, t:t + 1],
                                                             scalar2=MAGIC, op0=ALU.mult,
                                                             op1=ALU.add)
                                        vector.tensor_scalar(out=tq42[:, 0:w], in0=tq42[:, 0:w],
                                                             scalar1=MAGIC - 128.0,
                                                             scalar2=MAGIC + 127.0,
                                                             op0=ALU.max, op1=ALU.min)
                                        vector.tensor_scalar(out=hq_tm[:, sl],
                                                             in0=tq42[:, 0:w],
                                                             scalar1=-MAGIC,
                                                             scalar2=None, op0=ALU.add)
                                    c_v.inc(vector.tensor_scalar(out=hq_tm[:, I1:I1P],
                                                                 in0=hq_tm[:, 0:I1P - I1],
                                                                 scalar1=0.0, scalar2=None,
                                                                 op0=ALU.mult))
                                for j in range(12):
                                    vector.wait_ge(_c_t, t_tr4c(q4, t, j))
                                    ps = psT4c0 if j % 2 == 0 else psT4c1
                                    c_v.inc(vector.tensor_copy(
                                        out=hq_fm[:, 4 * j:4 * (j + 1), :]
                                        .rearrange("p a b -> p (a b)"), in_=ps[:]))
                                sxt = sxt40 if t % 2 == 0 else sxt41
                                xsem = m_xa if t % 2 == 0 else m_xb
                                x2st = x2st0 if t % 2 == 0 else x2st1
                                xqsem = m_qs[t % 2]
                                vector.wait_ge(_c_s, s_dn4(q4, t))
                                xsem.wait(vector, sxt4_val(q4, t))
                                if 8 * q4 + t >= 2:
                                    pj = 8 * q4 + t - 2
                                    xqsem.wait(vector, q0__[t % 2]
                                               + 16 * (4 * (pj // 8) + (pj % 8) // 2 + 1))
                                cs = slice(512 * q4, 512 * (q4 + 1))
                                vector.tensor_tensor(out=s1h[:], in0=sxt[:, 512:1024],
                                                     in1=kb0[:, cs], op=ALU.mult)
                                if t >= 4:
                                    vector.tensor_tensor(out=tq42[:, 0:512],
                                                         in0=sxt[:, 1024:1536],
                                                         in1=kb1[:, cs], op=ALU.mult)
                                    vector.tensor_tensor(out=s1h[:], in0=s1h[:],
                                                         in1=tq42[:, 0:512], op=ALU.add)
                                vector.tensor_tensor(out=x2st[:], in0=s1h[:], in1=g1h[:],
                                                     op=ALU.mult)
                                c_v.inc(vector.tensor_tensor(out=x2st[:], in0=x2st[:],
                                                             in1=sxt[:, 0:512], op=ALU.add))

                    @blk.tensor
                    def _(tensor):
                        for q4 in range(4):
                            wsem = m_w2a if q4 % 2 == 0 else m_w2b
                            wsem.wait(tensor, wdh2_vals[q4])
                            for t in range(NT):
                                prt = hrt0 if t % 2 == 0 else hrt1
                                psem = m_spa if t % 2 == 0 else m_spb
                                PQ = hq_tm if q4 == 0 else prt
                                if q4 == 0:
                                    tensor.wait_ge(_c_v, v_q4(t))
                                else:
                                    psem.wait(tensor, prt1_val(q4, t))
                                for j in range(12):
                                    gj = 12 * (8 * q4 + t) + j
                                    if gj >= 2:
                                        pj = gj - 2
                                        qt_, jj = divmod(pj, 12)
                                        tensor.wait_ge(_c_v, v_ev4c(qt_ // 8, qt_ % 8, jj))
                                    ps = psT4c0 if j % 2 == 0 else psT4c1
                                    for i2 in range(4):
                                        oc = 4 * j + i2
                                        inst = tensor.transpose(
                                            ps[:, 128 * i2:128 * (i2 + 1)],
                                            in_=PQ[:, 128 * oc:128 * (oc + 1)],
                                            identity=c_idb[:])
                                        if i2 == 3:
                                            c_t.inc(inst)
                                tensor.wait_ge(_c_v, v_ev4c(q4, t, 11))
                                if 8 * q4 + t >= 2:
                                    pj = 8 * q4 + t - 2
                                    tensor.wait_ge(_c_s, s_dn4(pj // 8, pj % 8))
                                psD = psD40 if (8 * q4 + t) % 2 == 0 else psD41
                                for oc in range(48):
                                    inst = tensor.matmul(psD[:], lhsT=hq_fm[:, oc, :],
                                                         rhs=wdh2[:, oc // 6, oc % 6, :],
                                                         start=(oc == 0), stop=(oc == 47))
                                    if oc == 47:
                                        c_t.inc(inst)

                    @blk.scalar
                    def _(scalar):
                        scalar = SafeEng(scalar, h_s, sstate)
                        for q4 in range(4):
                            for t in range(NT):
                                if q4 == 0:
                                    pqsem = m_oa if t % 2 == 0 else m_ob
                                    scalar.wait_ge(_c_v, v_q4(t))
                                    pqsem.dma(scalar.dma_start(
                                        out=hqd[128 * t:128 * (t + 1), :], in_=hq_tm[:]))
                                scalar.wait_ge(_c_t, t_dn4(q4, t))
                                if 8 * q4 + t >= 1:
                                    pj = 8 * q4 + t - 1
                                    scalar.wait_ge(_c_v, v_gat4(pj // 8, pj % 8))
                                psD = psD40 if (8 * q4 + t) % 2 == 0 else psD41
                                c_s.inc(scalar.activation(out=g1h[:], in_=psD[:],
                                                          func=AF.Silu,
                                                          scale=spd1[:, t:t + 1]))
                                if d_g1 is not None:
                                    scalar.wait_ge(_c_s, c_s.v)
                                    m_dbg.dma(scalar.dma_start(
                                        out=d_g1[128 * t:128 * (t + 1),
                                                 512 * q4:512 * (q4 + 1)],
                                        in_=g1h[:]))
                                    scalar.wait_ge(_m_dbg, m_dbg.v)
                                x2st = x2st0 if t % 2 == 0 else x2st1
                                xqsem = m_qs[t % 2]
                                scalar.wait_ge(_c_v, v_gat4(q4, t))
                                xqsem.dma(scalar.dma_start(
                                    out=out[128 * t:128 * (t + 1), 512 * q4:512 * (q4 + 1)],
                                    in_=x2st[:]))

        @blk.sync
        def _(sync):
            sync.wait_ge(_m_out, m_out.v)
            sync.wait_ge(_m_gp, m_gp.v)
            sync.wait_ge(_m_sc, m_sc.v)
            sync.wait_ge(_m_cc, m_cc.v)
            sync.wait_ge(_m_dbg, m_dbg.v)
            for s in m_qs:
                sync.wait_ge(s.sem, s.v)
            sync.wait_ge(_m_oa, m_oa.v)
            sync.wait_ge(_m_ob, m_ob.v)
            sync.wait_ge(_m_kv, m_kv.v)
            sync.wait_ge(_m_zc, m_zc.v)
            sync.wait_ge(_m_pwa, m_pwa.v)
            sync.wait_ge(_m_pwb, m_pwb.v)
            sync.wait_ge(_m_xa, m_xa.v)
            sync.wait_ge(_m_xb, m_xb.v)
            sync.wait_ge(_m_wa, m_wa.v)
            sync.wait_ge(_m_wb, m_wb.v)
    return nc, dbg


def make_inmaps(inputs):
    g = {k: np.asarray(v) for k, v in inputs.items()}
    x = np.ascontiguousarray(g["L0_attn_x"], dtype=np.float32)
    WQ = np.asarray(g["L0_attn_W_Q"], np.float32)
    WK = np.asarray(g["L0_attn_W_K"], np.float32)
    WV = np.asarray(g["L0_attn_W_V"], np.float32)
    WO = np.asarray(g["L0_attn_W_O"], np.float32)
    wgT = np.asarray(g["L0_ffn_w_gate"], np.float32).T
    wuT = np.asarray(g["L0_ffn_w_up"], np.float32).T
    wdT = np.asarray(g["L0_ffn_w_down"], np.float32).T
    w1T = np.asarray(g["L1_ffn_w1"], np.float32).T
    w3T = np.asarray(g["L1_ffn_w3"], np.float32).T
    w2T = np.asarray(g["L1_ffn_w2"], np.float32).T
    w2Tp = np.zeros((I1P, D), np.float32)
    w2Tp[:I1] = w2T
    wo_t = np.ascontiguousarray(WO.transpose(0, 2, 1))

    trim = np.triu(np.full((128, 128), -1e9, np.float32), k=1)
    idm = np.eye(128, dtype=np.float32)
    onescv = np.ones((128, 1), np.float32)
    onesrv = np.ones((1, 128), np.float32)

    def chunked_vec(v):
        return np.ascontiguousarray(np.asarray(v, np.float32).reshape(DCK, 128).T)

    def chunked_mat(m):
        return np.ascontiguousarray(
            np.asarray(m, np.float32).reshape(DCK, 128, 8).transpose(1, 0, 2).reshape(128, DCK * 8))

    ldt = chunked_vec(g["L1_attn_log_dt"])
    dpar = chunked_vec(g["L1_attn_D_param"])
    arl = chunked_mat(g["L1_attn_A_real_log"])
    aim = chunked_mat(g["L1_attn_A_imag"])
    cre = chunked_mat(g["L1_attn_C_re"])
    cim = chunked_mat(g["L1_attn_C_im"])

    in_maps = []
    for c in range(8):
        b = c // 4
        h0 = 4 * (c % 4)
        m = {
            "xa": np.ascontiguousarray(x[b]),
            "xo": np.ascontiguousarray(x[:, SEQ8 * c:SEQ8 * (c + 1), :]).reshape(TOK, D),
            "wq_t": np.ascontiguousarray(WQ[h0:h0 + 4].transpose(0, 2, 1)),
            "wk_t": np.ascontiguousarray(WK[h0:h0 + 4].transpose(0, 2, 1)),
            "wv_t": np.ascontiguousarray(WV[h0:h0 + 4].transpose(0, 2, 1)),
            "wo_t": wo_t,
            "wg_s": np.ascontiguousarray(wgT[256 * c:256 * (c + 1)]),
            "wu_s": np.ascontiguousarray(wuT[256 * c:256 * (c + 1)]),
            "wd_s": np.ascontiguousarray(wdT[1024 * c:1024 * (c + 1)]),
            "w1_s": np.ascontiguousarray(w1T[256 * c:256 * (c + 1)]),
            "w3_s": np.ascontiguousarray(w3T[256 * c:256 * (c + 1)]),
            "w2_s": np.ascontiguousarray(w2Tp[768 * c:768 * (c + 1)]),
            "tri": trim, "ident": idm, "onesc": onescv, "onesr": onesrv,
            "ldt_in": ldt, "dpar_in": dpar, "arl_in": arl, "aim_in": aim,
            "cre_in": cre, "cim_in": cim,
            "zsel_in": (np.arange(128, dtype=np.int32)[None, :] + 128 * c + 1024 * np.arange(8, dtype=np.int32)[:, None]).reshape(1024, 1).astype(np.int32).copy(),
            "jnk_in": np.zeros((128, 16), np.float32),
        }
        in_maps.append(m)
    return in_maps


def kernel(**inputs):
    nc, _ = build()
    in_maps = make_inmaps(inputs)
    res = run_bass_kernel_spmd(nc, in_maps, core_ids=list(range(8)))
    y = np.zeros((B, S, D), np.float32)
    for c in range(8):
        y[:, SEQ8 * c:SEQ8 * (c + 1), :] = res.results[c]["out"].reshape(B, SEQ8, D)
    return y
